# version 07
# baseline (speedup 1.0000x reference)
import numpy as np
import ml_dtypes
from contextlib import ExitStack

import concourse.bass as bass
import concourse.mybir as mybir
from concourse.bass_utils import run_bass_kernel_spmd

F32 = mybir.dt.float32
BF16 = mybir.dt.bfloat16
AF = mybir.ActivationFunctionType
ALU = mybir.AluOpType
AXX = mybir.AxisListType.X

T = 2048
D = 1024
H = 4096
IF = 2816
NTT = 16
NDC = 8
NHC = 32
NIC = 22
IF_SLICES = [(0, 512), (512, 512), (1024, 512), (1536, 512), (2048, 512), (2560, 256)]

O_IDENT = 0
O_EW = 64
O_EU = 72
O_ST = 80
O_MS = 272
O_MLS = 1296
O_XA = 2320
O_NRM = 18704
O_RW = 26896
O_SCR = 35088
O_WT = 43360
O_WO = 44384
O_XT = 48480
ACOLS = 50688

(ST_SUM, ST_SSQ, ST_M, ST_MSQ, ST_VAR, ST_CN, ST_INV, ST_BN, ST_BI, ST_C2) = range(10)
EPS_LN = 1e-5
EPS_RMS = 1e-6


def _stc(slot):
    return O_ST + 16 * slot


class Sem:
    def __init__(self, h, name):
        self.h = h
        self.name = name
        self.n = 0

    def tick(self, ins, k=1):
        ins.then_inc(self.h, k)
        self.n += k
        return self.n


_DBG_PHASE = None


def build_nc(dbg_phase=None):
    nc = bass.Bass("TRN2", target_bir_lowering=False)
    dbgh = None
    if dbg_phase is not None:
        dbgh = nc.dram_tensor("dbgarena", [128, ACOLS], F32,
                              kind="ExternalOutput").ap()

    xh = nc.dram_tensor("x", [T, D], F32, kind="ExternalInput").ap()
    yh = nc.dram_tensor("y", [T, D], F32, kind="ExternalOutput").ap()
    x2h = nc.dram_tensor("x2s", [T, D], F32, kind="Internal").ap()
    wkh = nc.dram_tensor("wkT", [D, D], F32, kind="ExternalInput").ap()
    wvh = nc.dram_tensor("wvT", [D, D], F32, kind="ExternalInput").ap()
    wrh = nc.dram_tensor("wrT", [D, D], F32, kind="ExternalInput").ap()
    woh = nc.dram_tensor("woT", [D, D], F32, kind="ExternalInput").ap()
    wfch = nc.dram_tensor("wfcT", [D, H], F32, kind="ExternalInput").ap()
    wmlph = nc.dram_tensor("wmlpT", [H, D], F32, kind="ExternalInput").ap()
    wgh = nc.dram_tensor("wg", [IF, D], F32, kind="ExternalInput").ap()
    wuh = nc.dram_tensor("wu", [IF, D], F32, kind="ExternalInput").ap()
    wdh = nc.dram_tensor("wd", [D, IF], F32, kind="ExternalInput").ap()
    tdh = nc.dram_tensor("tdp", [128, 8], F32, kind="ExternalInput").ap()
    tfh = nc.dram_tensor("tfp", [128, 8], F32, kind="ExternalInput").ap()
    msh = nc.dram_tensor("msp", [128, D], F32, kind="ExternalInput").ap()
    mlsh = nc.dram_tensor("mlsp", [128, D], F32, kind="ExternalInput").ap()
    idh = nc.dram_tensor("identp", [128, 128], BF16, kind="ExternalInput").ap()

    with ExitStack() as es:
        AR = es.enter_context(nc.sbuf_tensor("arena", [128, ACOLS], F32))
        ps = [es.enter_context(nc.psum_tensor(f"psum{i}", [128, 512], F32))
              for i in range(8)]
        qy = Sem(es.enter_context(nc.semaphore("qy")), "qy")
        qs = Sem(es.enter_context(nc.semaphore("qs")), "qs")
        qg = Sem(es.enter_context(nc.semaphore("qg")), "qg")
        qga = Sem(es.enter_context(nc.semaphore("qga")), "qga")
        qgb = Sem(es.enter_context(nc.semaphore("qgb")), "qgb")
        qxa = Sem(es.enter_context(nc.semaphore("qxa")), "qxa")
        qxb = Sem(es.enter_context(nc.semaphore("qxb")), "qxb")
        qyo = Sem(es.enter_context(nc.semaphore("qyo")), "qyo")
        qga2 = Sem(es.enter_context(nc.semaphore("qga2")), "qga2")
        qgb2 = Sem(es.enter_context(nc.semaphore("qgb2")), "qgb2")
        qgc2 = Sem(es.enter_context(nc.semaphore("qgc2")), "qgc2")
        qgd2 = Sem(es.enter_context(nc.semaphore("qgd2")), "qgd2")
        smm = Sem(es.enter_context(nc.semaphore("smm")), "smm")
        sv = Sem(es.enter_context(nc.semaphore("sv")), "sv")
        ss = Sem(es.enter_context(nc.semaphore("ss")), "ss")
        sg = Sem(es.enter_context(nc.semaphore("sg")), "sg")
        block = es.enter_context(nc.Block())

        _scope_st = []
        ARb = AR[:].bitcast(BF16)

        def f32(c0, w):
            return AR[:, c0:c0 + w]

        def bf(c0f32, wbf):
            return ARb[:, 2 * c0f32:2 * c0f32 + wbf]

        ident = ARb[:, 0:128]
        psb = [p[:].bitcast(BF16) for p in ps]

        bank_free = [(None, 0)] * 8
        bank_fill = [(None, 0)] * 8

        def wait(e, sem, val):
            assert val <= sem.n, f"wait {sem.name}={val} > emitted {sem.n}"
            if val > 0:
                e.wait_ge(sem.h, val)

        def pe_wait_free(e, bk):
            s, v = bank_free[bk]
            if s is not None:
                wait(e, s, v)

        def use_wait_fill(e, bk):
            s, v = bank_fill[bk]
            assert s is not None, f"bank {bk} never filled"
            wait(e, s, v)

        def dump_arena(tag):
            if dbg_phase != tag:
                return
            @block.sync
            def _(y):
                wait(y, sv, sv.n)
                wait(y, ss, ss.n)
                wait(y, sg, sg.n)
                wait(y, smm, smm.n)
                wait(y, qg, qg.n)
                wait(y, qs, qs.n)
                wait(y, qga, qga.n)
                wait(y, qgb, qgb.n)
                wait(y, qxa, qxa.n)
                wait(y, qxb, qxb.n)
                wait(y, qyo, qyo.n)
                ins = y.dma_start(out=dbgh[:], in_=AR[:])
                qy.tick(ins, 16)
            barrier = qy.n
            @block.vector
            def _(v):
                wait(v, qy, barrier)
            @block.scalar
            def _(s):
                wait(s, qy, barrier)
            @block.gpsimd
            def _(g):
                wait(g, qy, barrier)
            @block.tensor
            def _(t):
                wait(t, qy, barrier)

        def nrmv(dc, t0, w):
            return ARb[:, 2 * O_NRM + dc * T + t0: 2 * O_NRM + dc * T + t0 + w]

        def rwv(dc, t0, w):
            return ARb[:, 2 * O_RW + dc * T + t0: 2 * O_RW + dc * T + t0 + w]

        def emit_quant(v, cz, cqa, cqu, cqn, cqh, cbm, cst, w):
            nb = w // 16
            zg = f32(cz, w).rearrange("p (g k) -> p g k", k=16)
            v.reduce_max(out=f32(cbm, nb), in_=zg, axis=AXX, apply_absolute_value=True)
            v.tensor_scalar(out=f32(cqn, w), in0=f32(cz, w), scalar1=0.0,
                            scalar2=None, op0=ALU.is_ge, op1=ALU.bypass)
            v.tensor_scalar(out=f32(cbm + nb, nb), in0=f32(cbm, nb), scalar1=1.0 / 6.0,
                            scalar2=1e-12, op0=ALU.mult, op1=ALU.max)
            v.tensor_scalar(out=f32(cqn, w), in0=f32(cqn, w), scalar1=2.0,
                            scalar2=-1.0, op0=ALU.mult, op1=ALU.add)
            v.reciprocal(out=f32(cbm + 2 * nb, nb), in_=f32(cbm + nb, nb))
            v.engine_nop()
            rcp_b = f32(cbm + 2 * nb, nb).unsqueeze(2).broadcast_to([128, nb, 16])
            scl_b = f32(cbm + nb, nb).unsqueeze(2).broadcast_to([128, nb, 16])
            qag = f32(cqa, w).rearrange("p (g k) -> p g k", k=16)
            qug = f32(cqu, w).rearrange("p (g k) -> p g k", k=16)
            v.tensor_tensor(out=qug, in0=qag, in1=rcp_b, op=ALU.mult)
            v.tensor_tensor(out=qag, in0=f32(cqn, w).rearrange("p (g k) -> p g k", k=16),
                            in1=scl_b, op=ALU.mult)
            u = f32(cqu, w)
            v.tensor_scalar(out=f32(cqn, w), in0=u, scalar1=0.25, scalar2=None,
                            op0=ALU.is_gt)
            v.tensor_scalar(out=f32(cqh, w), in0=u, scalar1=2.5, scalar2=None,
                            op0=ALU.is_gt)
            for tn, th in ((0.75, 3.5), (1.25, 5.0), (1.75, 5.0)):
                v.scalar_tensor_tensor(out=f32(cqn, w), in0=u, scalar=tn,
                                       in1=f32(cqn, w), op0=ALU.is_gt, op1=ALU.add)
                v.scalar_tensor_tensor(out=f32(cqh, w), in0=u, scalar=th,
                                       in1=f32(cqh, w), op0=ALU.is_gt, op1=ALU.add)
            v.engine_nop()
            v.scalar_tensor_tensor(out=f32(cqn, w), in0=f32(cqn, w), scalar=0.5,
                                   in1=f32(cqh, w), op0=ALU.mult, op1=ALU.add)
            v.engine_nop(); v.engine_nop()
            return v.tensor_tensor(
                out=bf(cst, w).rearrange("p (g k) -> p g k", k=16),
                in0=f32(cqn, w).rearrange("p (g k) -> p g k", k=16),
                in1=qag, op=ALU.mult)

        _scope_st.append(("phP0", nc.enter_named_scope("phP0", False)[0]))
        @block.sync
        def _(y):
            qy.tick(y.dma_start(out=ARb[:, 0:128], in_=idh[:]), 16)
            qy.tick(y.dma_start(out=f32(O_EW, 8), in_=tdh[:]), 16)
            qy.tick(y.dma_start(out=f32(O_EU, 8), in_=tfh[:]), 16)
            qy.tick(y.dma_start(out=f32(O_MS, D), in_=msh[:]), 16)
            qy.tick(y.dma_start(out=f32(O_MLS, D), in_=mlsh[:]), 16)
            for tt in range(NTT):
                ins = y.dma_start(out=f32(O_XA + tt * D, D),
                                  in_=xh[tt * 128:(tt + 1) * 128, :])
                qy.tick(ins, 16)
        n_pre = 16 * 5
        x_all = qy.n

        @block.scalar
        def _(s):
            wait(s, qy, x_all)
            s.activation(out=f32(O_EW, 8), in_=f32(O_EW, 8), func=AF.Exp)
            s.activation(out=f32(O_EW, 8), in_=f32(O_EW, 8), func=AF.Exp, scale=-1.0)
            ins = s.activation(out=f32(O_EU, 8), in_=f32(O_EU, 8), func=AF.Exp)
            ss.tick(ins)
        ew_ready = ss.n

        nc.leave_named_scope(*_scope_st.pop(), False)
        _scope_st.append(("phA", nc.enter_named_scope("phA", False)[0]))
        @block.vector
        def _(v):
            wait(v, qy, x_all)
            for tt in range(NTT):
                ins = v.tensor_scalar(
                    out=f32(35600, D), in0=f32(O_XA + tt * D, D), scalar1=1.0,
                    scalar2=None, op0=ALU.mult, op1=ALU.add,
                    accum_out=AR[:, _stc(ST_SUM) + tt:_stc(ST_SUM) + tt + 1])
                sv.tick(ins)

        @block.scalar
        def _(s):
            wait(s, qy, x_all)
            for tt in range(NTT):
                ins = s.activation(
                    out=f32(36624, D), in_=f32(O_XA + tt * D, D), func=AF.Square,
                    accum_out=AR[:, _stc(ST_SSQ) + tt:_stc(ST_SSQ) + tt + 1])
                ss.tick(ins)
        a_ssq = ss.n

        @block.vector
        def _(v):
            wait(v, ss, a_ssq)
            v.tensor_scalar(out=f32(_stc(ST_M), 16), in0=f32(_stc(ST_SUM), 16),
                            scalar1=1.0 / D, scalar2=None, op0=ALU.mult,
                            op1=ALU.bypass)
            v.engine_nop(); v.engine_nop(); v.engine_nop()
            v.tensor_tensor(out=f32(_stc(ST_MSQ), 16), in0=f32(_stc(ST_M), 16),
                            in1=f32(_stc(ST_M), 16), op=ALU.mult)
            v.engine_nop(); v.engine_nop(); v.engine_nop()
            ins = v.scalar_tensor_tensor(
                out=f32(_stc(ST_VAR), 16), in0=f32(_stc(ST_SSQ), 16), scalar=1.0 / D,
                in1=f32(_stc(ST_MSQ), 16), op0=ALU.mult, op1=ALU.subtract)
            sv.tick(ins)
        a_var = sv.n

        @block.vector
        def _(v):
            v.engine_nop(); v.engine_nop(); v.engine_nop()
            v.tensor_scalar(out=f32(_stc(ST_CN), 16), in0=f32(_stc(ST_VAR), 16),
                            scalar1=1.0 + EPS_RMS, scalar2=EPS_LN * EPS_RMS,
                            op0=ALU.mult, op1=ALU.add)
            ins = v.tensor_scalar(out=f32(_stc(ST_INV), 16), in0=f32(_stc(ST_VAR), 16),
                                  scalar1=EPS_LN, scalar2=None, op0=ALU.add,
                                  op1=ALU.bypass)
            sv.tick(ins)
        a_args = sv.n

        @block.scalar
        def _(s):
            wait(s, sv, a_args)
            s.activation(out=f32(_stc(ST_CN), 16), in_=f32(_stc(ST_CN), 16),
                         func=AF.Ln)
            s.activation(out=f32(_stc(ST_CN), 16), in_=f32(_stc(ST_CN), 16),
                         func=AF.Exp, scale=-0.5)
            s.activation(out=f32(_stc(ST_INV), 16), in_=f32(_stc(ST_INV), 16),
                         func=AF.Ln)
            ins = s.activation(out=f32(_stc(ST_INV), 16), in_=f32(_stc(ST_INV), 16),
                               func=AF.Exp, scale=-0.5)
            ss.tick(ins)
        a_invcn = ss.n

        @block.vector
        def _(v):
            wait(v, ss, a_invcn)
            v.scalar_tensor_tensor(out=f32(_stc(ST_BN), 16), in0=f32(_stc(ST_M), 16),
                                   scalar=-1.0, in1=f32(_stc(ST_CN), 16),
                                   op0=ALU.mult, op1=ALU.mult)
            ins = v.scalar_tensor_tensor(out=f32(_stc(ST_BI), 16),
                                         in0=f32(_stc(ST_M), 16), scalar=-1.0,
                                         in1=f32(_stc(ST_INV), 16),
                                         op0=ALU.mult, op1=ALU.mult)
            sv.tick(ins)
        a_smalls = sv.n

        N1STG = 35088
        for tt in range(NTT):
            @block.scalar
            def _(s, tt=tt):
                wait(s, sv, a_smalls)
                if bank_fill[7][0] is not None:
                    wait(s, smm, bank_fill[7][1])
                ins = s.activation(
                    out=bf(N1STG, D), in_=f32(O_XA + tt * D, D), func=AF.Identity,
                    bias=AR[:, _stc(ST_BN) + tt:_stc(ST_BN) + tt + 1],
                    scale=AR[:, _stc(ST_CN) + tt:_stc(ST_CN) + tt + 1])
                ss.tick(ins)
            n1_made = ss.n

            @block.tensor
            def _(t, tt=tt, n1_made=n1_made):
                wait(t, ss, n1_made)
                for half, bk in ((0, 6), (1, 7)):
                    pe_wait_free(t, bk)
                    for j in range(4):
                        ins = t.transpose(
                            psb[bk][:, j * 128:(j + 1) * 128],
                            bf(N1STG, D)[:, (half * 4 + j) * 128:(half * 4 + j + 1) * 128],
                            ident)
                    smm.tick(ins)
                    bank_fill[bk] = (smm, smm.n)

            @block.vector
            def _(v, tt=tt):
                nview = ARb[:, 2 * O_NRM:2 * O_NRM + 8 * T].rearrange(
                    "p (c t) -> p c t", c=8)
                for half, bk in ((0, 6), (1, 7)):
                    use_wait_fill(v, bk)
                    ins = v.tensor_copy(
                        out=nview[:, half * 4:half * 4 + 4, tt * 128:(tt + 1) * 128],
                        in_=psb[bk][:, 0:512].rearrange("p (c t) -> p c t", c=4))
                    sv.tick(ins)
                    bank_free[bk] = (sv, sv.n)
                ins = v.tensor_scalar(
                    out=f32(O_XA + tt * D, D), in0=f32(O_XA + tt * D, D),
                    scalar1=AR[:, _stc(ST_INV) + tt:_stc(ST_INV) + tt + 1],
                    scalar2=AR[:, _stc(ST_BI) + tt:_stc(ST_BI) + tt + 1],
                    op0=ALU.mult, op1=ALU.add)
                sv.tick(ins)
        normed1_ready = sv.n
        dump_arena("A")

        nc.leave_named_scope(*_scope_st.pop(), False)
        _scope_st.append(("phB", nc.enter_named_scope("phB", False)[0]))
        C_KV, C_S, C_R, C_KSB = 35088, 37152, 39232, 41280
        wt_slot = [O_WT, O_WT + 512]
        wload_count = []
        wload_used = {}
        load_idx = [0]
        mlp_guard = [None]

        def wq_load(g, src_ap, nbf=1024):
            li = load_idx[0]
            load_idx[0] += 1
            if li >= 2:
                s_, v_ = wload_used[li - 2]
                wait(g, s_, v_)
            if mlp_guard[0] is not None:
                wait(g, smm, mlp_guard[0])
                mlp_guard[0] = None
            dst = bf(wt_slot[li % 2], nbf)
            if src_ap.ndim == 3:
                dst = dst.rearrange("p (c m) -> p c m", c=8)
            qsem = qga if li % 2 == 0 else qgb
            ins = g.dma_start(out=dst, in_=src_ap)
            qsem.tick(ins, 16)
            wload_count.append((qsem, qsem.n))
            return li

        def colslice8(wh, o):
            return wh[:, o * 128:(o + 1) * 128].rearrange("(c p) m -> p c m", p=128)

        @block.vector
        def _(v):
            ins = v.memset(f32(C_S, 1), 0.0)
            sv.tick(ins)
        szero = sv.n

        for o in range(NDC):
            lis = {}

            @block.gpsimd
            def _(g, o=o):
                lis["k"] = wq_load(g, colslice8(wkh, o))
                lis["v"] = wq_load(g, colslice8(wvh, o))

            @block.tensor
            def _(t, o=o):
                if o == 0:
                    wait(t, sv, normed1_ready)
                for nm, b0 in (("k", 0), ("v", 4)):
                    li = lis[nm]
                    wait(t, *wload_count[li])
                    for bk in range(b0, b0 + 4):
                        pe_wait_free(t, bk)
                    wslot = bf(wt_slot[li % 2], 1024)
                    for t4 in range(4):
                        for dc in range(NDC):
                            ins = t.matmul(ps[b0 + t4][:],
                                           wslot[:, dc * 128:(dc + 1) * 128],
                                           nrmv(dc, t4 * 512, 512),
                                           start=(dc == 0), stop=(dc == NDC - 1))
                    smm.tick(ins)
                    for bk in range(b0, b0 + 4):
                        bank_fill[bk] = (smm, smm.n)
                    wload_used[li] = (smm, smm.n)

            @block.gpsimd
            def _(g, o=o):
                lis["r"] = wq_load(g, colslice8(wrh, o))

            @block.scalar
            def _(s, o=o):
                use_wait_fill(s, 0)
                for t4 in range(4):
                    ins = s.copy(out=f32(C_KSB + t4 * 512, 512), in_=ps[t4][:])
                ss.tick(ins)
            k_cp = ss.n
            for bk in range(4):
                bank_free[bk] = (ss, k_cp)

            @block.vector
            def _(v, o=o, k_cp=k_cp):
                wait(v, ss, k_cp)
                use_wait_fill(v, 4)
                for t4 in range(4):
                    ins = v.tensor_tensor(out=f32(C_KV + t4 * 512, 512),
                                          in0=f32(C_KSB + t4 * 512, 512),
                                          in1=ps[4 + t4][:], op=ALU.mult)
                sv.tick(ins)
            kv_made = sv.n
            for bk in range(4, 8):
                bank_free[bk] = (sv, kv_made)

            @block.tensor
            def _(t, o=o):
                li = lis["r"]
                wait(t, *wload_count[li])
                for bk in range(4):
                    pe_wait_free(t, bk)
                wslot = bf(wt_slot[li % 2], 1024)
                for t4 in range(4):
                    for dc in range(NDC):
                        ins = t.matmul(ps[t4][:], wslot[:, dc * 128:(dc + 1) * 128],
                                       nrmv(dc, t4 * 512, 512),
                                       start=(dc == 0), stop=(dc == NDC - 1))
                smm.tick(ins)
                for bk in range(4):
                    bank_fill[bk] = (smm, smm.n)
                wload_used[li] = (smm, smm.n)

            @block.scalar
            def _(s, o=o):
                use_wait_fill(s, 0)
                for t4 in range(4):
                    ins = s.activation(out=f32(C_R + t4 * 512, 512), in_=ps[t4][:],
                                       func=AF.Sigmoid)
                ss.tick(ins)
            r_made = ss.n
            for bk in range(4):
                bank_free[bk] = (ss, r_made)

            @block.vector
            def _(v, o=o, r_made=r_made):
                if o == 0:
                    wait(v, sv, szero)
                    wait(v, ss, ew_ready)
                ewb = AR[:, O_EW + o:O_EW + o + 1].broadcast_to([128, T])
                v.engine_nop(); v.engine_nop(); v.engine_nop()
                v.tensor_tensor_scan(out=f32(C_S + 1, T), data0=ewb,
                                     data1=f32(C_KV, T), initial=0.0,
                                     op0=ALU.mult, op1=ALU.add)
                v.engine_nop(); v.engine_nop(); v.engine_nop()
                v.scalar_tensor_tensor(out=f32(C_KV, T), in0=f32(C_KV, T),
                                       scalar=AR[:, O_EU + o:O_EU + o + 1],
                                       in1=f32(C_S, T), op0=ALU.mult, op1=ALU.add)
                wait(v, ss, r_made)
                v.engine_nop(); v.engine_nop(); v.engine_nop()
                ins = v.tensor_tensor(out=rwv(o, 0, T), in0=f32(C_KV, T),
                                      in1=f32(C_R, T), op=ALU.mult)
                sv.tick(ins)
        rw_ready = sv.n
        dump_arena("B")

        nc.leave_named_scope(*_scope_st.pop(), False)
        _scope_st.append(("phC", nc.enter_named_scope("phC", False)[0]))
        @block.gpsimd
        def _(g):
            for dc in range(NDC):
                ins = g.dma_start(out=bf(O_WO + dc * 512, 1024),
                                  in_=woh[dc * 128:(dc + 1) * 128, :])
                qg.tick(ins, 16)
        wo_loaded = qg.n

        C_TMP = 35088
        g_adds = []
        for tt in range(NTT):
            for n in range(2):
                bk = (tt * 2 + n) % 4

                @block.tensor
                def _(t, tt=tt, n=n, bk=bk):
                    if tt == 0 and n == 0:
                        wait(t, qg, wo_loaded)
                        wait(t, sv, rw_ready)
                    pe_wait_free(t, bk)
                    for dc in range(NDC):
                        ins = t.matmul(
                            ps[bk][:], rwv(dc, tt * 128, 128),
                            bf(O_WO + dc * 512, 1024)[:, n * 512:(n + 1) * 512],
                            start=(dc == 0), stop=(dc == NDC - 1))
                    smm.tick(ins)
                    bank_fill[bk] = (smm, smm.n)

                @block.vector
                def _(v, tt=tt, n=n, bk=bk):
                    use_wait_fill(v, bk)
                    if len(g_adds) >= 2:
                        wait(v, sg, g_adds[-2])
                    ins = v.tensor_tensor(out=f32(C_TMP + (n % 2) * 512, 512),
                                          in0=ps[bk][:],
                                          in1=f32(O_MS + n * 512, 512), op=ALU.mult)
                    sv.tick(ins)
                    bank_free[bk] = (sv, sv.n)
                tmp_made = sv.n

                @block.gpsimd
                def _(g, tt=tt, n=n, tmp_made=tmp_made):
                    wait(g, sv, tmp_made)
                    ins = g.tensor_tensor(out=f32(O_XA + tt * D + n * 512, 512),
                                          in0=f32(O_XA + tt * D + n * 512, 512),
                                          in1=f32(C_TMP + (n % 2) * 512, 512),
                                          op=ALU.add)
                    sg.tick(ins)
                g_adds.append(sg.n)
        xa_after_c = sg.n
        dump_arena("C")

        nc.leave_named_scope(*_scope_st.pop(), False)
        _scope_st.append(("phD", nc.enter_named_scope("phD", False)[0]))
        C_HSB, C_ESB = 35088, 37136
        mlp_idx = [0]
        mlp_used = {}
        mlp_count = {}
        mlp_sems = [qga2, qgb2, qgc2, qgd2]

        def g_col(hc):
            if hc < 16:
                return 2 * O_RW + hc * 1024
            if hc < 24:
                return 2 * 22800 + (hc - 16) * 1024
            return 2 * O_WO + (hc - 24) * 1024

        for half in range(2):
            tt0 = half * 8

            @block.scalar
            def _(s, half=half, tt0=tt0):
                wait(s, sg, sg.n)
                for i in range(8):
                    ins = s.activation(
                        out=f32(36624, D), in_=f32(O_XA + (tt0 + i) * D, D),
                        func=AF.Square,
                        accum_out=AR[:, _stc(ST_SSQ) + i:_stc(ST_SSQ) + i + 1])
                ss.tick(ins)
            d_ssq = ss.n

            @block.vector
            def _(v, d_ssq=d_ssq):
                wait(v, ss, d_ssq)
                ins = v.tensor_scalar(out=f32(_stc(ST_C2), 8), in0=f32(_stc(ST_SSQ), 8),
                                      scalar1=1.0 / D, scalar2=EPS_RMS,
                                      op0=ALU.mult, op1=ALU.add)
                sv.tick(ins)
            d_arg = sv.n

            @block.scalar
            def _(s, d_arg=d_arg):
                wait(s, sv, d_arg)
                s.activation(out=f32(_stc(ST_C2), 8), in_=f32(_stc(ST_C2), 8),
                             func=AF.Ln)
                ins = s.activation(out=f32(_stc(ST_C2), 8), in_=f32(_stc(ST_C2), 8),
                                   func=AF.Exp, scale=-0.5)
                ss.tick(ins)
            c2_done = ss.n

            for i in range(8):
                @block.scalar
                def _(s, i=i, tt0=tt0, c2_done=c2_done):
                    wait(s, ss, c2_done)
                    if bank_fill[7][0] is not None:
                        wait(s, smm, bank_fill[7][1])
                    ins = s.activation(
                        out=bf(N1STG, D), in_=f32(O_XA + (tt0 + i) * D, D),
                        func=AF.Identity,
                        scale=AR[:, _stc(ST_C2) + i:_stc(ST_C2) + i + 1])
                    ss.tick(ins)
                n2_made = ss.n

                @block.tensor
                def _(t, i=i, n2_made=n2_made):
                    wait(t, ss, n2_made)
                    for hf, bk in ((0, 6), (1, 7)):
                        pe_wait_free(t, bk)
                        for j in range(4):
                            ins = t.transpose(
                                psb[bk][:, j * 128:(j + 1) * 128],
                                bf(N1STG, D)[:, (hf * 4 + j) * 128:(hf * 4 + j + 1) * 128],
                                ident)
                        smm.tick(ins)
                        bank_fill[bk] = (smm, smm.n)

                @block.vector
                def _(v, i=i):
                    n2view = ARb[:, 2 * O_NRM:2 * O_NRM + 8 * 1024].rearrange(
                        "p (c t) -> p c t", c=8)
                    for hf, bk in ((0, 6), (1, 7)):
                        use_wait_fill(v, bk)
                        ins = v.tensor_copy(
                            out=n2view[:, hf * 4:hf * 4 + 4, i * 128:(i + 1) * 128],
                            in_=psb[bk][:, 0:512].rearrange("p (c t) -> p c t", c=4))
                        sv.tick(ins)
                        bank_free[bk] = (sv, sv.n)
            n2_ready = sv.n

            def n2v(dc, t0, w):
                return ARb[:, 2 * O_NRM + dc * 1024 + t0: 2 * O_NRM + dc * 1024 + t0 + w]

            for hc in range(NHC):
                lis_fc_holder = [None]

                @block.gpsimd
                def _(g, hc=hc, lis_fc_holder=lis_fc_holder):
                    lis_fc_holder[0] = wq_load(g, wfch[:, hc * 128:(hc + 1) * 128]
                                               .rearrange("(c p) m -> p c m", p=128))

                for t4 in range(2):
                    bk = t4

                    @block.tensor
                    def _(t, hc=hc, t4=t4, bk=bk, n2_ready=n2_ready,
                          lis_fc_holder=lis_fc_holder):
                        li = lis_fc_holder[0]
                        if hc == 0 and t4 == 0:
                            wait(t, sv, n2_ready)
                        wait(t, *wload_count[li])
                        pe_wait_free(t, bk)
                        wslot = bf(wt_slot[li % 2], 1024)
                        for dc in range(NDC):
                            ins = t.matmul(ps[bk][:], wslot[:, dc * 128:(dc + 1) * 128],
                                           n2v(dc, t4 * 512, 512),
                                           start=(dc == 0), stop=(dc == NDC - 1))
                        smm.tick(ins)
                        bank_fill[bk] = (smm, smm.n)
                        if t4 == 1:
                            wload_used[li] = (smm, smm.n)

                    @block.scalar
                    def _(s, t4=t4, bk=bk):
                        use_wait_fill(s, bk)
                        wait(s, sv, sv.n)
                        ins = s.copy(out=f32(C_HSB + t4 * 512, 512), in_=ps[bk][:])
                        ss.tick(ins)
                    h_cp = ss.n
                    bank_free[bk] = (ss, h_cp)

                    @block.vector
                    def _(v, hc=hc, t4=t4, h_cp=h_cp):
                        wait(v, ss, h_cp)
                        v.scalar_tensor_tensor(out=f32(C_ESB + t4 * 512, 512),
                                               in0=f32(C_HSB + t4 * 512, 512),
                                               scalar=0.25,
                                               in1=f32(C_HSB + t4 * 512, 512),
                                               op0=ALU.mult, op1=ALU.max)
                        v.engine_nop(); v.engine_nop(); v.engine_nop()
                        ins = v.tensor_tensor(
                            out=ARb[:, g_col(hc) + t4 * 512:g_col(hc) + t4 * 512 + 512],
                            in0=f32(C_HSB + t4 * 512, 512),
                            in1=f32(C_ESB + t4 * 512, 512), op=ALU.mult)
                        sv.tick(ins)
            g_ready = sv.n

            for n in range(2):
                for hc in range(NHC):
                    li_holder = [None]

                    @block.gpsimd
                    def _(g, hc=hc, n=n, li_holder=li_holder):
                        li = mlp_idx[0]
                        mlp_idx[0] += 1
                        if li >= 4:
                            s_, v_ = mlp_used[li - 4]
                            wait(g, s_, v_)
                        if li % 32 < 4 and load_idx[0] >= 1 and                                 (load_idx[0] - 1) in wload_used:
                            wait(g, *wload_used[load_idx[0] - 1])
                        qsem = mlp_sems[li % 4]
                        ins = g.dma_start(
                            out=bf(O_WT + (li % 4) * 256, 512),
                            in_=wmlph[hc * 128:(hc + 1) * 128, n * 512:(n + 1) * 512])
                        qsem.tick(ins, 16)
                        mlp_count[li] = (qsem, qsem.n)
                        li_holder[0] = li

                    @block.tensor
                    def _(t, hc=hc, n=n, li_holder=li_holder, g_ready=g_ready):
                        li = li_holder[0]
                        if hc == 0:
                            wait(t, sv, g_ready)
                            for bk in range(8):
                                pe_wait_free(t, bk)
                        wait(t, *mlp_count[li])
                        wslot = bf(O_WT + (li % 4) * 256, 512)
                        for i in range(8):
                            ins = t.matmul(
                                ps[i][:],
                                ARb[:, g_col(hc) + i * 128:g_col(hc) + i * 128 + 128],
                                wslot,
                                start=(hc == 0), stop=(hc == NHC - 1))
                        smm.tick(ins)
                        mlp_used[li] = (smm, smm.n)
                        if hc == NHC - 1:
                            for i in range(8):
                                bank_fill[i] = (smm, smm.n)
                            mlp_guard[0] = smm.n

                for i in range(8):
                    @block.vector
                    def _(v, i=i, n=n):
                        use_wait_fill(v, i)
                        if len(g_adds) >= 2:
                            wait(v, sg, g_adds[-2])
                        ins = v.tensor_tensor(out=f32(C_TMP + (i % 2) * 512, 512),
                                              in0=ps[i][:],
                                              in1=f32(O_MLS + n * 512, 512),
                                              op=ALU.mult)
                        sv.tick(ins)
                        bank_free[i] = (sv, sv.n)
                    tmp_made = sv.n

                    @block.gpsimd
                    def _(g, i=i, n=n, tt0=tt0, tmp_made=tmp_made):
                        wait(g, sv, tmp_made)
                        ins = g.tensor_tensor(
                            out=f32(O_XA + (tt0 + i) * D + n * 512, 512),
                            in0=f32(O_XA + (tt0 + i) * D + n * 512, 512),
                            in1=f32(C_TMP + (i % 2) * 512, 512), op=ALU.add)
                        sg.tick(ins)
                    g_adds.append(sg.n)
        xa_after_d = sg.n
        dump_arena("D")

        nc.leave_named_scope(*_scope_st.pop(), False)
        _scope_st.append(("phE", nc.enter_named_scope("phE", False)[0]))
        C_Z = (35088, 36112)
        C_QA = (37136, 38160)
        C_QU, C_QN, C_QH, C_BM, C_QST = 39184, 40208, 41232, 42256, 42448
        xt_slot = [O_XT, O_XT + 1024]
        x2_stored = [0] * NTT
        e_adds = []
        for tt in range(NTT):
            @block.sync
            def _(y, tt=tt):
                if len(e_adds) >= 2:
                    wait(y, sv, e_adds[-2])
                qsem = qxa if tt % 2 == 0 else qxb
                ins = y.dma_start(out=f32(xt_slot[tt % 2], D),
                                  in_=xh[tt * 128:(tt + 1) * 128, :])
                qsem.tick(ins, 16)
            xt_ld = (qxa if tt % 2 == 0 else qxb, (qxa if tt % 2 == 0 else qxb).n)

            @block.vector
            def _(v, tt=tt, xt_ld=xt_ld):
                if tt == 0:
                    wait(v, sg, xa_after_d)
                wait(v, *xt_ld)
                v.tensor_tensor(out=f32(O_XA + tt * D, D), in0=f32(O_XA + tt * D, D),
                                in1=f32(xt_slot[tt % 2], D), op=ALU.add)
                v.engine_nop(); v.engine_nop(); v.engine_nop()
                ins = v.tensor_scalar(
                    out=f32(C_QA[0], D), in0=f32(O_XA + tt * D, D), scalar1=1.0,
                    scalar2=None, op0=ALU.mult, op1=ALU.add,
                    accum_out=AR[:, _stc(ST_SUM) + tt:_stc(ST_SUM) + tt + 1])
                sv.tick(ins)
            e_adds.append(sv.n)
            x2_done = sv.n

            @block.sync
            def _(y, tt=tt, x2_done=x2_done):
                wait(y, sv, x2_done)
                ins = y.dma_start(out=x2h[tt * 128:(tt + 1) * 128, :],
                                  in_=f32(O_XA + tt * D, D))
                qs.tick(ins, 16)
            x2_stored[tt] = qs.n

            @block.scalar
            def _(s, tt=tt, x2_done=x2_done):
                wait(s, sv, x2_done)
                ins = s.activation(
                    out=f32(36624, D), in_=f32(O_XA + tt * D, D), func=AF.Square,
                    accum_out=AR[:, _stc(ST_SSQ) + tt:_stc(ST_SSQ) + tt + 1])
                ss.tick(ins)
        e_ssq = ss.n

        @block.vector
        def _(v):
            wait(v, ss, e_ssq)
            v.tensor_scalar(out=f32(_stc(ST_M), 16), in0=f32(_stc(ST_SUM), 16),
                            scalar1=1.0 / D, scalar2=None, op0=ALU.mult,
                            op1=ALU.bypass)
            v.engine_nop(); v.engine_nop(); v.engine_nop()
            v.tensor_tensor(out=f32(_stc(ST_MSQ), 16), in0=f32(_stc(ST_M), 16),
                            in1=f32(_stc(ST_M), 16), op=ALU.mult)
            v.engine_nop(); v.engine_nop(); v.engine_nop()
            ins = v.scalar_tensor_tensor(
                out=f32(_stc(ST_VAR), 16), in0=f32(_stc(ST_SSQ), 16), scalar=1.0 / D,
                in1=f32(_stc(ST_MSQ), 16), op0=ALU.mult, op1=ALU.subtract)
            sv.tick(ins)
        e_var = sv.n

        @block.vector
        def _(v):
            v.engine_nop(); v.engine_nop(); v.engine_nop()
            ins = v.tensor_scalar(out=f32(_stc(ST_INV), 16), in0=f32(_stc(ST_VAR), 16),
                                  scalar1=EPS_LN, scalar2=None, op0=ALU.add,
                                  op1=ALU.bypass)
            sv.tick(ins)
        e_arg = sv.n

        @block.scalar
        def _(s):
            wait(s, sv, e_arg)
            s.activation(out=f32(_stc(ST_INV), 16), in_=f32(_stc(ST_INV), 16),
                         func=AF.Ln)
            ins = s.activation(out=f32(_stc(ST_INV), 16), in_=f32(_stc(ST_INV), 16),
                               func=AF.Exp, scale=-0.5)
            ss.tick(ins)
        e_inv = ss.n

        @block.vector
        def _(v):
            wait(v, ss, e_inv)
            ins = v.scalar_tensor_tensor(out=f32(_stc(ST_BI), 16),
                                         in0=f32(_stc(ST_M), 16), scalar=-1.0,
                                         in1=f32(_stc(ST_INV), 16),
                                         op0=ALU.mult, op1=ALU.mult)
            sv.tick(ins)
        e_smalls = sv.n

        e_q_done = []
        for tt in range(NTT):
            @block.scalar
            def _(s, tt=tt):
                if tt == 0:
                    wait(s, sv, e_smalls)
                if tt >= 2:
                    wait(s, sv, e_q_done[tt - 2])
                s.activation(out=f32(C_Z[tt % 2], D), in_=f32(O_XA + tt * D, D),
                             func=AF.Identity,
                             bias=AR[:, _stc(ST_BI) + tt:_stc(ST_BI) + tt + 1],
                             scale=AR[:, _stc(ST_INV) + tt:_stc(ST_INV) + tt + 1])
                ins = s.activation(out=f32(C_QA[tt % 2], D), in_=f32(C_Z[tt % 2], D),
                                   func=AF.Abs)
                ss.tick(ins)
            z_made = ss.n

            @block.vector
            def _(v, tt=tt, z_made=z_made):
                wait(v, ss, z_made)
                if bank_fill[7][0] is not None and tt > 0:
                    wait(v, smm, bank_fill[7][1])
                ins = emit_quant(v, C_Z[tt % 2], C_QA[tt % 2], C_QU, C_QN, C_QH,
                                 C_BM, C_QST, D)
                sv.tick(ins)
            q_made = sv.n
            e_q_done.append(q_made)

            @block.tensor
            def _(t, tt=tt, q_made=q_made):
                wait(t, sv, q_made)
                for hf, bk in ((0, 6), (1, 7)):
                    pe_wait_free(t, bk)
                    for j in range(4):
                        ins = t.transpose(
                            psb[bk][:, j * 128:(j + 1) * 128],
                            bf(C_QST, D)[:, (hf * 4 + j) * 128:(hf * 4 + j + 1) * 128],
                            ident)
                    smm.tick(ins)
                    bank_fill[bk] = (smm, smm.n)

            @block.vector
            def _(v, tt=tt):
                xqview = ARb[:, 2 * O_NRM:2 * O_NRM + 8 * T].rearrange(
                    "p (c t) -> p c t", c=8)
                for hf, bk in ((0, 6), (1, 7)):
                    use_wait_fill(v, bk)
                    ins = v.tensor_copy(
                        out=xqview[:, hf * 4:hf * 4 + 4, tt * 128:(tt + 1) * 128],
                        in_=psb[bk][:, 0:512].rearrange("p (c t) -> p c t", c=4))
                    sv.tick(ins)
                    bank_free[bk] = (sv, sv.n)
        xq_ready = sv.n
        dump_arena("E")

        nc.leave_named_scope(*_scope_st.pop(), False)
        _scope_st.append(("phF", nc.enter_named_scope("phF", False)[0]))
        def wgate_col(dc):
            return 2 * O_XA + dc * IF

        def wup_col(dc):
            if dc < 3:
                return 2 * 13584 + dc * IF
            return 2 * O_SCR + (dc - 3) * IF

        def wdn_col(ic):
            if ic < 16:
                return 2 * O_RW + ic * 1024
            if ic < 18:
                return 2 * 42128 + (ic - 16) * 1024
            if ic < 20:
                return 2 * O_WT + (ic - 18) * 1024
            return 2 * O_XT + (ic - 20) * 1024

        D_Z = (2320, 3728)
        D_QA = (5136, 6544)
        D_QU, D_QN, D_QH, D_BM, D_ST = 7952, 9360, 10768, 12176, 12440
        U_Z = (44384, 45408)
        U_QA = (46432, 47456)
        U_QU, U_QN = 272, 1296
        U_QH = 49504
        U_ST, U_BM = 17808, 18320

        fjobs = []
        for rb in range(8):
            for hq in range(2):
                fjobs.append(("d", rb, hq))
        for rb in range(NIC):
            fjobs.append(("g", rb, 0))
        for rb in range(NIC):
            fjobs.append(("u", rb, 0))

        fquant_done = []
        for jidx, (kind, rb, hq) in enumerate(fjobs):
            if kind == "d":
                width = 1408
                cz, cqa = D_Z[jidx % 2], D_QA[jidx % 2]
                cqu, cqn, cqh, cbm, cst = D_QU, D_QN, D_QH, D_BM, D_ST
            else:
                width = D
                cz, cqa = U_Z[jidx % 2], U_QA[jidx % 2]
                cqu, cqn, cqh, cbm, cst = U_QU, U_QN, U_QH, U_BM, U_ST
            qsl = qxa if jidx % 2 == 0 else qxb

            @block.scalar
            def _(s, jidx=jidx, kind=kind, rb=rb, hq=hq, cz=cz, cqa=cqa,
                  width=width, qsl=qsl):
                if jidx == 0:
                    wait(s, qs, x2_stored[NTT - 1])
                    wait(s, sv, xq_ready)
                elif jidx >= 2:
                    wait(s, sv, fquant_done[jidx - 2])
                if jidx == 16:
                    wait(s, sv, sv.n)
                if kind == "d":
                    srcap = wdh[rb * 128:(rb + 1) * 128, hq * 1408:(hq + 1) * 1408]
                elif kind == "g":
                    srcap = wgh[rb * 128:(rb + 1) * 128, :]
                else:
                    srcap = wuh[rb * 128:(rb + 1) * 128, :]
                ins = s.dma_start(out=f32(cz, width), in_=srcap)
                qsl.tick(ins, 16)
                wait(s, qsl, qsl.n)
                ins = s.activation(out=f32(cqa, width), in_=f32(cz, width), func=AF.Abs)
                ss.tick(ins)
            fz = ss.n

            @block.vector
            def _(v, jidx=jidx, fz=fz, cz=cz, cqa=cqa, cqu=cqu, cqn=cqn, cqh=cqh,
                  cbm=cbm, cst=cst, width=width):
                wait(v, ss, fz)
                if bank_fill[7][0] is not None:
                    wait(v, smm, bank_fill[7][1])
                ins = emit_quant(v, cz, cqa, cqu, cqn, cqh, cbm, cst, width)
                sv.tick(ins)
            fq = sv.n
            fquant_done.append(fq)

            ntl = width // 128

            def tile_col(j, kind=kind, rb=rb, hq=hq):
                if kind == "g":
                    return wgate_col(j) + rb * 128
                if kind == "u":
                    return wup_col(j) + rb * 128
                ic = hq * 11 + j
                return wdn_col(ic) + rb * 128

            done = 0
            gi = 0
            while done < ntl:
                bk = 6 + (gi % 2)
                cnt = min(4, ntl - done)

                @block.tensor
                def _(t, fq=fq, cst=cst, bk=bk, done=done, cnt=cnt, width=width):
                    if done == 0:
                        wait(t, sv, fq)
                    pe_wait_free(t, bk)
                    for j in range(cnt):
                        ins = t.transpose(
                            psb[bk][:, j * 128:(j + 1) * 128],
                            bf(cst, width)[:, (done + j) * 128:(done + j + 1) * 128],
                            ident)
                    smm.tick(ins)
                    bank_fill[bk] = (smm, smm.n)

                @block.vector
                def _(v, bk=bk, done=done, cnt=cnt, tile_col=tile_col):
                    use_wait_fill(v, bk)
                    for j in range(cnt):
                        ins = v.tensor_copy(
                            out=ARb[:, tile_col(done + j):tile_col(done + j) + 128],
                            in_=psb[bk][:, j * 128:(j + 1) * 128])
                    sv.tick(ins)
                    bank_free[bk] = (sv, sv.n)
                done += cnt
                gi += 1
        wq_ready = sv.n
        dump_arena("F")

        nc.leave_named_scope(*_scope_st.pop(), False)
        _scope_st.append(("phG", nc.enter_named_scope("phG", False)[0]))
        G_SIL, G_QU = 272, 1808
        H2S = (784, 49504)
        QAS = (1296, 50016)
        G_QN, G_QH, G_BM, G_ST = 44384, 44896, 45408, 45536
        G_HQ, G_X2 = 45792, 47200

        def xqv(dc, t0, w):
            return ARb[:, 2 * O_NRM + dc * T + t0: 2 * O_NRM + dc * T + t0 + w]

        y_stores = []
        g_out_done = []
        down_fill_prev = [0]
        sil_cons = [0]
        pend_evac = [None]
        for tt in range(NTT):
            @block.sync
            def _(y, tt=tt):
                if tt == 0:
                    wait(y, qs, x2_stored[NTT - 1])
                    wait(y, sv, wq_ready)
                if y_stores:
                    wait(y, qyo, y_stores[-1])
                ins = y.dma_start(out=f32(G_X2, D),
                                  in_=x2h[tt * 128:(tt + 1) * 128, :])
                qxa.tick(ins, 16)
            x2_ld = (qxa, qxa.n)

            for sl, (o0, w) in enumerate(IF_SLICES):
                gbk, ubk = (0, 2)[sl % 2], (1, 3)[sl % 2]
                tbk = (4, 7)[sl % 2]
                h2c = H2S[sl % 2]
                qac = QAS[sl % 2]

                for wnm, bk in (("gate", gbk), ("up", ubk)):
                    @block.tensor
                    def _(t, tt=tt, sl=sl, o0=o0, w=w, wnm=wnm, bk=bk):
                        if tt == 0 and sl == 0 and wnm == "gate":
                            wait(t, sv, wq_ready)
                        pe_wait_free(t, bk)
                        for dc in range(NDC):
                            col = (wgate_col(dc) if wnm == "gate" else wup_col(dc)) + o0
                            ins = t.matmul(ps[bk][:, 0:w],
                                           xqv(dc, tt * 128, 128),
                                           ARb[:, col:col + w],
                                           start=(dc == 0), stop=(dc == NDC - 1))
                        smm.tick(ins)
                        bank_fill[bk] = (smm, smm.n)

                @block.scalar
                def _(s, w=w, gbk=gbk):
                    use_wait_fill(s, gbk)
                    if sil_cons[0]:
                        wait(s, sv, sil_cons[0])
                    ins = s.activation(out=f32(G_SIL, w), in_=ps[gbk][:, 0:w],
                                       func=AF.Silu)
                    ss.tick(ins)
                sil_made = ss.n
                bank_free[gbk] = (ss, sil_made)

                @block.vector
                def _(v, w=w, sil_made=sil_made, ubk=ubk, h2c=h2c):
                    wait(v, ss, sil_made)
                    use_wait_fill(v, ubk)
                    ins = v.tensor_tensor(out=f32(h2c, w), in0=ps[ubk][:, 0:w],
                                          in1=f32(G_SIL, w), op=ALU.mult)
                    sv.tick(ins)
                h2_made = sv.n
                sil_cons[0] = h2_made
                bank_free[ubk] = (sv, h2_made)

                @block.scalar
                def _(s, w=w, h2_made=h2_made, h2c=h2c, qac=qac):
                    wait(s, sv, h2_made)
                    ins = s.activation(out=f32(qac, w), in_=f32(h2c, w), func=AF.Abs)
                    ss.tick(ins)
                habs = ss.n

                @block.vector
                def _(v, w=w, habs=habs, h2c=h2c, qac=qac, tt=tt, sl=sl):
                    if pend_evac[0] is not None:
                        psl, pw, pbk, pfill = pend_evac[0]
                        wait(v, smm, pfill)
                        if down_fill_prev[0]:
                            wait(v, smm, down_fill_prev[0])
                        nt = pw // 128
                        ins = v.tensor_copy(
                            out=bf(G_HQ, IF)[:, psl * 512:psl * 512 + nt * 128],
                            in_=psb[pbk][:, 0:nt * 128])
                        sv.tick(ins)
                        bank_free[pbk] = (sv, sv.n)
                        pend_evac[0] = None
                    wait(v, ss, habs)
                    v.engine_nop(); v.engine_nop()
                    ins = emit_quant(v, h2c, qac, G_QU, G_QN, G_QH, G_BM, G_ST, w)
                    sv.tick(ins)
                hq_made = sv.n

                @block.tensor
                def _(t, sl=sl, w=w, hq_made=hq_made, tbk=tbk):
                    wait(t, sv, hq_made)
                    pe_wait_free(t, tbk)
                    for j in range(w // 128):
                        ins = t.transpose(psb[tbk][:, j * 128:(j + 1) * 128],
                                          bf(G_ST, w)[:, j * 128:(j + 1) * 128], ident)
                    smm.tick(ins)
                    bank_fill[tbk] = (smm, smm.n)
                pend_evac[0] = (sl, w, tbk, smm.n)

            @block.vector
            def _(v, tt=tt):
                psl, pw, pbk, pfill = pend_evac[0]
                wait(v, smm, pfill)
                if down_fill_prev[0]:
                    wait(v, smm, down_fill_prev[0])
                nt = pw // 128
                ins = v.tensor_copy(
                    out=bf(G_HQ, IF)[:, psl * 512:psl * 512 + nt * 128],
                    in_=psb[pbk][:, 0:nt * 128])
                sv.tick(ins)
                bank_free[pbk] = (sv, sv.n)
                pend_evac[0] = None
            hq_fm_ready = sv.n

            @block.tensor
            def _(t, tt=tt, hq_fm_ready=hq_fm_ready):
                wait(t, sv, hq_fm_ready)
                for n in range(2):
                    bk = 5 + n
                    pe_wait_free(t, bk)
                    for ic in range(NIC):
                        ins = t.matmul(
                            ps[bk][:], bf(G_HQ, IF)[:, ic * 128:(ic + 1) * 128],
                            ARb[:, wdn_col(ic) + n * 512:wdn_col(ic) + n * 512 + 512],
                            start=(ic == 0), stop=(ic == NIC - 1))
                    smm.tick(ins)
                    bank_fill[bk] = (smm, smm.n)
                down_fill_prev[0] = smm.n

            @block.vector
            def _(v, tt=tt, x2_ld=x2_ld):
                wait(v, *x2_ld)
                for n in range(2):
                    bk = 5 + n
                    use_wait_fill(v, bk)
                    ins = v.tensor_tensor(out=f32(G_X2 + n * 512, 512),
                                          in0=ps[bk][:],
                                          in1=f32(G_X2 + n * 512, 512), op=ALU.add)
                    sv.tick(ins)
                    bank_free[bk] = (sv, sv.n)
            g_out_done.append(sv.n)

            @block.sync
            def _(y, tt=tt):
                wait(y, sv, g_out_done[-1])
                ins = y.dma_start(out=yh[tt * 128:(tt + 1) * 128, :],
                                  in_=f32(G_X2, D))
                qyo.tick(ins, 16)
            y_stores.append(qyo.n)
            if tt == 0:
                dump_arena("G0")

        nc.leave_named_scope(*_scope_st.pop(), False)

        @block.sync
        def _(y):
            wait(y, qyo, qyo.n)

    return nc


_NC_CACHE = {}


def kernel(L0_attn_x, L0_attn_Wk, L0_attn_Wv, L0_attn_Wr, L0_attn_Wo,
           L0_attn_time_decay, L0_attn_time_first, L0_attn_mixer_scale,
           L0_attn_Wfc, L0_attn_Wmlp, L0_attn_mlp_scale,
           L0_ffn_w_gate, L0_ffn_w_up, L0_ffn_w_down):
    x = np.ascontiguousarray(np.asarray(L0_attn_x, dtype=np.float32))
    B = x.shape[0]
    assert x.shape == (8, T, D)

    def t_(a):
        return np.ascontiguousarray(np.asarray(a, dtype=np.float32).T)

    shared = {
        "wkT": t_(L0_attn_Wk),
        "wvT": t_(L0_attn_Wv),
        "wrT": t_(L0_attn_Wr),
        "woT": t_(L0_attn_Wo),
        "wfcT": t_(L0_attn_Wfc),
        "wmlpT": t_(L0_attn_Wmlp),
        "wg": np.ascontiguousarray(np.asarray(L0_ffn_w_gate, dtype=np.float32)),
        "wu": np.ascontiguousarray(np.asarray(L0_ffn_w_up, dtype=np.float32)),
        "wd": np.ascontiguousarray(np.asarray(L0_ffn_w_down, dtype=np.float32)),
        "tdp": np.ascontiguousarray(
            np.asarray(L0_attn_time_decay, dtype=np.float32).reshape(8, 128).T),
        "tfp": np.ascontiguousarray(
            np.asarray(L0_attn_time_first, dtype=np.float32).reshape(8, 128).T),
        "msp": np.ascontiguousarray(np.broadcast_to(
            np.asarray(L0_attn_mixer_scale, dtype=np.float32)[None, :], (128, D))),
        "mlsp": np.ascontiguousarray(np.broadcast_to(
            np.asarray(L0_attn_mlp_scale, dtype=np.float32)[None, :], (128, D))),
        "identp": np.eye(128, dtype=ml_dtypes.bfloat16),
    }

    key = ("nc", _DBG_PHASE)
    if key not in _NC_CACHE:
        _NC_CACHE[key] = build_nc(_DBG_PHASE)
    nc = _NC_CACHE[key]

    in_maps = [dict(shared, x=np.ascontiguousarray(x[c])) for c in range(B)]
    res = run_bass_kernel_spmd(nc, in_maps, core_ids=list(range(B)))
    if _DBG_PHASE is not None:
        _NC_CACHE["dbg"] = [r.get("dbgarena") for r in res.results]
    out = np.stack([r["y"] for r in res.results], axis=0)
    return out.astype(np.float32)


# version 08
# speedup vs baseline: 1.3401x; 1.3401x over previous
import numpy as np
import ml_dtypes
from contextlib import ExitStack

import concourse.bass as bass
import concourse.mybir as mybir
from concourse.bass_utils import run_bass_kernel_spmd

F32 = mybir.dt.float32
BF16 = mybir.dt.bfloat16
AF = mybir.ActivationFunctionType
ALU = mybir.AluOpType
AXX = mybir.AxisListType.X

T = 2048
D = 1024
H = 4096
IF = 2816
NTT = 16
NDC = 8
NHC = 32
NIC = 22
IF_SLICES = [(0, 512), (512, 512), (1024, 512), (1536, 512), (2048, 512), (2560, 256)]

O_IDENT = 0
O_EW = 64
O_EU = 72
O_ST = 80
O_MS = 272
O_MLS = 1296
O_XA = 2320
O_NRM = 18704
O_RW = 26896
O_SCR = 35088
O_WT = 43360
O_WO = 44384
O_XT = 48480
ACOLS = 50688

(ST_SUM, ST_SSQ, ST_M, ST_MSQ, ST_VAR, ST_CN, ST_INV, ST_BN, ST_BI, ST_C2) = range(10)
EPS_LN = 1e-5
EPS_RMS = 1e-6


def _stc(slot):
    return O_ST + 16 * slot


class Sem:
    def __init__(self, h, name):
        self.h = h
        self.name = name
        self.n = 0

    def tick(self, ins, k=1):
        ins.then_inc(self.h, k)
        self.n += k
        return self.n


_DBG_PHASE = None


def build_nc(dbg_phase=None):
    nc = bass.Bass("TRN2", target_bir_lowering=False)
    dbgh = None
    if dbg_phase is not None:
        dbgh = nc.dram_tensor("dbgarena", [128, ACOLS], F32,
                              kind="ExternalOutput").ap()

    xh = nc.dram_tensor("x", [T, D], F32, kind="ExternalInput").ap()
    yh = nc.dram_tensor("y", [T, D], F32, kind="ExternalOutput").ap()
    x2h = nc.dram_tensor("x2s", [T, D], F32, kind="Internal").ap()
    wkh = nc.dram_tensor("wkT", [D, D], F32, kind="ExternalInput").ap()
    wvh = nc.dram_tensor("wvT", [D, D], F32, kind="ExternalInput").ap()
    wrh = nc.dram_tensor("wrT", [D, D], F32, kind="ExternalInput").ap()
    woh = nc.dram_tensor("woT", [D, D], F32, kind="ExternalInput").ap()
    wfch = nc.dram_tensor("wfcT", [D, H], F32, kind="ExternalInput").ap()
    wmlph = nc.dram_tensor("wmlpT", [H, D], F32, kind="ExternalInput").ap()
    wgh = nc.dram_tensor("wg", [IF, D], F32, kind="ExternalInput").ap()
    wuh = nc.dram_tensor("wu", [IF, D], F32, kind="ExternalInput").ap()
    wdh = nc.dram_tensor("wd", [D, IF], F32, kind="ExternalInput").ap()
    tdh = nc.dram_tensor("tdp", [128, 8], F32, kind="ExternalInput").ap()
    tfh = nc.dram_tensor("tfp", [128, 8], F32, kind="ExternalInput").ap()
    msh = nc.dram_tensor("msp", [128, D], F32, kind="ExternalInput").ap()
    mlsh = nc.dram_tensor("mlsp", [128, D], F32, kind="ExternalInput").ap()
    idh = nc.dram_tensor("identp", [128, 128], BF16, kind="ExternalInput").ap()

    with ExitStack() as es:
        AR = es.enter_context(nc.sbuf_tensor("arena", [128, ACOLS], F32))
        ps = [es.enter_context(nc.psum_tensor(f"psum{i}", [128, 512], F32))
              for i in range(8)]
        qy = Sem(es.enter_context(nc.semaphore("qy")), "qy")
        qs = Sem(es.enter_context(nc.semaphore("qs")), "qs")
        qg = Sem(es.enter_context(nc.semaphore("qg")), "qg")
        qga = Sem(es.enter_context(nc.semaphore("qga")), "qga")
        qgb = Sem(es.enter_context(nc.semaphore("qgb")), "qgb")
        qxa = Sem(es.enter_context(nc.semaphore("qxa")), "qxa")
        qxb = Sem(es.enter_context(nc.semaphore("qxb")), "qxb")
        qyo = Sem(es.enter_context(nc.semaphore("qyo")), "qyo")
        qga2 = Sem(es.enter_context(nc.semaphore("qga2")), "qga2")
        qgb2 = Sem(es.enter_context(nc.semaphore("qgb2")), "qgb2")
        qgc2 = Sem(es.enter_context(nc.semaphore("qgc2")), "qgc2")
        qgd2 = Sem(es.enter_context(nc.semaphore("qgd2")), "qgd2")
        smm = Sem(es.enter_context(nc.semaphore("smm")), "smm")
        sv = Sem(es.enter_context(nc.semaphore("sv")), "sv")
        ss = Sem(es.enter_context(nc.semaphore("ss")), "ss")
        sg = Sem(es.enter_context(nc.semaphore("sg")), "sg")
        block = es.enter_context(nc.Block())

        _scope_st = []
        ARb = AR[:].bitcast(BF16)

        def f32(c0, w):
            return AR[:, c0:c0 + w]

        def bf(c0f32, wbf):
            return ARb[:, 2 * c0f32:2 * c0f32 + wbf]

        ident = ARb[:, 0:128]
        psb = [p[:].bitcast(BF16) for p in ps]

        bank_free = [(None, 0)] * 8
        bank_fill = [(None, 0)] * 8

        def wait(e, sem, val):
            assert val <= sem.n, f"wait {sem.name}={val} > emitted {sem.n}"
            if val > 0:
                e.wait_ge(sem.h, val)

        def pe_wait_free(e, bk):
            s, v = bank_free[bk]
            if s is not None:
                wait(e, s, v)

        def use_wait_fill(e, bk):
            s, v = bank_fill[bk]
            assert s is not None, f"bank {bk} never filled"
            wait(e, s, v)

        def dump_arena(tag):
            if dbg_phase != tag:
                return
            @block.sync
            def _(y):
                wait(y, sv, sv.n)
                wait(y, ss, ss.n)
                wait(y, sg, sg.n)
                wait(y, smm, smm.n)
                wait(y, qg, qg.n)
                wait(y, qs, qs.n)
                wait(y, qga, qga.n)
                wait(y, qgb, qgb.n)
                wait(y, qxa, qxa.n)
                wait(y, qxb, qxb.n)
                wait(y, qyo, qyo.n)
                ins = y.dma_start(out=dbgh[:], in_=AR[:])
                qy.tick(ins, 16)
            barrier = qy.n
            @block.vector
            def _(v):
                wait(v, qy, barrier)
            @block.scalar
            def _(s):
                wait(s, qy, barrier)
            @block.gpsimd
            def _(g):
                wait(g, qy, barrier)
            @block.tensor
            def _(t):
                wait(t, qy, barrier)

        def nrmv(dc, t0, w):
            return ARb[:, 2 * O_NRM + dc * T + t0: 2 * O_NRM + dc * T + t0 + w]

        def rwv(dc, t0, w):
            return ARb[:, 2 * O_RW + dc * T + t0: 2 * O_RW + dc * T + t0 + w]

        def emit_quant(v, cz, cqa, cqu, cqn, cqh, cbm, cst, w):
            nb = w // 16
            zg = f32(cz, w).rearrange("p (g k) -> p g k", k=16)
            v.reduce_max(out=f32(cbm, nb), in_=zg, axis=AXX, apply_absolute_value=True)
            v.tensor_scalar(out=f32(cqn, w), in0=f32(cz, w), scalar1=0.0,
                            scalar2=None, op0=ALU.is_ge, op1=ALU.bypass)
            v.tensor_scalar(out=f32(cbm + nb, nb), in0=f32(cbm, nb), scalar1=1.0 / 6.0,
                            scalar2=1e-12, op0=ALU.mult, op1=ALU.max)
            v.tensor_scalar(out=f32(cqn, w), in0=f32(cqn, w), scalar1=2.0,
                            scalar2=-1.0, op0=ALU.mult, op1=ALU.add)
            v.reciprocal(out=f32(cbm + 2 * nb, nb), in_=f32(cbm + nb, nb))
            v.engine_nop()
            rcp_b = f32(cbm + 2 * nb, nb).unsqueeze(2).broadcast_to([128, nb, 16])
            scl_b = f32(cbm + nb, nb).unsqueeze(2).broadcast_to([128, nb, 16])
            qag = f32(cqa, w).rearrange("p (g k) -> p g k", k=16)
            qug = f32(cqu, w).rearrange("p (g k) -> p g k", k=16)
            v.tensor_tensor(out=qug, in0=qag, in1=rcp_b, op=ALU.mult)
            v.tensor_tensor(out=qag, in0=f32(cqn, w).rearrange("p (g k) -> p g k", k=16),
                            in1=scl_b, op=ALU.mult)
            u = f32(cqu, w)
            v.tensor_scalar(out=f32(cqn, w), in0=u, scalar1=0.25, scalar2=None,
                            op0=ALU.is_gt)
            v.tensor_scalar(out=f32(cqh, w), in0=u, scalar1=2.5, scalar2=None,
                            op0=ALU.is_gt)
            for tn, th in ((0.75, 3.5), (1.25, 5.0), (1.75, 5.0)):
                v.scalar_tensor_tensor(out=f32(cqn, w), in0=u, scalar=tn,
                                       in1=f32(cqn, w), op0=ALU.is_gt, op1=ALU.add)
                v.scalar_tensor_tensor(out=f32(cqh, w), in0=u, scalar=th,
                                       in1=f32(cqh, w), op0=ALU.is_gt, op1=ALU.add)
            v.engine_nop()
            v.scalar_tensor_tensor(out=f32(cqn, w), in0=f32(cqn, w), scalar=0.5,
                                   in1=f32(cqh, w), op0=ALU.mult, op1=ALU.add)
            v.engine_nop(); v.engine_nop()
            return v.tensor_tensor(
                out=bf(cst, w).rearrange("p (g k) -> p g k", k=16),
                in0=f32(cqn, w).rearrange("p (g k) -> p g k", k=16),
                in1=qag, op=ALU.mult)

        _scope_st.append(("phP0", nc.enter_named_scope("phP0", False)[0]))
        @block.sync
        def _(y):
            qy.tick(y.dma_start(out=ARb[:, 0:128], in_=idh[:]), 16)
            qy.tick(y.dma_start(out=f32(O_EW, 8), in_=tdh[:]), 16)
            qy.tick(y.dma_start(out=f32(O_EU, 8), in_=tfh[:]), 16)
            qy.tick(y.dma_start(out=f32(O_MS, D), in_=msh[:]), 16)
            qy.tick(y.dma_start(out=f32(O_MLS, D), in_=mlsh[:]), 16)
            for tt in range(NTT):
                ins = y.dma_start(out=f32(O_XA + tt * D, D),
                                  in_=xh[tt * 128:(tt + 1) * 128, :])
                qy.tick(ins, 16)
        n_pre = 16 * 5
        x_all = qy.n

        @block.scalar
        def _(s):
            wait(s, qy, x_all)
            s.activation(out=f32(O_EW, 8), in_=f32(O_EW, 8), func=AF.Exp)
            s.activation(out=f32(O_EW, 8), in_=f32(O_EW, 8), func=AF.Exp, scale=-1.0)
            ins = s.activation(out=f32(O_EU, 8), in_=f32(O_EU, 8), func=AF.Exp)
            ss.tick(ins)
        ew_ready = ss.n

        nc.leave_named_scope(*_scope_st.pop(), False)
        _scope_st.append(("phA", nc.enter_named_scope("phA", False)[0]))
        @block.vector
        def _(v):
            wait(v, qy, x_all)
            for tt in range(NTT):
                ins = v.tensor_scalar(
                    out=f32(35600, D), in0=f32(O_XA + tt * D, D), scalar1=1.0,
                    scalar2=None, op0=ALU.mult, op1=ALU.add,
                    accum_out=AR[:, _stc(ST_SUM) + tt:_stc(ST_SUM) + tt + 1])
                sv.tick(ins)

        @block.scalar
        def _(s):
            wait(s, qy, x_all)
            for tt in range(NTT):
                ins = s.activation(
                    out=f32(36624, D), in_=f32(O_XA + tt * D, D), func=AF.Square,
                    accum_out=AR[:, _stc(ST_SSQ) + tt:_stc(ST_SSQ) + tt + 1])
                ss.tick(ins)
        a_ssq = ss.n

        @block.vector
        def _(v):
            wait(v, ss, a_ssq)
            v.tensor_scalar(out=f32(_stc(ST_M), 16), in0=f32(_stc(ST_SUM), 16),
                            scalar1=1.0 / D, scalar2=None, op0=ALU.mult,
                            op1=ALU.bypass)
            v.engine_nop(); v.engine_nop(); v.engine_nop()
            v.tensor_tensor(out=f32(_stc(ST_MSQ), 16), in0=f32(_stc(ST_M), 16),
                            in1=f32(_stc(ST_M), 16), op=ALU.mult)
            v.engine_nop(); v.engine_nop(); v.engine_nop()
            ins = v.scalar_tensor_tensor(
                out=f32(_stc(ST_VAR), 16), in0=f32(_stc(ST_SSQ), 16), scalar=1.0 / D,
                in1=f32(_stc(ST_MSQ), 16), op0=ALU.mult, op1=ALU.subtract)
            sv.tick(ins)
        a_var = sv.n

        @block.vector
        def _(v):
            v.engine_nop(); v.engine_nop(); v.engine_nop()
            v.tensor_scalar(out=f32(_stc(ST_CN), 16), in0=f32(_stc(ST_VAR), 16),
                            scalar1=1.0 + EPS_RMS, scalar2=EPS_LN * EPS_RMS,
                            op0=ALU.mult, op1=ALU.add)
            ins = v.tensor_scalar(out=f32(_stc(ST_INV), 16), in0=f32(_stc(ST_VAR), 16),
                                  scalar1=EPS_LN, scalar2=None, op0=ALU.add,
                                  op1=ALU.bypass)
            sv.tick(ins)
        a_args = sv.n

        @block.scalar
        def _(s):
            wait(s, sv, a_args)
            s.activation(out=f32(_stc(ST_CN), 16), in_=f32(_stc(ST_CN), 16),
                         func=AF.Ln)
            s.activation(out=f32(_stc(ST_CN), 16), in_=f32(_stc(ST_CN), 16),
                         func=AF.Exp, scale=-0.5)
            s.activation(out=f32(_stc(ST_INV), 16), in_=f32(_stc(ST_INV), 16),
                         func=AF.Ln)
            ins = s.activation(out=f32(_stc(ST_INV), 16), in_=f32(_stc(ST_INV), 16),
                               func=AF.Exp, scale=-0.5)
            ss.tick(ins)
        a_invcn = ss.n

        @block.vector
        def _(v):
            wait(v, ss, a_invcn)
            v.scalar_tensor_tensor(out=f32(_stc(ST_BN), 16), in0=f32(_stc(ST_M), 16),
                                   scalar=-1.0, in1=f32(_stc(ST_CN), 16),
                                   op0=ALU.mult, op1=ALU.mult)
            ins = v.scalar_tensor_tensor(out=f32(_stc(ST_BI), 16),
                                         in0=f32(_stc(ST_M), 16), scalar=-1.0,
                                         in1=f32(_stc(ST_INV), 16),
                                         op0=ALU.mult, op1=ALU.mult)
            sv.tick(ins)
        a_smalls = sv.n

        N1STG = 35088
        for tt in range(NTT):
            @block.scalar
            def _(s, tt=tt):
                wait(s, sv, a_smalls)
                if bank_fill[7][0] is not None:
                    wait(s, smm, bank_fill[7][1])
                ins = s.activation(
                    out=bf(N1STG, D), in_=f32(O_XA + tt * D, D), func=AF.Identity,
                    bias=AR[:, _stc(ST_BN) + tt:_stc(ST_BN) + tt + 1],
                    scale=AR[:, _stc(ST_CN) + tt:_stc(ST_CN) + tt + 1])
                ss.tick(ins)
            n1_made = ss.n

            @block.tensor
            def _(t, tt=tt, n1_made=n1_made):
                wait(t, ss, n1_made)
                for half, bk in ((0, 6), (1, 7)):
                    pe_wait_free(t, bk)
                    for j in range(4):
                        ins = t.transpose(
                            psb[bk][:, j * 128:(j + 1) * 128],
                            bf(N1STG, D)[:, (half * 4 + j) * 128:(half * 4 + j + 1) * 128],
                            ident)
                    smm.tick(ins)
                    bank_fill[bk] = (smm, smm.n)

            @block.vector
            def _(v, tt=tt):
                nview = ARb[:, 2 * O_NRM:2 * O_NRM + 8 * T].rearrange(
                    "p (c t) -> p c t", c=8)
                for half, bk in ((0, 6), (1, 7)):
                    use_wait_fill(v, bk)
                    ins = v.tensor_copy(
                        out=nview[:, half * 4:half * 4 + 4, tt * 128:(tt + 1) * 128],
                        in_=psb[bk][:, 0:512].rearrange("p (c t) -> p c t", c=4))
                    sv.tick(ins)
                    bank_free[bk] = (sv, sv.n)
                ins = v.tensor_scalar(
                    out=f32(O_XA + tt * D, D), in0=f32(O_XA + tt * D, D),
                    scalar1=AR[:, _stc(ST_INV) + tt:_stc(ST_INV) + tt + 1],
                    scalar2=AR[:, _stc(ST_BI) + tt:_stc(ST_BI) + tt + 1],
                    op0=ALU.mult, op1=ALU.add)
                sv.tick(ins)
        normed1_ready = sv.n
        dump_arena("A")

        nc.leave_named_scope(*_scope_st.pop(), False)
        _scope_st.append(("phB", nc.enter_named_scope("phB", False)[0]))
        C_KV, C_S, C_R, C_KSB = 35088, 37152, 39232, 41280
        wt_slot = [O_WT, O_WT + 512]
        wload_count = []
        wload_used = {}
        load_idx = [0]
        mlp_guard = [None]

        def wq_load(g, src_ap, nbf=1024):
            li = load_idx[0]
            load_idx[0] += 1
            if li >= 2:
                s_, v_ = wload_used[li - 2]
                wait(g, s_, v_)
            if mlp_guard[0] is not None:
                wait(g, smm, mlp_guard[0])
                mlp_guard[0] = None
            dst = bf(wt_slot[li % 2], nbf)
            if src_ap.ndim == 3:
                dst = dst.rearrange("p (c m) -> p c m", c=8)
            qsem = qga if li % 2 == 0 else qgb
            ins = g.dma_start(out=dst, in_=src_ap)
            qsem.tick(ins, 16)
            wload_count.append((qsem, qsem.n))
            return li

        def colslice8(wh, o):
            return wh[:, o * 128:(o + 1) * 128].rearrange("(c p) m -> p c m", p=128)

        @block.vector
        def _(v):
            ins = v.memset(f32(C_S, 1), 0.0)
            sv.tick(ins)
        szero = sv.n

        for o in range(NDC):
            lis = {}

            @block.gpsimd
            def _(g, o=o):
                lis["k"] = wq_load(g, colslice8(wkh, o))
                lis["v"] = wq_load(g, colslice8(wvh, o))

            @block.tensor
            def _(t, o=o):
                if o == 0:
                    wait(t, sv, normed1_ready)
                for nm, b0 in (("k", 0), ("v", 4)):
                    li = lis[nm]
                    wait(t, *wload_count[li])
                    for bk in range(b0, b0 + 4):
                        pe_wait_free(t, bk)
                    wslot = bf(wt_slot[li % 2], 1024)
                    for t4 in range(4):
                        for dc in range(NDC):
                            ins = t.matmul(ps[b0 + t4][:],
                                           wslot[:, dc * 128:(dc + 1) * 128],
                                           nrmv(dc, t4 * 512, 512),
                                           start=(dc == 0), stop=(dc == NDC - 1))
                    smm.tick(ins)
                    for bk in range(b0, b0 + 4):
                        bank_fill[bk] = (smm, smm.n)
                    wload_used[li] = (smm, smm.n)

            @block.gpsimd
            def _(g, o=o):
                lis["r"] = wq_load(g, colslice8(wrh, o))

            @block.scalar
            def _(s, o=o):
                use_wait_fill(s, 0)
                for t4 in range(4):
                    ins = s.copy(out=f32(C_KSB + t4 * 512, 512), in_=ps[t4][:])
                ss.tick(ins)
            k_cp = ss.n
            for bk in range(4):
                bank_free[bk] = (ss, k_cp)

            @block.vector
            def _(v, o=o, k_cp=k_cp):
                wait(v, ss, k_cp)
                use_wait_fill(v, 4)
                for t4 in range(4):
                    ins = v.tensor_tensor(out=f32(C_KV + t4 * 512, 512),
                                          in0=f32(C_KSB + t4 * 512, 512),
                                          in1=ps[4 + t4][:], op=ALU.mult)
                sv.tick(ins)
            kv_made = sv.n
            for bk in range(4, 8):
                bank_free[bk] = (sv, kv_made)

            @block.tensor
            def _(t, o=o):
                li = lis["r"]
                wait(t, *wload_count[li])
                for bk in range(4):
                    pe_wait_free(t, bk)
                wslot = bf(wt_slot[li % 2], 1024)
                for t4 in range(4):
                    for dc in range(NDC):
                        ins = t.matmul(ps[t4][:], wslot[:, dc * 128:(dc + 1) * 128],
                                       nrmv(dc, t4 * 512, 512),
                                       start=(dc == 0), stop=(dc == NDC - 1))
                smm.tick(ins)
                for bk in range(4):
                    bank_fill[bk] = (smm, smm.n)
                wload_used[li] = (smm, smm.n)

            @block.scalar
            def _(s, o=o):
                use_wait_fill(s, 0)
                for t4 in range(4):
                    ins = s.activation(out=f32(C_R + t4 * 512, 512), in_=ps[t4][:],
                                       func=AF.Sigmoid)
                ss.tick(ins)
            r_made = ss.n
            for bk in range(4):
                bank_free[bk] = (ss, r_made)

            @block.vector
            def _(v, o=o, r_made=r_made):
                if o == 0:
                    wait(v, sv, szero)
                    wait(v, ss, ew_ready)
                ewb = AR[:, O_EW + o:O_EW + o + 1].broadcast_to([128, T])
                v.engine_nop(); v.engine_nop(); v.engine_nop()
                v.tensor_tensor_scan(out=f32(C_S + 1, T), data0=ewb,
                                     data1=f32(C_KV, T), initial=0.0,
                                     op0=ALU.mult, op1=ALU.add)
                v.engine_nop(); v.engine_nop(); v.engine_nop()
                v.scalar_tensor_tensor(out=f32(C_KV, T), in0=f32(C_KV, T),
                                       scalar=AR[:, O_EU + o:O_EU + o + 1],
                                       in1=f32(C_S, T), op0=ALU.mult, op1=ALU.add)
                wait(v, ss, r_made)
                v.engine_nop(); v.engine_nop(); v.engine_nop()
                ins = v.tensor_tensor(out=rwv(o, 0, T), in0=f32(C_KV, T),
                                      in1=f32(C_R, T), op=ALU.mult)
                sv.tick(ins)
        rw_ready = sv.n
        dump_arena("B")

        nc.leave_named_scope(*_scope_st.pop(), False)
        _scope_st.append(("phC", nc.enter_named_scope("phC", False)[0]))
        @block.gpsimd
        def _(g):
            for dc in range(NDC):
                ins = g.dma_start(out=bf(O_WO + dc * 512, 1024),
                                  in_=woh[dc * 128:(dc + 1) * 128, :])
                qg.tick(ins, 16)
        wo_loaded = qg.n

        C_TMP = 35088
        g_adds = []
        for tt in range(NTT):
            for n in range(2):
                bk = (tt * 2 + n) % 4

                @block.tensor
                def _(t, tt=tt, n=n, bk=bk):
                    if tt == 0 and n == 0:
                        wait(t, qg, wo_loaded)
                        wait(t, sv, rw_ready)
                    pe_wait_free(t, bk)
                    for dc in range(NDC):
                        ins = t.matmul(
                            ps[bk][:], rwv(dc, tt * 128, 128),
                            bf(O_WO + dc * 512, 1024)[:, n * 512:(n + 1) * 512],
                            start=(dc == 0), stop=(dc == NDC - 1))
                    smm.tick(ins)
                    bank_fill[bk] = (smm, smm.n)

                @block.vector
                def _(v, tt=tt, n=n, bk=bk):
                    use_wait_fill(v, bk)
                    if len(g_adds) >= 2:
                        wait(v, sg, g_adds[-2])
                    ins = v.tensor_tensor(out=f32(C_TMP + (n % 2) * 512, 512),
                                          in0=ps[bk][:],
                                          in1=f32(O_MS + n * 512, 512), op=ALU.mult)
                    sv.tick(ins)
                    bank_free[bk] = (sv, sv.n)
                tmp_made = sv.n

                @block.gpsimd
                def _(g, tt=tt, n=n, tmp_made=tmp_made):
                    wait(g, sv, tmp_made)
                    ins = g.tensor_tensor(out=f32(O_XA + tt * D + n * 512, 512),
                                          in0=f32(O_XA + tt * D + n * 512, 512),
                                          in1=f32(C_TMP + (n % 2) * 512, 512),
                                          op=ALU.add)
                    sg.tick(ins)
                g_adds.append(sg.n)
        xa_after_c = sg.n
        dump_arena("C")

        nc.leave_named_scope(*_scope_st.pop(), False)
        _scope_st.append(("phD", nc.enter_named_scope("phD", False)[0]))
        C_HSB, C_ESB = 35088, 37136
        mlp_idx = [0]
        mlp_used = {}
        mlp_count = {}
        mlp_sems = [qga2, qgb2, qgc2, qgd2]

        def g_col(hc):
            if hc < 16:
                return 2 * O_RW + hc * 1024
            if hc < 24:
                return 2 * 22800 + (hc - 16) * 1024
            return 2 * O_WO + (hc - 24) * 1024

        for half in range(2):
            tt0 = half * 8

            @block.scalar
            def _(s, half=half, tt0=tt0):
                wait(s, sg, sg.n)
                for i in range(8):
                    ins = s.activation(
                        out=f32(36624, D), in_=f32(O_XA + (tt0 + i) * D, D),
                        func=AF.Square,
                        accum_out=AR[:, _stc(ST_SSQ) + i:_stc(ST_SSQ) + i + 1])
                ss.tick(ins)
            d_ssq = ss.n

            @block.vector
            def _(v, d_ssq=d_ssq):
                wait(v, ss, d_ssq)
                ins = v.tensor_scalar(out=f32(_stc(ST_C2), 8), in0=f32(_stc(ST_SSQ), 8),
                                      scalar1=1.0 / D, scalar2=EPS_RMS,
                                      op0=ALU.mult, op1=ALU.add)
                sv.tick(ins)
            d_arg = sv.n

            @block.scalar
            def _(s, d_arg=d_arg):
                wait(s, sv, d_arg)
                s.activation(out=f32(_stc(ST_C2), 8), in_=f32(_stc(ST_C2), 8),
                             func=AF.Ln)
                ins = s.activation(out=f32(_stc(ST_C2), 8), in_=f32(_stc(ST_C2), 8),
                                   func=AF.Exp, scale=-0.5)
                ss.tick(ins)
            c2_done = ss.n

            for i in range(8):
                @block.scalar
                def _(s, i=i, tt0=tt0, c2_done=c2_done):
                    wait(s, ss, c2_done)
                    if bank_fill[7][0] is not None:
                        wait(s, smm, bank_fill[7][1])
                    ins = s.activation(
                        out=bf(N1STG, D), in_=f32(O_XA + (tt0 + i) * D, D),
                        func=AF.Identity,
                        scale=AR[:, _stc(ST_C2) + i:_stc(ST_C2) + i + 1])
                    ss.tick(ins)
                n2_made = ss.n

                @block.tensor
                def _(t, i=i, n2_made=n2_made):
                    wait(t, ss, n2_made)
                    for hf, bk in ((0, 6), (1, 7)):
                        pe_wait_free(t, bk)
                        for j in range(4):
                            ins = t.transpose(
                                psb[bk][:, j * 128:(j + 1) * 128],
                                bf(N1STG, D)[:, (hf * 4 + j) * 128:(hf * 4 + j + 1) * 128],
                                ident)
                        smm.tick(ins)
                        bank_fill[bk] = (smm, smm.n)

                @block.vector
                def _(v, i=i):
                    n2view = ARb[:, 2 * O_NRM:2 * O_NRM + 8 * 1024].rearrange(
                        "p (c t) -> p c t", c=8)
                    for hf, bk in ((0, 6), (1, 7)):
                        use_wait_fill(v, bk)
                        ins = v.tensor_copy(
                            out=n2view[:, hf * 4:hf * 4 + 4, i * 128:(i + 1) * 128],
                            in_=psb[bk][:, 0:512].rearrange("p (c t) -> p c t", c=4))
                        sv.tick(ins)
                        bank_free[bk] = (sv, sv.n)
            n2_ready = sv.n

            def n2v(dc, t0, w):
                return ARb[:, 2 * O_NRM + dc * 1024 + t0: 2 * O_NRM + dc * 1024 + t0 + w]

            for hc in range(NHC):
                lis_fc_holder = [None]

                @block.gpsimd
                def _(g, hc=hc, lis_fc_holder=lis_fc_holder):
                    lis_fc_holder[0] = wq_load(g, wfch[:, hc * 128:(hc + 1) * 128]
                                               .rearrange("(c p) m -> p c m", p=128))

                for t4 in range(2):
                    bk = t4

                    @block.tensor
                    def _(t, hc=hc, t4=t4, bk=bk, n2_ready=n2_ready,
                          lis_fc_holder=lis_fc_holder):
                        li = lis_fc_holder[0]
                        if hc == 0 and t4 == 0:
                            wait(t, sv, n2_ready)
                        wait(t, *wload_count[li])
                        pe_wait_free(t, bk)
                        wslot = bf(wt_slot[li % 2], 1024)
                        for dc in range(NDC):
                            ins = t.matmul(ps[bk][:], wslot[:, dc * 128:(dc + 1) * 128],
                                           n2v(dc, t4 * 512, 512),
                                           start=(dc == 0), stop=(dc == NDC - 1))
                        smm.tick(ins)
                        bank_fill[bk] = (smm, smm.n)
                        if t4 == 1:
                            wload_used[li] = (smm, smm.n)

                    @block.scalar
                    def _(s, t4=t4, bk=bk):
                        use_wait_fill(s, bk)
                        wait(s, sv, sv.n)
                        ins = s.copy(out=f32(C_HSB + t4 * 512, 512), in_=ps[bk][:])
                        ss.tick(ins)
                    h_cp = ss.n
                    bank_free[bk] = (ss, h_cp)

                    @block.vector
                    def _(v, hc=hc, t4=t4, h_cp=h_cp):
                        wait(v, ss, h_cp)
                        v.scalar_tensor_tensor(out=f32(C_ESB + t4 * 512, 512),
                                               in0=f32(C_HSB + t4 * 512, 512),
                                               scalar=0.25,
                                               in1=f32(C_HSB + t4 * 512, 512),
                                               op0=ALU.mult, op1=ALU.max)
                        v.engine_nop(); v.engine_nop(); v.engine_nop()
                        ins = v.tensor_tensor(
                            out=ARb[:, g_col(hc) + t4 * 512:g_col(hc) + t4 * 512 + 512],
                            in0=f32(C_HSB + t4 * 512, 512),
                            in1=f32(C_ESB + t4 * 512, 512), op=ALU.mult)
                        sv.tick(ins)
            g_ready = sv.n

            for n in range(2):
                for hc in range(NHC):
                    li_holder = [None]

                    @block.gpsimd
                    def _(g, hc=hc, n=n, li_holder=li_holder):
                        li = mlp_idx[0]
                        mlp_idx[0] += 1
                        if li >= 4:
                            s_, v_ = mlp_used[li - 4]
                            wait(g, s_, v_)
                        if li % 32 < 4 and load_idx[0] >= 1 and                                 (load_idx[0] - 1) in wload_used:
                            wait(g, *wload_used[load_idx[0] - 1])
                        qsem = mlp_sems[li % 4]
                        ins = g.dma_start(
                            out=bf(O_WT + (li % 4) * 256, 512),
                            in_=wmlph[hc * 128:(hc + 1) * 128, n * 512:(n + 1) * 512])
                        qsem.tick(ins, 16)
                        mlp_count[li] = (qsem, qsem.n)
                        li_holder[0] = li

                    @block.tensor
                    def _(t, hc=hc, n=n, li_holder=li_holder, g_ready=g_ready):
                        li = li_holder[0]
                        if hc == 0:
                            wait(t, sv, g_ready)
                            for bk in range(8):
                                pe_wait_free(t, bk)
                        wait(t, *mlp_count[li])
                        wslot = bf(O_WT + (li % 4) * 256, 512)
                        for i in range(8):
                            ins = t.matmul(
                                ps[i][:],
                                ARb[:, g_col(hc) + i * 128:g_col(hc) + i * 128 + 128],
                                wslot,
                                start=(hc == 0), stop=(hc == NHC - 1))
                        smm.tick(ins)
                        mlp_used[li] = (smm, smm.n)
                        if hc == NHC - 1:
                            for i in range(8):
                                bank_fill[i] = (smm, smm.n)
                            mlp_guard[0] = smm.n

                for i in range(8):
                    @block.vector
                    def _(v, i=i, n=n):
                        use_wait_fill(v, i)
                        if len(g_adds) >= 2:
                            wait(v, sg, g_adds[-2])
                        ins = v.tensor_tensor(out=f32(C_TMP + (i % 2) * 512, 512),
                                              in0=ps[i][:],
                                              in1=f32(O_MLS + n * 512, 512),
                                              op=ALU.mult)
                        sv.tick(ins)
                        bank_free[i] = (sv, sv.n)
                    tmp_made = sv.n

                    @block.gpsimd
                    def _(g, i=i, n=n, tt0=tt0, tmp_made=tmp_made):
                        wait(g, sv, tmp_made)
                        ins = g.tensor_tensor(
                            out=f32(O_XA + (tt0 + i) * D + n * 512, 512),
                            in0=f32(O_XA + (tt0 + i) * D + n * 512, 512),
                            in1=f32(C_TMP + (i % 2) * 512, 512), op=ALU.add)
                        sg.tick(ins)
                    g_adds.append(sg.n)
        xa_after_d = sg.n
        dump_arena("D")

        nc.leave_named_scope(*_scope_st.pop(), False)
        _scope_st.append(("phE", nc.enter_named_scope("phE", False)[0]))
        C_Z = (35088, 36112)
        C_QA = (37136, 38160)
        C_QU, C_QN, C_QH, C_BM, C_QST = 39184, 40208, 41232, 42256, 42448
        xt_slot = [O_XT, O_XT + 1024]
        x2_stored = [0] * NTT
        e_adds = []
        for tt in range(NTT):
            @block.sync
            def _(y, tt=tt):
                if len(e_adds) >= 2:
                    wait(y, sv, e_adds[-2])
                qsem = qxa if tt % 2 == 0 else qxb
                ins = y.dma_start(out=f32(xt_slot[tt % 2], D),
                                  in_=xh[tt * 128:(tt + 1) * 128, :])
                qsem.tick(ins, 16)
            xt_ld = (qxa if tt % 2 == 0 else qxb, (qxa if tt % 2 == 0 else qxb).n)

            @block.vector
            def _(v, tt=tt, xt_ld=xt_ld):
                if tt == 0:
                    wait(v, sg, xa_after_d)
                wait(v, *xt_ld)
                v.tensor_tensor(out=f32(O_XA + tt * D, D), in0=f32(O_XA + tt * D, D),
                                in1=f32(xt_slot[tt % 2], D), op=ALU.add)
                v.engine_nop(); v.engine_nop(); v.engine_nop()
                ins = v.tensor_scalar(
                    out=f32(C_QA[0], D), in0=f32(O_XA + tt * D, D), scalar1=1.0,
                    scalar2=None, op0=ALU.mult, op1=ALU.add,
                    accum_out=AR[:, _stc(ST_SUM) + tt:_stc(ST_SUM) + tt + 1])
                sv.tick(ins)
            e_adds.append(sv.n)
            x2_done = sv.n

            @block.sync
            def _(y, tt=tt, x2_done=x2_done):
                wait(y, sv, x2_done)
                ins = y.dma_start(out=x2h[tt * 128:(tt + 1) * 128, :],
                                  in_=f32(O_XA + tt * D, D))
                qs.tick(ins, 16)
            x2_stored[tt] = qs.n

            @block.scalar
            def _(s, tt=tt, x2_done=x2_done):
                wait(s, sv, x2_done)
                ins = s.activation(
                    out=f32(36624, D), in_=f32(O_XA + tt * D, D), func=AF.Square,
                    accum_out=AR[:, _stc(ST_SSQ) + tt:_stc(ST_SSQ) + tt + 1])
                ss.tick(ins)
        e_ssq = ss.n

        @block.vector
        def _(v):
            wait(v, ss, e_ssq)
            v.tensor_scalar(out=f32(_stc(ST_M), 16), in0=f32(_stc(ST_SUM), 16),
                            scalar1=1.0 / D, scalar2=None, op0=ALU.mult,
                            op1=ALU.bypass)
            v.engine_nop(); v.engine_nop(); v.engine_nop()
            v.tensor_tensor(out=f32(_stc(ST_MSQ), 16), in0=f32(_stc(ST_M), 16),
                            in1=f32(_stc(ST_M), 16), op=ALU.mult)
            v.engine_nop(); v.engine_nop(); v.engine_nop()
            ins = v.scalar_tensor_tensor(
                out=f32(_stc(ST_VAR), 16), in0=f32(_stc(ST_SSQ), 16), scalar=1.0 / D,
                in1=f32(_stc(ST_MSQ), 16), op0=ALU.mult, op1=ALU.subtract)
            sv.tick(ins)
        e_var = sv.n

        @block.vector
        def _(v):
            v.engine_nop(); v.engine_nop(); v.engine_nop()
            ins = v.tensor_scalar(out=f32(_stc(ST_INV), 16), in0=f32(_stc(ST_VAR), 16),
                                  scalar1=EPS_LN, scalar2=None, op0=ALU.add,
                                  op1=ALU.bypass)
            sv.tick(ins)
        e_arg = sv.n

        @block.scalar
        def _(s):
            wait(s, sv, e_arg)
            s.activation(out=f32(_stc(ST_INV), 16), in_=f32(_stc(ST_INV), 16),
                         func=AF.Ln)
            ins = s.activation(out=f32(_stc(ST_INV), 16), in_=f32(_stc(ST_INV), 16),
                               func=AF.Exp, scale=-0.5)
            ss.tick(ins)
        e_inv = ss.n

        @block.vector
        def _(v):
            wait(v, ss, e_inv)
            ins = v.scalar_tensor_tensor(out=f32(_stc(ST_BI), 16),
                                         in0=f32(_stc(ST_M), 16), scalar=-1.0,
                                         in1=f32(_stc(ST_INV), 16),
                                         op0=ALU.mult, op1=ALU.mult)
            sv.tick(ins)
        e_smalls = sv.n

        e_q_done = []
        for tt in range(NTT):
            @block.scalar
            def _(s, tt=tt):
                if tt == 0:
                    wait(s, sv, e_smalls)
                if tt >= 2:
                    wait(s, sv, e_q_done[tt - 2])
                s.activation(out=f32(C_Z[tt % 2], D), in_=f32(O_XA + tt * D, D),
                             func=AF.Identity,
                             bias=AR[:, _stc(ST_BI) + tt:_stc(ST_BI) + tt + 1],
                             scale=AR[:, _stc(ST_INV) + tt:_stc(ST_INV) + tt + 1])
                ins = s.activation(out=f32(C_QA[tt % 2], D), in_=f32(C_Z[tt % 2], D),
                                   func=AF.Abs)
                ss.tick(ins)
            z_made = ss.n

            @block.vector
            def _(v, tt=tt, z_made=z_made):
                wait(v, ss, z_made)
                if bank_fill[7][0] is not None and tt > 0:
                    wait(v, smm, bank_fill[7][1])
                ins = emit_quant(v, C_Z[tt % 2], C_QA[tt % 2], C_QU, C_QN, C_QH,
                                 C_BM, C_QST, D)
                sv.tick(ins)
            q_made = sv.n
            e_q_done.append(q_made)

            @block.tensor
            def _(t, tt=tt, q_made=q_made):
                wait(t, sv, q_made)
                for hf, bk in ((0, 6), (1, 7)):
                    pe_wait_free(t, bk)
                    for j in range(4):
                        ins = t.transpose(
                            psb[bk][:, j * 128:(j + 1) * 128],
                            bf(C_QST, D)[:, (hf * 4 + j) * 128:(hf * 4 + j + 1) * 128],
                            ident)
                    smm.tick(ins)
                    bank_fill[bk] = (smm, smm.n)

            @block.vector
            def _(v, tt=tt):
                xqview = ARb[:, 2 * O_NRM:2 * O_NRM + 8 * T].rearrange(
                    "p (c t) -> p c t", c=8)
                for hf, bk in ((0, 6), (1, 7)):
                    use_wait_fill(v, bk)
                    ins = v.tensor_copy(
                        out=xqview[:, hf * 4:hf * 4 + 4, tt * 128:(tt + 1) * 128],
                        in_=psb[bk][:, 0:512].rearrange("p (c t) -> p c t", c=4))
                    sv.tick(ins)
                    bank_free[bk] = (sv, sv.n)
        xq_ready = sv.n
        dump_arena("E")

        nc.leave_named_scope(*_scope_st.pop(), False)
        _scope_st.append(("phF", nc.enter_named_scope("phF", False)[0]))
        def wgate_col(dc):
            return 2 * O_XA + dc * IF

        def wup_col(dc):
            if dc < 3:
                return 2 * 13584 + dc * IF
            return 2 * O_SCR + (dc - 3) * IF

        def wdn_col(ic):
            if ic < 16:
                return 2 * O_RW + ic * 1024
            if ic < 18:
                return 2 * 42128 + (ic - 16) * 1024
            if ic < 20:
                return 2 * O_WT + (ic - 18) * 1024
            return 2 * O_XT + (ic - 20) * 1024

        D_Z = (2320, 3728)
        D_QA = (5136, 6544)
        D_QU, D_QN, D_QH, D_BM, D_ST = 7952, 9360, 10768, 12176, 12440
        U_Z = (44384, 45408)
        U_QA = (46432, 47456)
        U_QU, U_QN = 272, 1296
        U_QH = 49504
        U_ST, U_BM = 17808, 18320

        fjobs = []
        for rb in range(8):
            for hq in range(2):
                fjobs.append(("d", rb, hq))
        for rb in range(NIC):
            fjobs.append(("g", rb, 0))
        for rb in range(NIC):
            fjobs.append(("u", rb, 0))

        fquant_done = []
        for jidx, (kind, rb, hq) in enumerate(fjobs):
            if kind == "d":
                width = 1408
                cz, cqa = D_Z[jidx % 2], D_QA[jidx % 2]
                cqu, cqn, cqh, cbm, cst = D_QU, D_QN, D_QH, D_BM, D_ST
            else:
                width = D
                cz, cqa = U_Z[jidx % 2], U_QA[jidx % 2]
                cqu, cqn, cqh, cbm, cst = U_QU, U_QN, U_QH, U_BM, U_ST
            qsl = qxa if jidx % 2 == 0 else qxb

            @block.scalar
            def _(s, jidx=jidx, kind=kind, rb=rb, hq=hq, cz=cz, cqa=cqa,
                  width=width, qsl=qsl):
                if jidx == 0:
                    wait(s, qs, x2_stored[NTT - 1])
                    wait(s, sv, xq_ready)
                elif jidx >= 2:
                    wait(s, sv, fquant_done[jidx - 2])
                if jidx == 16:
                    wait(s, sv, sv.n)
                if kind == "d":
                    srcap = wdh[rb * 128:(rb + 1) * 128, hq * 1408:(hq + 1) * 1408]
                elif kind == "g":
                    srcap = wgh[rb * 128:(rb + 1) * 128, :]
                else:
                    srcap = wuh[rb * 128:(rb + 1) * 128, :]
                ins = s.dma_start(out=f32(cz, width), in_=srcap)
                qsl.tick(ins, 16)
                wait(s, qsl, qsl.n)
                ins = s.activation(out=f32(cqa, width), in_=f32(cz, width), func=AF.Abs)
                ss.tick(ins)
            fz = ss.n

            @block.vector
            def _(v, jidx=jidx, fz=fz, cz=cz, cqa=cqa, cqu=cqu, cqn=cqn, cqh=cqh,
                  cbm=cbm, cst=cst, width=width):
                wait(v, ss, fz)
                if bank_fill[7][0] is not None:
                    wait(v, smm, bank_fill[7][1])
                ins = emit_quant(v, cz, cqa, cqu, cqn, cqh, cbm, cst, width)
                sv.tick(ins)
            fq = sv.n
            fquant_done.append(fq)

            ntl = width // 128

            def tile_col(j, kind=kind, rb=rb, hq=hq):
                if kind == "g":
                    return wgate_col(j) + rb * 128
                if kind == "u":
                    return wup_col(j) + rb * 128
                ic = hq * 11 + j
                return wdn_col(ic) + rb * 128

            done = 0
            gi = 0
            while done < ntl:
                bk = 6 + (gi % 2)
                cnt = min(4, ntl - done)

                @block.tensor
                def _(t, fq=fq, cst=cst, bk=bk, done=done, cnt=cnt, width=width):
                    if done == 0:
                        wait(t, sv, fq)
                    pe_wait_free(t, bk)
                    for j in range(cnt):
                        ins = t.transpose(
                            psb[bk][:, j * 128:(j + 1) * 128],
                            bf(cst, width)[:, (done + j) * 128:(done + j + 1) * 128],
                            ident)
                    smm.tick(ins)
                    bank_fill[bk] = (smm, smm.n)

                @block.vector
                def _(v, bk=bk, done=done, cnt=cnt, tile_col=tile_col):
                    use_wait_fill(v, bk)
                    for j in range(cnt):
                        ins = v.tensor_copy(
                            out=ARb[:, tile_col(done + j):tile_col(done + j) + 128],
                            in_=psb[bk][:, j * 128:(j + 1) * 128])
                    sv.tick(ins)
                    bank_free[bk] = (sv, sv.n)
                done += cnt
                gi += 1
        wq_ready = sv.n
        dump_arena("F")

        nc.leave_named_scope(*_scope_st.pop(), False)
        _scope_st.append(("phG", nc.enter_named_scope("phG", False)[0]))
        G_SIL, G_QU = 272, 1808
        H2S = (784, 49504)
        QAS = (1296, 50016)
        G_QN, G_QH, G_BM = 44384, 44896, 45408
        STS = (45536, 48224)
        G_HQ, G_X2 = 45792, 47200

        def xqv(dc, t0, w):
            return ARb[:, 2 * O_NRM + dc * T + t0: 2 * O_NRM + dc * T + t0 + w]

        y_stores = []
        g_out_done = []
        down_fill_prev = [0]
        sil_cons = [0]
        pend_evac = [None]
        pend_evac2 = []
        pend_tp = []
        st_guard = [0, 0]
        for tt in range(NTT):
            @block.sync
            def _(y, tt=tt):
                if tt == 0:
                    wait(y, qs, x2_stored[NTT - 1])
                    wait(y, sv, wq_ready)
                if y_stores:
                    wait(y, qyo, y_stores[-1])
                ins = y.dma_start(out=f32(G_X2, D),
                                  in_=x2h[tt * 128:(tt + 1) * 128, :])
                qxa.tick(ins, 16)
            x2_ld = (qxa, qxa.n)

            for sl, (o0, w) in enumerate(IF_SLICES):
                gbk, ubk = (0, 2)[sl % 2], (1, 3)[sl % 2]
                tbk = (4, 7)[sl % 2]
                h2c = H2S[sl % 2]
                qac = QAS[sl % 2]
                stc = STS[sl % 2]

                for wnm, bk in (("gate", gbk), ("up", ubk)):
                    @block.tensor
                    def _(t, tt=tt, sl=sl, o0=o0, w=w, wnm=wnm, bk=bk):
                        if tt == 0 and sl == 0 and wnm == "gate":
                            wait(t, sv, wq_ready)
                        pe_wait_free(t, bk)
                        for dc in range(NDC):
                            col = (wgate_col(dc) if wnm == "gate" else wup_col(dc)) + o0
                            ins = t.matmul(ps[bk][:, 0:w],
                                           xqv(dc, tt * 128, 128),
                                           ARb[:, col:col + w],
                                           start=(dc == 0), stop=(dc == NDC - 1))
                        smm.tick(ins)
                        bank_fill[bk] = (smm, smm.n)

                @block.tensor
                def _(t, sl=sl):
                    if pend_tp:
                        psl, pw, ptbk, pstc, phq = pend_tp.pop(0)
                        wait(t, sv, phq)
                        pe_wait_free(t, ptbk)
                        for j in range(pw // 128):
                            ins = t.transpose(psb[ptbk][:, j * 128:(j + 1) * 128],
                                              bf(pstc, pw)[:, j * 128:(j + 1) * 128],
                                              ident)
                        smm.tick(ins)
                        bank_fill[ptbk] = (smm, smm.n)
                        st_guard[psl % 2] = smm.n
                        assert pend_evac[0] is None
                        pend_evac[0] = (psl, pw, ptbk, smm.n)

                @block.scalar
                def _(s, w=w, gbk=gbk):
                    use_wait_fill(s, gbk)
                    if sil_cons[0]:
                        wait(s, sv, sil_cons[0])
                    ins = s.activation(out=f32(G_SIL, w), in_=ps[gbk][:, 0:w],
                                       func=AF.Silu)
                    ss.tick(ins)
                sil_made = ss.n
                bank_free[gbk] = (ss, sil_made)

                @block.vector
                def _(v, w=w, sil_made=sil_made, ubk=ubk, h2c=h2c):
                    wait(v, ss, sil_made)
                    use_wait_fill(v, ubk)
                    ins = v.tensor_tensor(out=f32(h2c, w), in0=ps[ubk][:, 0:w],
                                          in1=f32(G_SIL, w), op=ALU.mult)
                    sv.tick(ins)
                h2_made = sv.n
                sil_cons[0] = h2_made
                bank_free[ubk] = (sv, h2_made)

                @block.scalar
                def _(s, w=w, h2_made=h2_made, h2c=h2c, qac=qac):
                    wait(s, sv, h2_made)
                    ins = s.activation(out=f32(qac, w), in_=f32(h2c, w), func=AF.Abs)
                    ss.tick(ins)
                habs = ss.n

                @block.vector
                def _(v, w=w, habs=habs, h2c=h2c, qac=qac, stc=stc, tt=tt, sl=sl):
                    if pend_evac[0] is not None:
                        psl, pw, pbk, pfill = pend_evac[0]
                        wait(v, smm, pfill)
                        if down_fill_prev[0]:
                            wait(v, smm, down_fill_prev[0])
                        nt = pw // 128
                        ins = v.tensor_copy(
                            out=bf(G_HQ, IF)[:, psl * 512:psl * 512 + nt * 128],
                            in_=psb[pbk][:, 0:nt * 128])
                        sv.tick(ins)
                        bank_free[pbk] = (sv, sv.n)
                        pend_evac[0] = None
                    wait(v, ss, habs)
                    if st_guard[sl % 2]:
                        wait(v, smm, st_guard[sl % 2])
                    v.engine_nop(); v.engine_nop()
                    ins = emit_quant(v, h2c, qac, G_QU, G_QN, G_QH, G_BM, stc, w)
                    sv.tick(ins)
                hq_made = sv.n
                pend_tp.append((sl, w, tbk, stc, hq_made))

            @block.tensor
            def _(t, tt=tt):
                while pend_tp:
                    psl, pw, ptbk, pstc, phq = pend_tp.pop(0)
                    wait(t, sv, phq)
                    pe_wait_free(t, ptbk)
                    for j in range(pw // 128):
                        ins = t.transpose(psb[ptbk][:, j * 128:(j + 1) * 128],
                                          bf(pstc, pw)[:, j * 128:(j + 1) * 128],
                                          ident)
                    smm.tick(ins)
                    bank_fill[ptbk] = (smm, smm.n)
                    st_guard[psl % 2] = smm.n
                    pend_evac2.append((psl, pw, ptbk, smm.n))

            @block.vector
            def _(v, tt=tt):
                if pend_evac[0] is not None:
                    pend_evac2.insert(0, pend_evac[0])
                    pend_evac[0] = None
                while pend_evac2:
                    psl, pw, pbk, pfill = pend_evac2.pop(0)
                    wait(v, smm, pfill)
                    if down_fill_prev[0]:
                        wait(v, smm, down_fill_prev[0])
                    nt = pw // 128
                    ins = v.tensor_copy(
                        out=bf(G_HQ, IF)[:, psl * 512:psl * 512 + nt * 128],
                        in_=psb[pbk][:, 0:nt * 128])
                    sv.tick(ins)
                    bank_free[pbk] = (sv, sv.n)
            hq_fm_ready = sv.n

            @block.tensor
            def _(t, tt=tt, hq_fm_ready=hq_fm_ready):
                wait(t, sv, hq_fm_ready)
                for n in range(2):
                    bk = 5 + n
                    pe_wait_free(t, bk)
                    for ic in range(NIC):
                        ins = t.matmul(
                            ps[bk][:], bf(G_HQ, IF)[:, ic * 128:(ic + 1) * 128],
                            ARb[:, wdn_col(ic) + n * 512:wdn_col(ic) + n * 512 + 512],
                            start=(ic == 0), stop=(ic == NIC - 1))
                    smm.tick(ins)
                    bank_fill[bk] = (smm, smm.n)
                down_fill_prev[0] = smm.n

            @block.vector
            def _(v, tt=tt, x2_ld=x2_ld):
                wait(v, *x2_ld)
                for n in range(2):
                    bk = 5 + n
                    use_wait_fill(v, bk)
                    ins = v.tensor_tensor(out=f32(G_X2 + n * 512, 512),
                                          in0=ps[bk][:],
                                          in1=f32(G_X2 + n * 512, 512), op=ALU.add)
                    sv.tick(ins)
                    bank_free[bk] = (sv, sv.n)
            g_out_done.append(sv.n)

            @block.sync
            def _(y, tt=tt):
                wait(y, sv, g_out_done[-1])
                ins = y.dma_start(out=yh[tt * 128:(tt + 1) * 128, :],
                                  in_=f32(G_X2, D))
                qyo.tick(ins, 16)
            y_stores.append(qyo.n)
            if tt == 0:
                dump_arena("G0")

        nc.leave_named_scope(*_scope_st.pop(), False)

        @block.sync
        def _(y):
            wait(y, qyo, qyo.n)

    return nc


_NC_CACHE = {}


def kernel(L0_attn_x, L0_attn_Wk, L0_attn_Wv, L0_attn_Wr, L0_attn_Wo,
           L0_attn_time_decay, L0_attn_time_first, L0_attn_mixer_scale,
           L0_attn_Wfc, L0_attn_Wmlp, L0_attn_mlp_scale,
           L0_ffn_w_gate, L0_ffn_w_up, L0_ffn_w_down):
    x = np.ascontiguousarray(np.asarray(L0_attn_x, dtype=np.float32))
    B = x.shape[0]
    assert x.shape == (8, T, D)

    def t_(a):
        return np.ascontiguousarray(np.asarray(a, dtype=np.float32).T)

    shared = {
        "wkT": t_(L0_attn_Wk),
        "wvT": t_(L0_attn_Wv),
        "wrT": t_(L0_attn_Wr),
        "woT": t_(L0_attn_Wo),
        "wfcT": t_(L0_attn_Wfc),
        "wmlpT": t_(L0_attn_Wmlp),
        "wg": np.ascontiguousarray(np.asarray(L0_ffn_w_gate, dtype=np.float32)),
        "wu": np.ascontiguousarray(np.asarray(L0_ffn_w_up, dtype=np.float32)),
        "wd": np.ascontiguousarray(np.asarray(L0_ffn_w_down, dtype=np.float32)),
        "tdp": np.ascontiguousarray(
            np.asarray(L0_attn_time_decay, dtype=np.float32).reshape(8, 128).T),
        "tfp": np.ascontiguousarray(
            np.asarray(L0_attn_time_first, dtype=np.float32).reshape(8, 128).T),
        "msp": np.ascontiguousarray(np.broadcast_to(
            np.asarray(L0_attn_mixer_scale, dtype=np.float32)[None, :], (128, D))),
        "mlsp": np.ascontiguousarray(np.broadcast_to(
            np.asarray(L0_attn_mlp_scale, dtype=np.float32)[None, :], (128, D))),
        "identp": np.eye(128, dtype=ml_dtypes.bfloat16),
    }

    key = ("nc", _DBG_PHASE)
    if key not in _NC_CACHE:
        _NC_CACHE[key] = build_nc(_DBG_PHASE)
    nc = _NC_CACHE[key]

    in_maps = [dict(shared, x=np.ascontiguousarray(x[c])) for c in range(B)]
    res = run_bass_kernel_spmd(nc, in_maps, core_ids=list(range(B)))
    if _DBG_PHASE is not None:
        _NC_CACHE["dbg"] = [r.get("dbgarena") for r in res.results]
    out = np.stack([r["y"] for r in res.results], axis=0)
    return out.astype(np.float32)


# version 09
# speedup vs baseline: 1.4458x; 1.0789x over previous
import numpy as np
import ml_dtypes
from contextlib import ExitStack

import concourse.bass as bass
import concourse.mybir as mybir
from concourse.bass_utils import run_bass_kernel_spmd

F32 = mybir.dt.float32
BF16 = mybir.dt.bfloat16
AF = mybir.ActivationFunctionType
ALU = mybir.AluOpType
AXX = mybir.AxisListType.X

T = 2048
D = 1024
H = 4096
IF = 2816
NTT = 16
NDC = 8
NHC = 32
NIC = 22
IF_SLICES = [(0, 512), (512, 512), (1024, 512), (1536, 512), (2048, 512), (2560, 256)]

O_IDENT = 0
O_EW = 64
O_EU = 72
O_ST = 80
O_MS = 272
O_MLS = 1296
O_XA = 2320
O_NRM = 18704
O_RW = 26896
O_SCR = 35088
O_WT = 43360
O_WO = 44384
O_XT = 48480
ACOLS = 50688

(ST_SUM, ST_SSQ, ST_M, ST_MSQ, ST_VAR, ST_CN, ST_INV, ST_BN, ST_BI, ST_C2) = range(10)
EPS_LN = 1e-5
EPS_RMS = 1e-6


def _stc(slot):
    return O_ST + 16 * slot


class Sem:
    def __init__(self, h, name):
        self.h = h
        self.name = name
        self.n = 0

    def tick(self, ins, k=1):
        ins.then_inc(self.h, k)
        self.n += k
        return self.n


_DBG_PHASE = None


def build_nc(dbg_phase=None):
    nc = bass.Bass("TRN2", target_bir_lowering=False)
    dbgh = None
    if dbg_phase is not None:
        dbgh = nc.dram_tensor("dbgarena", [128, ACOLS], F32,
                              kind="ExternalOutput").ap()

    xh = nc.dram_tensor("x", [T, D], F32, kind="ExternalInput").ap()
    yh = nc.dram_tensor("y", [T, D], F32, kind="ExternalOutput").ap()
    x2h = nc.dram_tensor("x2s", [T, D], F32, kind="Internal").ap()
    wqgh = nc.dram_tensor("wqg_s", [IF, D], BF16, kind="Internal").ap()
    wquh = nc.dram_tensor("wqu_s", [IF, D], BF16, kind="Internal").ap()
    wqdh = nc.dram_tensor("wqd_s", [D, IF], BF16, kind="Internal").ap()
    wkh = nc.dram_tensor("wkT", [D, D], F32, kind="ExternalInput").ap()
    wvh = nc.dram_tensor("wvT", [D, D], F32, kind="ExternalInput").ap()
    wrh = nc.dram_tensor("wrT", [D, D], F32, kind="ExternalInput").ap()
    woh = nc.dram_tensor("woT", [D, D], F32, kind="ExternalInput").ap()
    wfch = nc.dram_tensor("wfcT", [D, H], F32, kind="ExternalInput").ap()
    wmlph = nc.dram_tensor("wmlpT", [H, D], F32, kind="ExternalInput").ap()
    wgh = nc.dram_tensor("wg", [IF, D], F32, kind="ExternalInput").ap()
    wuh = nc.dram_tensor("wu", [IF, D], F32, kind="ExternalInput").ap()
    wdh = nc.dram_tensor("wd", [D, IF], F32, kind="ExternalInput").ap()
    tdh = nc.dram_tensor("tdp", [128, 8], F32, kind="ExternalInput").ap()
    tfh = nc.dram_tensor("tfp", [128, 8], F32, kind="ExternalInput").ap()
    msh = nc.dram_tensor("msp", [128, D], F32, kind="ExternalInput").ap()
    mlsh = nc.dram_tensor("mlsp", [128, D], F32, kind="ExternalInput").ap()
    idh = nc.dram_tensor("identp", [128, 128], BF16, kind="ExternalInput").ap()

    with ExitStack() as es:
        AR = es.enter_context(nc.sbuf_tensor("arena", [128, ACOLS], F32))
        ps = [es.enter_context(nc.psum_tensor(f"psum{i}", [128, 512], F32))
              for i in range(8)]
        qy = Sem(es.enter_context(nc.semaphore("qy")), "qy")
        qs = Sem(es.enter_context(nc.semaphore("qs")), "qs")
        qg = Sem(es.enter_context(nc.semaphore("qg")), "qg")
        qga = Sem(es.enter_context(nc.semaphore("qga")), "qga")
        qgb = Sem(es.enter_context(nc.semaphore("qgb")), "qgb")
        qxa = Sem(es.enter_context(nc.semaphore("qxa")), "qxa")
        qxb = Sem(es.enter_context(nc.semaphore("qxb")), "qxb")
        qyo = Sem(es.enter_context(nc.semaphore("qyo")), "qyo")
        qga2 = Sem(es.enter_context(nc.semaphore("qga2")), "qga2")
        qgb2 = Sem(es.enter_context(nc.semaphore("qgb2")), "qgb2")
        qgc2 = Sem(es.enter_context(nc.semaphore("qgc2")), "qgc2")
        qgd2 = Sem(es.enter_context(nc.semaphore("qgd2")), "qgd2")
        qwa = Sem(es.enter_context(nc.semaphore("qwa")), "qwa")
        qwb = Sem(es.enter_context(nc.semaphore("qwb")), "qwb")
        qfa = Sem(es.enter_context(nc.semaphore("qfa")), "qfa")
        qfb = Sem(es.enter_context(nc.semaphore("qfb")), "qfb")
        smm = Sem(es.enter_context(nc.semaphore("smm")), "smm")
        sv = Sem(es.enter_context(nc.semaphore("sv")), "sv")
        ss = Sem(es.enter_context(nc.semaphore("ss")), "ss")
        sg = Sem(es.enter_context(nc.semaphore("sg")), "sg")
        block = es.enter_context(nc.Block())

        _scope_st = []
        ARb = AR[:].bitcast(BF16)

        def f32(c0, w):
            return AR[:, c0:c0 + w]

        def bf(c0f32, wbf):
            return ARb[:, 2 * c0f32:2 * c0f32 + wbf]

        ident = ARb[:, 0:128]
        psb = [p[:].bitcast(BF16) for p in ps]

        bank_free = [(None, 0)] * 8
        bank_fill = [(None, 0)] * 8

        def wait(e, sem, val):
            assert val <= sem.n, f"wait {sem.name}={val} > emitted {sem.n}"
            if val > 0:
                e.wait_ge(sem.h, val)

        def pe_wait_free(e, bk):
            s, v = bank_free[bk]
            if s is not None:
                wait(e, s, v)

        def use_wait_fill(e, bk):
            s, v = bank_fill[bk]
            assert s is not None, f"bank {bk} never filled"
            wait(e, s, v)

        def dump_arena(tag):
            if dbg_phase != tag:
                return
            @block.sync
            def _(y):
                wait(y, sv, sv.n)
                wait(y, ss, ss.n)
                wait(y, sg, sg.n)
                wait(y, smm, smm.n)
                wait(y, qg, qg.n)
                wait(y, qs, qs.n)
                wait(y, qga, qga.n)
                wait(y, qgb, qgb.n)
                wait(y, qxa, qxa.n)
                wait(y, qxb, qxb.n)
                wait(y, qyo, qyo.n)
                ins = y.dma_start(out=dbgh[:], in_=AR[:])
                qy.tick(ins, 16)
            barrier = qy.n
            @block.vector
            def _(v):
                wait(v, qy, barrier)
            @block.scalar
            def _(s):
                wait(s, qy, barrier)
            @block.gpsimd
            def _(g):
                wait(g, qy, barrier)
            @block.tensor
            def _(t):
                wait(t, qy, barrier)

        def nrmv(dc, t0, w):
            return ARb[:, 2 * O_NRM + dc * T + t0: 2 * O_NRM + dc * T + t0 + w]

        def rwv(dc, t0, w):
            return ARb[:, 2 * O_RW + dc * T + t0: 2 * O_RW + dc * T + t0 + w]

        def emit_quant(v, cz, cqa, cqu, cqn, cqh, cbm, cst, w):
            nb = w // 16
            zg = f32(cz, w).rearrange("p (g k) -> p g k", k=16)
            v.reduce_max(out=f32(cbm, nb), in_=zg, axis=AXX, apply_absolute_value=True)
            v.tensor_scalar(out=f32(cqn, w), in0=f32(cz, w), scalar1=0.0,
                            scalar2=None, op0=ALU.is_ge, op1=ALU.bypass)
            v.tensor_scalar(out=f32(cbm + nb, nb), in0=f32(cbm, nb), scalar1=1.0 / 6.0,
                            scalar2=1e-12, op0=ALU.mult, op1=ALU.max)
            v.tensor_scalar(out=f32(cqn, w), in0=f32(cqn, w), scalar1=2.0,
                            scalar2=-1.0, op0=ALU.mult, op1=ALU.add)
            v.reciprocal(out=f32(cbm + 2 * nb, nb), in_=f32(cbm + nb, nb))
            v.engine_nop()
            rcp_b = f32(cbm + 2 * nb, nb).unsqueeze(2).broadcast_to([128, nb, 16])
            scl_b = f32(cbm + nb, nb).unsqueeze(2).broadcast_to([128, nb, 16])
            qag = f32(cqa, w).rearrange("p (g k) -> p g k", k=16)
            qug = f32(cqu, w).rearrange("p (g k) -> p g k", k=16)
            v.tensor_tensor(out=qug, in0=qag, in1=rcp_b, op=ALU.mult)
            v.tensor_tensor(out=qag, in0=f32(cqn, w).rearrange("p (g k) -> p g k", k=16),
                            in1=scl_b, op=ALU.mult)
            u = f32(cqu, w)
            v.tensor_scalar(out=f32(cqn, w), in0=u, scalar1=0.25, scalar2=None,
                            op0=ALU.is_gt)
            v.tensor_scalar(out=f32(cqh, w), in0=u, scalar1=2.5, scalar2=None,
                            op0=ALU.is_gt)
            for tn, th in ((0.75, 3.5), (1.25, 5.0), (1.75, 5.0)):
                v.scalar_tensor_tensor(out=f32(cqn, w), in0=u, scalar=tn,
                                       in1=f32(cqn, w), op0=ALU.is_gt, op1=ALU.add)
                v.scalar_tensor_tensor(out=f32(cqh, w), in0=u, scalar=th,
                                       in1=f32(cqh, w), op0=ALU.is_gt, op1=ALU.add)
            v.engine_nop()
            v.scalar_tensor_tensor(out=f32(cqn, w), in0=f32(cqn, w), scalar=0.5,
                                   in1=f32(cqh, w), op0=ALU.mult, op1=ALU.add)
            v.engine_nop(); v.engine_nop()
            return v.tensor_tensor(
                out=bf(cst, w).rearrange("p (g k) -> p g k", k=16),
                in0=f32(cqn, w).rearrange("p (g k) -> p g k", k=16),
                in1=qag, op=ALU.mult)

        _scope_st.append(("phP0", nc.enter_named_scope("phP0", False)[0]))
        @block.sync
        def _(y):
            qy.tick(y.dma_start(out=ARb[:, 0:128], in_=idh[:]), 16)
            qy.tick(y.dma_start(out=f32(O_EW, 8), in_=tdh[:]), 16)
            qy.tick(y.dma_start(out=f32(O_EU, 8), in_=tfh[:]), 16)
            qy.tick(y.dma_start(out=f32(O_MS, D), in_=msh[:]), 16)
            qy.tick(y.dma_start(out=f32(O_MLS, D), in_=mlsh[:]), 16)
            for tt in range(NTT):
                ins = y.dma_start(out=f32(O_XA + tt * D, D),
                                  in_=xh[tt * 128:(tt + 1) * 128, :])
                qy.tick(ins, 16)
        n_pre = 16 * 5
        x_all = qy.n

        @block.scalar
        def _(s):
            wait(s, qy, x_all)
            s.activation(out=f32(O_EW, 8), in_=f32(O_EW, 8), func=AF.Exp)
            s.activation(out=f32(O_EW, 8), in_=f32(O_EW, 8), func=AF.Exp, scale=-1.0)
            ins = s.activation(out=f32(O_EU, 8), in_=f32(O_EU, 8), func=AF.Exp)
            ss.tick(ins)
        ew_ready = ss.n

        nc.leave_named_scope(*_scope_st.pop(), False)
        _scope_st.append(("phA", nc.enter_named_scope("phA", False)[0]))
        @block.vector
        def _(v):
            wait(v, qy, x_all)
            for tt in range(NTT):
                ins = v.tensor_scalar(
                    out=f32(35600, D), in0=f32(O_XA + tt * D, D), scalar1=1.0,
                    scalar2=None, op0=ALU.mult, op1=ALU.add,
                    accum_out=AR[:, _stc(ST_SUM) + tt:_stc(ST_SUM) + tt + 1])
                sv.tick(ins)

        @block.scalar
        def _(s):
            wait(s, qy, x_all)
            for tt in range(NTT):
                ins = s.activation(
                    out=f32(36624, D), in_=f32(O_XA + tt * D, D), func=AF.Square,
                    accum_out=AR[:, _stc(ST_SSQ) + tt:_stc(ST_SSQ) + tt + 1])
                ss.tick(ins)
        a_ssq = ss.n

        @block.vector
        def _(v):
            wait(v, ss, a_ssq)
            v.tensor_scalar(out=f32(_stc(ST_M), 16), in0=f32(_stc(ST_SUM), 16),
                            scalar1=1.0 / D, scalar2=None, op0=ALU.mult,
                            op1=ALU.bypass)
            v.engine_nop(); v.engine_nop(); v.engine_nop()
            v.tensor_tensor(out=f32(_stc(ST_MSQ), 16), in0=f32(_stc(ST_M), 16),
                            in1=f32(_stc(ST_M), 16), op=ALU.mult)
            v.engine_nop(); v.engine_nop(); v.engine_nop()
            ins = v.scalar_tensor_tensor(
                out=f32(_stc(ST_VAR), 16), in0=f32(_stc(ST_SSQ), 16), scalar=1.0 / D,
                in1=f32(_stc(ST_MSQ), 16), op0=ALU.mult, op1=ALU.subtract)
            sv.tick(ins)
        a_var = sv.n

        @block.vector
        def _(v):
            v.engine_nop(); v.engine_nop(); v.engine_nop()
            v.tensor_scalar(out=f32(_stc(ST_CN), 16), in0=f32(_stc(ST_VAR), 16),
                            scalar1=1.0 + EPS_RMS, scalar2=EPS_LN * EPS_RMS,
                            op0=ALU.mult, op1=ALU.add)
            ins = v.tensor_scalar(out=f32(_stc(ST_INV), 16), in0=f32(_stc(ST_VAR), 16),
                                  scalar1=EPS_LN, scalar2=None, op0=ALU.add,
                                  op1=ALU.bypass)
            sv.tick(ins)
        a_args = sv.n

        @block.scalar
        def _(s):
            wait(s, sv, a_args)
            s.activation(out=f32(_stc(ST_CN), 16), in_=f32(_stc(ST_CN), 16),
                         func=AF.Ln)
            s.activation(out=f32(_stc(ST_CN), 16), in_=f32(_stc(ST_CN), 16),
                         func=AF.Exp, scale=-0.5)
            s.activation(out=f32(_stc(ST_INV), 16), in_=f32(_stc(ST_INV), 16),
                         func=AF.Ln)
            ins = s.activation(out=f32(_stc(ST_INV), 16), in_=f32(_stc(ST_INV), 16),
                               func=AF.Exp, scale=-0.5)
            ss.tick(ins)
        a_invcn = ss.n

        @block.vector
        def _(v):
            wait(v, ss, a_invcn)
            v.scalar_tensor_tensor(out=f32(_stc(ST_BN), 16), in0=f32(_stc(ST_M), 16),
                                   scalar=-1.0, in1=f32(_stc(ST_CN), 16),
                                   op0=ALU.mult, op1=ALU.mult)
            ins = v.scalar_tensor_tensor(out=f32(_stc(ST_BI), 16),
                                         in0=f32(_stc(ST_M), 16), scalar=-1.0,
                                         in1=f32(_stc(ST_INV), 16),
                                         op0=ALU.mult, op1=ALU.mult)
            sv.tick(ins)
        a_smalls = sv.n

        N1STG = 35088
        for tt in range(NTT):
            @block.scalar
            def _(s, tt=tt):
                wait(s, sv, a_smalls)
                if bank_fill[7][0] is not None:
                    wait(s, smm, bank_fill[7][1])
                ins = s.activation(
                    out=bf(N1STG, D), in_=f32(O_XA + tt * D, D), func=AF.Identity,
                    bias=AR[:, _stc(ST_BN) + tt:_stc(ST_BN) + tt + 1],
                    scale=AR[:, _stc(ST_CN) + tt:_stc(ST_CN) + tt + 1])
                ss.tick(ins)
            n1_made = ss.n

            @block.tensor
            def _(t, tt=tt, n1_made=n1_made):
                wait(t, ss, n1_made)
                for half, bk in ((0, 6), (1, 7)):
                    pe_wait_free(t, bk)
                    for j in range(4):
                        ins = t.transpose(
                            psb[bk][:, j * 128:(j + 1) * 128],
                            bf(N1STG, D)[:, (half * 4 + j) * 128:(half * 4 + j + 1) * 128],
                            ident)
                    smm.tick(ins)
                    bank_fill[bk] = (smm, smm.n)

            @block.vector
            def _(v, tt=tt):
                nview = ARb[:, 2 * O_NRM:2 * O_NRM + 8 * T].rearrange(
                    "p (c t) -> p c t", c=8)
                for half, bk in ((0, 6), (1, 7)):
                    use_wait_fill(v, bk)
                    ins = v.tensor_copy(
                        out=nview[:, half * 4:half * 4 + 4, tt * 128:(tt + 1) * 128],
                        in_=psb[bk][:, 0:512].rearrange("p (c t) -> p c t", c=4))
                    sv.tick(ins)
                    bank_free[bk] = (sv, sv.n)
                ins = v.tensor_scalar(
                    out=f32(O_XA + tt * D, D), in0=f32(O_XA + tt * D, D),
                    scalar1=AR[:, _stc(ST_INV) + tt:_stc(ST_INV) + tt + 1],
                    scalar2=AR[:, _stc(ST_BI) + tt:_stc(ST_BI) + tt + 1],
                    op0=ALU.mult, op1=ALU.add)
                sv.tick(ins)
        normed1_ready = sv.n
        dump_arena("A")

        nc.leave_named_scope(*_scope_st.pop(), False)
        _scope_st.append(("phB", nc.enter_named_scope("phB", False)[0]))
        C_KV, C_S, C_R, C_KSB = 35088, 37152, 39232, 41280
        wt_slot = [O_WT, O_WT + 512]
        wload_count = []
        wload_used = {}
        load_idx = [0]
        mlp_guard = [None]

        def wq_load(g, src_ap, nbf=1024):
            li = load_idx[0]
            load_idx[0] += 1
            if li >= 2:
                s_, v_ = wload_used[li - 2]
                wait(g, s_, v_)
            if mlp_guard[0] is not None:
                wait(g, smm, mlp_guard[0])
                mlp_guard[0] = None
            dst = bf(wt_slot[li % 2], nbf)
            if src_ap.ndim == 3:
                dst = dst.rearrange("p (c m) -> p c m", c=8)
            qsem = qga if li % 2 == 0 else qgb
            ins = g.dma_start(out=dst, in_=src_ap)
            qsem.tick(ins, 16)
            wload_count.append((qsem, qsem.n))
            return li

        def colslice8(wh, o):
            return wh[:, o * 128:(o + 1) * 128].rearrange("(c p) m -> p c m", p=128)

        @block.vector
        def _(v):
            ins = v.memset(f32(C_S, 1), 0.0)
            sv.tick(ins)
        szero = sv.n

        for o in range(NDC):
            lis = {}

            @block.gpsimd
            def _(g, o=o):
                lis["k"] = wq_load(g, colslice8(wkh, o))
                lis["v"] = wq_load(g, colslice8(wvh, o))

            @block.tensor
            def _(t, o=o):
                if o == 0:
                    wait(t, sv, normed1_ready)
                for nm, b0 in (("k", 0), ("v", 4)):
                    li = lis[nm]
                    wait(t, *wload_count[li])
                    for bk in range(b0, b0 + 4):
                        pe_wait_free(t, bk)
                    wslot = bf(wt_slot[li % 2], 1024)
                    for t4 in range(4):
                        for dc in range(NDC):
                            ins = t.matmul(ps[b0 + t4][:],
                                           wslot[:, dc * 128:(dc + 1) * 128],
                                           nrmv(dc, t4 * 512, 512),
                                           start=(dc == 0), stop=(dc == NDC - 1))
                    smm.tick(ins)
                    for bk in range(b0, b0 + 4):
                        bank_fill[bk] = (smm, smm.n)
                    wload_used[li] = (smm, smm.n)

            @block.gpsimd
            def _(g, o=o):
                lis["r"] = wq_load(g, colslice8(wrh, o))

            @block.scalar
            def _(s, o=o):
                use_wait_fill(s, 0)
                for t4 in range(4):
                    ins = s.copy(out=f32(C_KSB + t4 * 512, 512), in_=ps[t4][:])
                ss.tick(ins)
            k_cp = ss.n
            for bk in range(4):
                bank_free[bk] = (ss, k_cp)

            @block.vector
            def _(v, o=o, k_cp=k_cp):
                wait(v, ss, k_cp)
                use_wait_fill(v, 4)
                for t4 in range(4):
                    ins = v.tensor_tensor(out=f32(C_KV + t4 * 512, 512),
                                          in0=f32(C_KSB + t4 * 512, 512),
                                          in1=ps[4 + t4][:], op=ALU.mult)
                sv.tick(ins)
            kv_made = sv.n
            for bk in range(4, 8):
                bank_free[bk] = (sv, kv_made)

            @block.tensor
            def _(t, o=o):
                li = lis["r"]
                wait(t, *wload_count[li])
                for bk in range(4):
                    pe_wait_free(t, bk)
                wslot = bf(wt_slot[li % 2], 1024)
                for t4 in range(4):
                    for dc in range(NDC):
                        ins = t.matmul(ps[t4][:], wslot[:, dc * 128:(dc + 1) * 128],
                                       nrmv(dc, t4 * 512, 512),
                                       start=(dc == 0), stop=(dc == NDC - 1))
                smm.tick(ins)
                for bk in range(4):
                    bank_fill[bk] = (smm, smm.n)
                wload_used[li] = (smm, smm.n)

            @block.scalar
            def _(s, o=o):
                use_wait_fill(s, 0)
                for t4 in range(4):
                    ins = s.activation(out=f32(C_R + t4 * 512, 512), in_=ps[t4][:],
                                       func=AF.Sigmoid)
                ss.tick(ins)
            r_made = ss.n
            for bk in range(4):
                bank_free[bk] = (ss, r_made)

            @block.vector
            def _(v, o=o, r_made=r_made):
                if o == 0:
                    wait(v, sv, szero)
                    wait(v, ss, ew_ready)
                ewb = AR[:, O_EW + o:O_EW + o + 1].broadcast_to([128, T])
                v.engine_nop(); v.engine_nop(); v.engine_nop()
                v.tensor_tensor_scan(out=f32(C_S + 1, T), data0=ewb,
                                     data1=f32(C_KV, T), initial=0.0,
                                     op0=ALU.mult, op1=ALU.add)
                v.engine_nop(); v.engine_nop(); v.engine_nop()
                v.scalar_tensor_tensor(out=f32(C_KV, T), in0=f32(C_KV, T),
                                       scalar=AR[:, O_EU + o:O_EU + o + 1],
                                       in1=f32(C_S, T), op0=ALU.mult, op1=ALU.add)
                wait(v, ss, r_made)
                v.engine_nop(); v.engine_nop(); v.engine_nop()
                ins = v.tensor_tensor(out=rwv(o, 0, T), in0=f32(C_KV, T),
                                      in1=f32(C_R, T), op=ALU.mult)
                sv.tick(ins)
        rw_ready = sv.n
        dump_arena("B")

        nc.leave_named_scope(*_scope_st.pop(), False)
        _scope_st.append(("phC", nc.enter_named_scope("phC", False)[0]))
        FP_Z = (48480, 48992, 49504)
        FP_QA = (39184, 39696, 40208)
        FP_QU, FP_QN, FP_QH, FP_BM = 40720, 41232, 41744, 42256
        FP_ST = (42352, 42608)
        fpjobs = []
        for rb in range(NIC):
            fpjobs.append(("g", rb, 0, 512))
            fpjobs.append(("g", rb, 512, 512))
        for rb in range(NIC):
            fpjobs.append(("u", rb, 0, 512))
            fpjobs.append(("u", rb, 512, 512))
        for rb in range(8):
            for c0 in range(0, IF, 512):
                fpjobs.append(("d", rb, c0, min(512, IF - c0)))
        fp_q_done = []
        fp_state = [0]

        def fp_src(kind, rb, c0, w):
            wsrc = {"g": wgh, "u": wuh, "d": wdh}[kind]
            return wsrc[rb * 128:(rb + 1) * 128, c0:c0 + w]

        def fp_dst(kind, rb, c0, w):
            wdst = {"g": wqgh, "u": wquh, "d": wqdh}[kind]
            return wdst[rb * 128:(rb + 1) * 128, c0:c0 + w]

        def fp_emit_wb(y, j):
            wait(y, sv, fp_q_done[j])
            ins = y.dma_start(out=fp_dst(*fpjobs[j]),
                              in_=bf(FP_ST[j % 2], fpjobs[j][3]))
            (qwa if j % 2 == 0 else qwb).tick(ins, 16)

        def emit_fp_job():
            j = fp_state[0]
            if j >= len(fpjobs):
                return
            fp_state[0] += 1
            kind, rb, c0, w = fpjobs[j]
            qf = qfa if j % 2 == 0 else qfb

            @block.sync
            def _(y, j=j, kind=kind, rb=rb, c0=c0, w=w, qf=qf):
                if j == 0:
                    wait(y, sv, rw_ready)
                if j >= 3:
                    wait(y, sv, fp_q_done[j - 3])
                ins = y.dma_start(out=f32(FP_Z[j % 3], w),
                                  in_=fp_src(kind, rb, c0, w))
                qf.tick(ins, 16)
                if j >= 1:
                    fp_emit_wb(y, j - 1)

            @block.scalar
            def _(s, j=j, w=w, qf=qf):
                if j == 0:
                    wait(s, sv, rw_ready)
                if j >= 3:
                    wait(s, sv, fp_q_done[j - 3])
                wait(s, qf, qf.n)
                ins = s.activation(out=f32(FP_QA[j % 3], w),
                                   in_=f32(FP_Z[j % 3], w), func=AF.Abs)
                ss.tick(ins)
            fp_abs = ss.n

            @block.vector
            def _(v, j=j, w=w, fp_abs=fp_abs):
                wait(v, ss, fp_abs)
                if j >= 2:
                    qwp = qwa if j % 2 == 0 else qwb
                    wait(v, qwp, qwp.n)
                ins = emit_quant(v, FP_Z[j % 3], FP_QA[j % 3], FP_QU, FP_QN,
                                 FP_QH, FP_BM, FP_ST[j % 2], w)
                sv.tick(ins)
            fp_q_done.append(sv.n)

        def emit_fp_flush():
            @block.sync
            def _(y):
                fp_emit_wb(y, len(fpjobs) - 1)

        @block.gpsimd
        def _(g):
            for dc in range(NDC):
                ins = g.dma_start(out=bf(O_WO + dc * 512, 1024),
                                  in_=woh[dc * 128:(dc + 1) * 128, :])
                qg.tick(ins, 16)
        wo_loaded = qg.n

        C_TMP = 35088
        g_adds = []
        for tt in range(NTT):
            for n in range(2):
                bk = (tt * 2 + n) % 4

                @block.tensor
                def _(t, tt=tt, n=n, bk=bk):
                    if tt == 0 and n == 0:
                        wait(t, qg, wo_loaded)
                        wait(t, sv, rw_ready)
                    pe_wait_free(t, bk)
                    for dc in range(NDC):
                        ins = t.matmul(
                            ps[bk][:], rwv(dc, tt * 128, 128),
                            bf(O_WO + dc * 512, 1024)[:, n * 512:(n + 1) * 512],
                            start=(dc == 0), stop=(dc == NDC - 1))
                    smm.tick(ins)
                    bank_fill[bk] = (smm, smm.n)

                @block.vector
                def _(v, tt=tt, n=n, bk=bk):
                    use_wait_fill(v, bk)
                    if len(g_adds) >= 2:
                        wait(v, sg, g_adds[-2])
                    ins = v.tensor_tensor(out=f32(C_TMP + (n % 2) * 512, 512),
                                          in0=ps[bk][:],
                                          in1=f32(O_MS + n * 512, 512), op=ALU.mult)
                    sv.tick(ins)
                    bank_free[bk] = (sv, sv.n)
                tmp_made = sv.n

                @block.gpsimd
                def _(g, tt=tt, n=n, tmp_made=tmp_made):
                    wait(g, sv, tmp_made)
                    ins = g.tensor_tensor(out=f32(O_XA + tt * D + n * 512, 512),
                                          in0=f32(O_XA + tt * D + n * 512, 512),
                                          in1=f32(C_TMP + (n % 2) * 512, 512),
                                          op=ALU.add)
                    sg.tick(ins)
                g_adds.append(sg.n)
                emit_fp_job()
        xa_after_c = sg.n
        dump_arena("C")

        nc.leave_named_scope(*_scope_st.pop(), False)
        _scope_st.append(("phD", nc.enter_named_scope("phD", False)[0]))
        C_HSB, C_ESB = 35088, 37136
        mlp_idx = [0]
        mlp_used = {}
        mlp_count = {}
        mlp_sems = [qga2, qgb2, qgc2, qgd2]

        def g_col(hc):
            if hc < 16:
                return 2 * O_RW + hc * 1024
            if hc < 24:
                return 2 * 22800 + (hc - 16) * 1024
            return 2 * O_WO + (hc - 24) * 1024

        hsb_used = [0, 0]
        for half in range(2):
            tt0 = half * 8

            @block.scalar
            def _(s, half=half, tt0=tt0):
                wait(s, sg, sg.n)
                for i in range(8):
                    ins = s.activation(
                        out=f32(36624, D), in_=f32(O_XA + (tt0 + i) * D, D),
                        func=AF.Square,
                        accum_out=AR[:, _stc(ST_SSQ) + i:_stc(ST_SSQ) + i + 1])
                ss.tick(ins)
            d_ssq = ss.n

            @block.vector
            def _(v, d_ssq=d_ssq):
                wait(v, ss, d_ssq)
                ins = v.tensor_scalar(out=f32(_stc(ST_C2), 8), in0=f32(_stc(ST_SSQ), 8),
                                      scalar1=1.0 / D, scalar2=EPS_RMS,
                                      op0=ALU.mult, op1=ALU.add)
                sv.tick(ins)
            d_arg = sv.n

            @block.scalar
            def _(s, d_arg=d_arg):
                wait(s, sv, d_arg)
                s.activation(out=f32(_stc(ST_C2), 8), in_=f32(_stc(ST_C2), 8),
                             func=AF.Ln)
                ins = s.activation(out=f32(_stc(ST_C2), 8), in_=f32(_stc(ST_C2), 8),
                                   func=AF.Exp, scale=-0.5)
                ss.tick(ins)
            c2_done = ss.n

            for i in range(8):
                @block.scalar
                def _(s, i=i, tt0=tt0, c2_done=c2_done):
                    wait(s, ss, c2_done)
                    if bank_fill[7][0] is not None:
                        wait(s, smm, bank_fill[7][1])
                    ins = s.activation(
                        out=bf(N1STG, D), in_=f32(O_XA + (tt0 + i) * D, D),
                        func=AF.Identity,
                        scale=AR[:, _stc(ST_C2) + i:_stc(ST_C2) + i + 1])
                    ss.tick(ins)
                n2_made = ss.n

                @block.tensor
                def _(t, i=i, n2_made=n2_made):
                    wait(t, ss, n2_made)
                    for hf, bk in ((0, 6), (1, 7)):
                        pe_wait_free(t, bk)
                        for j in range(4):
                            ins = t.transpose(
                                psb[bk][:, j * 128:(j + 1) * 128],
                                bf(N1STG, D)[:, (hf * 4 + j) * 128:(hf * 4 + j + 1) * 128],
                                ident)
                        smm.tick(ins)
                        bank_fill[bk] = (smm, smm.n)

                @block.vector
                def _(v, i=i):
                    n2view = ARb[:, 2 * O_NRM:2 * O_NRM + 8 * 1024].rearrange(
                        "p (c t) -> p c t", c=8)
                    for hf, bk in ((0, 6), (1, 7)):
                        use_wait_fill(v, bk)
                        ins = v.tensor_copy(
                            out=n2view[:, hf * 4:hf * 4 + 4, i * 128:(i + 1) * 128],
                            in_=psb[bk][:, 0:512].rearrange("p (c t) -> p c t", c=4))
                        sv.tick(ins)
                        bank_free[bk] = (sv, sv.n)
                emit_fp_job()
            n2_ready = sv.n

            def n2v(dc, t0, w):
                return ARb[:, 2 * O_NRM + dc * 1024 + t0: 2 * O_NRM + dc * 1024 + t0 + w]

            for hc in range(NHC):
                lis_fc_holder = [None]

                @block.gpsimd
                def _(g, hc=hc, lis_fc_holder=lis_fc_holder):
                    lis_fc_holder[0] = wq_load(g, wfch[:, hc * 128:(hc + 1) * 128]
                                               .rearrange("(c p) m -> p c m", p=128))

                for t4 in range(2):
                    bk = t4

                    @block.tensor
                    def _(t, hc=hc, t4=t4, bk=bk, n2_ready=n2_ready,
                          lis_fc_holder=lis_fc_holder):
                        li = lis_fc_holder[0]
                        if hc == 0 and t4 == 0:
                            wait(t, sv, n2_ready)
                        wait(t, *wload_count[li])
                        pe_wait_free(t, bk)
                        wslot = bf(wt_slot[li % 2], 1024)
                        for dc in range(NDC):
                            ins = t.matmul(ps[bk][:], wslot[:, dc * 128:(dc + 1) * 128],
                                           n2v(dc, t4 * 512, 512),
                                           start=(dc == 0), stop=(dc == NDC - 1))
                        smm.tick(ins)
                        bank_fill[bk] = (smm, smm.n)
                        if t4 == 1:
                            wload_used[li] = (smm, smm.n)

                    @block.scalar
                    def _(s, t4=t4, bk=bk):
                        use_wait_fill(s, bk)
                        wait(s, sv, hsb_used[t4])
                        ins = s.copy(out=f32(C_HSB + t4 * 512, 512), in_=ps[bk][:])
                        ss.tick(ins)
                    h_cp = ss.n
                    bank_free[bk] = (ss, h_cp)

                    @block.vector
                    def _(v, hc=hc, t4=t4, h_cp=h_cp):
                        wait(v, ss, h_cp)
                        v.scalar_tensor_tensor(out=f32(C_ESB + t4 * 512, 512),
                                               in0=f32(C_HSB + t4 * 512, 512),
                                               scalar=0.25,
                                               in1=f32(C_HSB + t4 * 512, 512),
                                               op0=ALU.mult, op1=ALU.max)
                        v.engine_nop(); v.engine_nop(); v.engine_nop()
                        ins = v.tensor_tensor(
                            out=ARb[:, g_col(hc) + t4 * 512:g_col(hc) + t4 * 512 + 512],
                            in0=f32(C_HSB + t4 * 512, 512),
                            in1=f32(C_ESB + t4 * 512, 512), op=ALU.mult)
                        sv.tick(ins)
                    hsb_used[t4] = sv.n
                emit_fp_job()
            g_ready = sv.n

            for n in range(2):
                for hc in range(NHC):
                    li_holder = [None]

                    @block.gpsimd
                    def _(g, hc=hc, n=n, li_holder=li_holder):
                        li = mlp_idx[0]
                        mlp_idx[0] += 1
                        if li >= 4:
                            s_, v_ = mlp_used[li - 4]
                            wait(g, s_, v_)
                        if li % 32 < 4 and load_idx[0] >= 1 and                                 (load_idx[0] - 1) in wload_used:
                            wait(g, *wload_used[load_idx[0] - 1])
                        qsem = mlp_sems[li % 4]
                        ins = g.dma_start(
                            out=bf(O_WT + (li % 4) * 256, 512),
                            in_=wmlph[hc * 128:(hc + 1) * 128, n * 512:(n + 1) * 512])
                        qsem.tick(ins, 16)
                        mlp_count[li] = (qsem, qsem.n)
                        li_holder[0] = li

                    @block.tensor
                    def _(t, hc=hc, n=n, li_holder=li_holder, g_ready=g_ready):
                        li = li_holder[0]
                        if hc == 0:
                            wait(t, sv, g_ready)
                            for bk in range(8):
                                pe_wait_free(t, bk)
                        wait(t, *mlp_count[li])
                        wslot = bf(O_WT + (li % 4) * 256, 512)
                        for i in range(8):
                            ins = t.matmul(
                                ps[i][:],
                                ARb[:, g_col(hc) + i * 128:g_col(hc) + i * 128 + 128],
                                wslot,
                                start=(hc == 0), stop=(hc == NHC - 1))
                        smm.tick(ins)
                        mlp_used[li] = (smm, smm.n)
                        if hc == NHC - 1:
                            for i in range(8):
                                bank_fill[i] = (smm, smm.n)
                            mlp_guard[0] = smm.n
                    if hc % 4 == 0:
                        emit_fp_job()

                for i in range(8):
                    @block.vector
                    def _(v, i=i, n=n):
                        use_wait_fill(v, i)
                        if len(g_adds) >= 2:
                            wait(v, sg, g_adds[-2])
                        ins = v.tensor_tensor(out=f32(C_TMP + (i % 2) * 512, 512),
                                              in0=ps[i][:],
                                              in1=f32(O_MLS + n * 512, 512),
                                              op=ALU.mult)
                        sv.tick(ins)
                        bank_free[i] = (sv, sv.n)
                    tmp_made = sv.n

                    @block.gpsimd
                    def _(g, i=i, n=n, tt0=tt0, tmp_made=tmp_made):
                        wait(g, sv, tmp_made)
                        ins = g.tensor_tensor(
                            out=f32(O_XA + (tt0 + i) * D + n * 512, 512),
                            in0=f32(O_XA + (tt0 + i) * D + n * 512, 512),
                            in1=f32(C_TMP + (i % 2) * 512, 512), op=ALU.add)
                        sg.tick(ins)
                    g_adds.append(sg.n)
        while fp_state[0] < len(fpjobs):
            emit_fp_job()
        emit_fp_flush()
        fp_last_q = fp_q_done[-1]
        xa_after_d = sg.n
        dump_arena("D")

        nc.leave_named_scope(*_scope_st.pop(), False)
        _scope_st.append(("phE", nc.enter_named_scope("phE", False)[0]))
        C_Z = (35088, 36112)
        C_QA = (37136, 38160)
        C_QU, C_QN, C_QH, C_BM, C_QST = 39184, 40208, 41232, 42256, 42448
        xt_slot = [O_XT, O_XT + 1024]
        x2_stored = [0] * NTT
        e_adds = []
        for tt in range(NTT):
            @block.sync
            def _(y, tt=tt):
                if tt == 0:
                    wait(y, sv, fp_last_q)
                if len(e_adds) >= 2:
                    wait(y, sv, e_adds[-2])
                qsem = qxa if tt % 2 == 0 else qxb
                ins = y.dma_start(out=f32(xt_slot[tt % 2], D),
                                  in_=xh[tt * 128:(tt + 1) * 128, :])
                qsem.tick(ins, 16)
            xt_ld = (qxa if tt % 2 == 0 else qxb, (qxa if tt % 2 == 0 else qxb).n)

            @block.vector
            def _(v, tt=tt, xt_ld=xt_ld):
                if tt == 0:
                    wait(v, sg, xa_after_d)
                wait(v, *xt_ld)
                v.tensor_tensor(out=f32(O_XA + tt * D, D), in0=f32(O_XA + tt * D, D),
                                in1=f32(xt_slot[tt % 2], D), op=ALU.add)
                v.engine_nop(); v.engine_nop(); v.engine_nop()
                ins = v.tensor_scalar(
                    out=f32(C_QA[0], D), in0=f32(O_XA + tt * D, D), scalar1=1.0,
                    scalar2=None, op0=ALU.mult, op1=ALU.add,
                    accum_out=AR[:, _stc(ST_SUM) + tt:_stc(ST_SUM) + tt + 1])
                sv.tick(ins)
            e_adds.append(sv.n)
            x2_done = sv.n

            @block.sync
            def _(y, tt=tt, x2_done=x2_done):
                wait(y, sv, x2_done)
                ins = y.dma_start(out=x2h[tt * 128:(tt + 1) * 128, :],
                                  in_=f32(O_XA + tt * D, D))
                qs.tick(ins, 16)
            x2_stored[tt] = qs.n

            @block.scalar
            def _(s, tt=tt, x2_done=x2_done):
                wait(s, sv, x2_done)
                ins = s.activation(
                    out=f32(36624, D), in_=f32(O_XA + tt * D, D), func=AF.Square,
                    accum_out=AR[:, _stc(ST_SSQ) + tt:_stc(ST_SSQ) + tt + 1])
                ss.tick(ins)
        e_ssq = ss.n

        @block.vector
        def _(v):
            wait(v, ss, e_ssq)
            v.tensor_scalar(out=f32(_stc(ST_M), 16), in0=f32(_stc(ST_SUM), 16),
                            scalar1=1.0 / D, scalar2=None, op0=ALU.mult,
                            op1=ALU.bypass)
            v.engine_nop(); v.engine_nop(); v.engine_nop()
            v.tensor_tensor(out=f32(_stc(ST_MSQ), 16), in0=f32(_stc(ST_M), 16),
                            in1=f32(_stc(ST_M), 16), op=ALU.mult)
            v.engine_nop(); v.engine_nop(); v.engine_nop()
            ins = v.scalar_tensor_tensor(
                out=f32(_stc(ST_VAR), 16), in0=f32(_stc(ST_SSQ), 16), scalar=1.0 / D,
                in1=f32(_stc(ST_MSQ), 16), op0=ALU.mult, op1=ALU.subtract)
            sv.tick(ins)
        e_var = sv.n

        @block.vector
        def _(v):
            v.engine_nop(); v.engine_nop(); v.engine_nop()
            ins = v.tensor_scalar(out=f32(_stc(ST_INV), 16), in0=f32(_stc(ST_VAR), 16),
                                  scalar1=EPS_LN, scalar2=None, op0=ALU.add,
                                  op1=ALU.bypass)
            sv.tick(ins)
        e_arg = sv.n

        @block.scalar
        def _(s):
            wait(s, sv, e_arg)
            s.activation(out=f32(_stc(ST_INV), 16), in_=f32(_stc(ST_INV), 16),
                         func=AF.Ln)
            ins = s.activation(out=f32(_stc(ST_INV), 16), in_=f32(_stc(ST_INV), 16),
                               func=AF.Exp, scale=-0.5)
            ss.tick(ins)
        e_inv = ss.n

        @block.vector
        def _(v):
            wait(v, ss, e_inv)
            ins = v.scalar_tensor_tensor(out=f32(_stc(ST_BI), 16),
                                         in0=f32(_stc(ST_M), 16), scalar=-1.0,
                                         in1=f32(_stc(ST_INV), 16),
                                         op0=ALU.mult, op1=ALU.mult)
            sv.tick(ins)
        e_smalls = sv.n

        e_q_done = []
        for tt in range(NTT):
            @block.scalar
            def _(s, tt=tt):
                if tt == 0:
                    wait(s, sv, e_smalls)
                if tt >= 2:
                    wait(s, sv, e_q_done[tt - 2])
                s.activation(out=f32(C_Z[tt % 2], D), in_=f32(O_XA + tt * D, D),
                             func=AF.Identity,
                             bias=AR[:, _stc(ST_BI) + tt:_stc(ST_BI) + tt + 1],
                             scale=AR[:, _stc(ST_INV) + tt:_stc(ST_INV) + tt + 1])
                ins = s.activation(out=f32(C_QA[tt % 2], D), in_=f32(C_Z[tt % 2], D),
                                   func=AF.Abs)
                ss.tick(ins)
            z_made = ss.n

            @block.vector
            def _(v, tt=tt, z_made=z_made):
                if tt == 0:
                    wait(v, qwa, qwa.n)
                    wait(v, qwb, qwb.n)
                wait(v, ss, z_made)
                if bank_fill[7][0] is not None and tt > 0:
                    wait(v, smm, bank_fill[7][1])
                ins = emit_quant(v, C_Z[tt % 2], C_QA[tt % 2], C_QU, C_QN, C_QH,
                                 C_BM, C_QST, D)
                sv.tick(ins)
            q_made = sv.n
            e_q_done.append(q_made)

            @block.tensor
            def _(t, tt=tt, q_made=q_made):
                wait(t, sv, q_made)
                for hf, bk in ((0, 6), (1, 7)):
                    pe_wait_free(t, bk)
                    for j in range(4):
                        ins = t.transpose(
                            psb[bk][:, j * 128:(j + 1) * 128],
                            bf(C_QST, D)[:, (hf * 4 + j) * 128:(hf * 4 + j + 1) * 128],
                            ident)
                    smm.tick(ins)
                    bank_fill[bk] = (smm, smm.n)

            @block.vector
            def _(v, tt=tt):
                xqview = ARb[:, 2 * O_NRM:2 * O_NRM + 8 * T].rearrange(
                    "p (c t) -> p c t", c=8)
                for hf, bk in ((0, 6), (1, 7)):
                    use_wait_fill(v, bk)
                    ins = v.tensor_copy(
                        out=xqview[:, hf * 4:hf * 4 + 4, tt * 128:(tt + 1) * 128],
                        in_=psb[bk][:, 0:512].rearrange("p (c t) -> p c t", c=4))
                    sv.tick(ins)
                    bank_free[bk] = (sv, sv.n)
        xq_ready = sv.n
        dump_arena("E")

        nc.leave_named_scope(*_scope_st.pop(), False)
        _scope_st.append(("phF", nc.enter_named_scope("phF", False)[0]))
        def wgate_col(dc):
            return 2 * O_XA + dc * IF

        def wup_col(dc):
            if dc < 3:
                return 2 * 13584 + dc * IF
            return 2 * O_SCR + (dc - 3) * IF

        def wdn_col(ic):
            if ic < 16:
                return 2 * O_RW + ic * 1024
            if ic < 18:
                return 2 * 42128 + (ic - 16) * 1024
            if ic < 20:
                return 2 * O_WT + (ic - 18) * 1024
            return 2 * O_XT + (ic - 20) * 1024

        def f2_evac(s, kind, rb, i0, cnt, bk):
            off = rb * 128
            src4 = psb[bk][:, 0:cnt * 128].rearrange("p (c t) -> p c t", c=cnt)
            if kind == "g":
                gv = ARb[:, 2 * O_XA:2 * O_XA + 8 * IF].rearrange(
                    "p (c t) -> p c t", t=IF)
                return s.copy(out=gv[:, i0:i0 + 4, off:off + 128], in_=src4)
            if kind == "u":
                if i0 == 0:
                    uv = ARb[:, 2 * 13584:2 * 13584 + 3 * IF].rearrange(
                        "p (c t) -> p c t", t=IF)
                    s.copy(out=uv[:, 0:3, off:off + 128],
                           in_=psb[bk][:, 0:384].rearrange("p (c t) -> p c t", c=3))
                    return s.copy(
                        out=ARb[:, wup_col(3) + off:wup_col(3) + off + 128],
                        in_=psb[bk][:, 384:512])
                uv = ARb[:, 2 * O_SCR:2 * O_SCR + 5 * IF].rearrange(
                    "p (c t) -> p c t", t=IF)
                return s.copy(out=uv[:, 1:5, off:off + 128], in_=src4)
            if i0 < 16:
                dv = ARb[:, 2 * O_RW:2 * O_RW + 16 * 1024].rearrange(
                    "p (c t) -> p c t", t=1024)
                return s.copy(out=dv[:, i0:i0 + 4, off:off + 128], in_=src4)
            if i0 == 16:
                dv = ARb[:, 2 * 42128:2 * 42128 + 2048].rearrange(
                    "p (c t) -> p c t", t=1024)
                s.copy(out=dv[:, 0:2, off:off + 128],
                       in_=psb[bk][:, 0:256].rearrange("p (c t) -> p c t", c=2))
                dv2 = ARb[:, 2 * O_WT:2 * O_WT + 2048].rearrange(
                    "p (c t) -> p c t", t=1024)
                return s.copy(out=dv2[:, 0:2, off:off + 128],
                              in_=psb[bk][:, 256:512].rearrange(
                                  "p (c t) -> p c t", c=2))
            dv = ARb[:, 2 * O_XT:2 * O_XT + 2048].rearrange(
                "p (c t) -> p c t", t=1024)
            return s.copy(out=dv[:, 0:2, off:off + 128],
                          in_=psb[bk][:, 0:256].rearrange("p (c t) -> p c t", c=2))

        FS = (44384, 45792)
        f2chunks = []
        for rb in range(NIC):
            f2chunks.append(("g", rb))
            f2chunks.append(("u", rb))
        for rb in range(8):
            f2chunks.append(("d", rb))
        f2_src = {"g": wqgh, "u": wquh, "d": wqdh}
        f2_done = []
        mlp_guard_final = mlp_guard[0]
        dgroups = [(0, 4, 6), (4, 4, 7), (8, 4, 6), (12, 4, 7), (16, 4, 6),
                   (20, 2, 7)]
        for k, (kind, rb) in enumerate(f2chunks):
            nbf = IF if kind == "d" else 1024
            qf = qfa if k % 2 == 0 else qfb

            @block.sync
            def _(y, k=k, kind=kind, rb=rb, nbf=nbf, qf=qf):
                if k == 0:
                    if mlp_guard_final is not None:
                        wait(y, smm, mlp_guard_final)
                    wait(y, qwa, qwa.n)
                    wait(y, qwb, qwb.n)
                if k >= 2:
                    wait(y, ss, f2_done[k - 2])
                ins = y.dma_start(out=bf(FS[k % 2], nbf),
                                  in_=f2_src[kind][rb * 128:(rb + 1) * 128, :])
                qf.tick(ins, 16)
            ldw = (qf, qf.n)

            groups = dgroups if kind == "d" else [(0, 4, 6), (4, 4, 7)]
            for gi, (i0, cnt, bk) in enumerate(groups):
                @block.tensor
                def _(t, k=k, gi=gi, i0=i0, cnt=cnt, bk=bk, nbf=nbf, ldw=ldw):
                    if k == 0 and gi == 0:
                        wait(t, sv, xq_ready)
                    if gi == 0:
                        wait(t, *ldw)
                    pe_wait_free(t, bk)
                    for j in range(cnt):
                        ins = t.transpose(
                            psb[bk][:, j * 128:(j + 1) * 128],
                            bf(FS[k % 2], nbf)[:, (i0 + j) * 128:(i0 + j + 1) * 128],
                            ident)
                    smm.tick(ins)
                    bank_fill[bk] = (smm, smm.n)

                @block.scalar
                def _(s, k=k, gi=gi, kind=kind, rb=rb, i0=i0, cnt=cnt, bk=bk):
                    if k == 0 and gi == 0:
                        wait(s, sv, xq_ready)
                        wait(s, qs, x2_stored[NTT - 1])
                    use_wait_fill(s, bk)
                    ins = f2_evac(s, kind, rb, i0, cnt, bk)
                    ss.tick(ins)
                    bank_free[bk] = (ss, ss.n)
            f2_done.append(ss.n)
        wq_ss_ready = ss.n
        dump_arena("F")

        nc.leave_named_scope(*_scope_st.pop(), False)
        _scope_st.append(("phG", nc.enter_named_scope("phG", False)[0]))
        G_SIL, G_QU = 272, 1808
        H2S = (784, 49504)
        QAS = (1296, 50016)
        G_QN, G_QH, G_BM = 44384, 44896, 45408
        STS = (45536, 48224)
        G_HQ, G_X2 = 45792, 47200

        def xqv(dc, t0, w):
            return ARb[:, 2 * O_NRM + dc * T + t0: 2 * O_NRM + dc * T + t0 + w]

        y_stores = []
        g_out_done = []
        down_fill_prev = [0]
        sil_cons = [0]
        pend_evac = [None]
        pend_evac2 = []
        pend_tp = []
        st_guard = [0, 0]
        for tt in range(NTT):
            @block.sync
            def _(y, tt=tt):
                if tt == 0:
                    wait(y, qs, x2_stored[NTT - 1])
                if y_stores:
                    wait(y, qyo, y_stores[-1])
                ins = y.dma_start(out=f32(G_X2, D),
                                  in_=x2h[tt * 128:(tt + 1) * 128, :])
                qxa.tick(ins, 16)
            x2_ld = (qxa, qxa.n)

            for sl, (o0, w) in enumerate(IF_SLICES):
                gbk, ubk = (0, 2)[sl % 2], (1, 3)[sl % 2]
                tbk = (4, 7)[sl % 2]
                h2c = H2S[sl % 2]
                qac = QAS[sl % 2]
                stc = STS[sl % 2]

                for wnm, bk in (("gate", gbk), ("up", ubk)):
                    @block.tensor
                    def _(t, tt=tt, sl=sl, o0=o0, w=w, wnm=wnm, bk=bk):
                        if tt == 0 and sl == 0 and wnm == "gate":
                            wait(t, ss, wq_ss_ready)
                        pe_wait_free(t, bk)
                        for dc in range(NDC):
                            col = (wgate_col(dc) if wnm == "gate" else wup_col(dc)) + o0
                            ins = t.matmul(ps[bk][:, 0:w],
                                           xqv(dc, tt * 128, 128),
                                           ARb[:, col:col + w],
                                           start=(dc == 0), stop=(dc == NDC - 1))
                        smm.tick(ins)
                        bank_fill[bk] = (smm, smm.n)

                @block.tensor
                def _(t, sl=sl):
                    if pend_tp:
                        psl, pw, ptbk, pstc, phq = pend_tp.pop(0)
                        wait(t, sv, phq)
                        pe_wait_free(t, ptbk)
                        for j in range(pw // 128):
                            ins = t.transpose(psb[ptbk][:, j * 128:(j + 1) * 128],
                                              bf(pstc, pw)[:, j * 128:(j + 1) * 128],
                                              ident)
                        smm.tick(ins)
                        bank_fill[ptbk] = (smm, smm.n)
                        st_guard[psl % 2] = smm.n
                        assert pend_evac[0] is None
                        pend_evac[0] = (psl, pw, ptbk, smm.n)

                @block.scalar
                def _(s, w=w, gbk=gbk):
                    use_wait_fill(s, gbk)
                    if sil_cons[0]:
                        wait(s, sv, sil_cons[0])
                    ins = s.activation(out=f32(G_SIL, w), in_=ps[gbk][:, 0:w],
                                       func=AF.Silu)
                    ss.tick(ins)
                sil_made = ss.n
                bank_free[gbk] = (ss, sil_made)

                @block.vector
                def _(v, w=w, sil_made=sil_made, ubk=ubk, h2c=h2c):
                    wait(v, ss, sil_made)
                    use_wait_fill(v, ubk)
                    ins = v.tensor_tensor(out=f32(h2c, w), in0=ps[ubk][:, 0:w],
                                          in1=f32(G_SIL, w), op=ALU.mult)
                    sv.tick(ins)
                h2_made = sv.n
                sil_cons[0] = h2_made
                bank_free[ubk] = (sv, h2_made)

                @block.scalar
                def _(s, w=w, h2_made=h2_made, h2c=h2c, qac=qac):
                    wait(s, sv, h2_made)
                    ins = s.activation(out=f32(qac, w), in_=f32(h2c, w), func=AF.Abs)
                    ss.tick(ins)
                habs = ss.n

                @block.vector
                def _(v, w=w, habs=habs, h2c=h2c, qac=qac, stc=stc, tt=tt, sl=sl):
                    if pend_evac[0] is not None:
                        psl, pw, pbk, pfill = pend_evac[0]
                        wait(v, smm, pfill)
                        if down_fill_prev[0]:
                            wait(v, smm, down_fill_prev[0])
                        nt = pw // 128
                        ins = v.tensor_copy(
                            out=bf(G_HQ, IF)[:, psl * 512:psl * 512 + nt * 128],
                            in_=psb[pbk][:, 0:nt * 128])
                        sv.tick(ins)
                        bank_free[pbk] = (sv, sv.n)
                        pend_evac[0] = None
                    wait(v, ss, habs)
                    if st_guard[sl % 2]:
                        wait(v, smm, st_guard[sl % 2])
                    v.engine_nop(); v.engine_nop()
                    ins = emit_quant(v, h2c, qac, G_QU, G_QN, G_QH, G_BM, stc, w)
                    sv.tick(ins)
                hq_made = sv.n
                pend_tp.append((sl, w, tbk, stc, hq_made))

            @block.tensor
            def _(t, tt=tt):
                while pend_tp:
                    psl, pw, ptbk, pstc, phq = pend_tp.pop(0)
                    wait(t, sv, phq)
                    pe_wait_free(t, ptbk)
                    for j in range(pw // 128):
                        ins = t.transpose(psb[ptbk][:, j * 128:(j + 1) * 128],
                                          bf(pstc, pw)[:, j * 128:(j + 1) * 128],
                                          ident)
                    smm.tick(ins)
                    bank_fill[ptbk] = (smm, smm.n)
                    st_guard[psl % 2] = smm.n
                    pend_evac2.append((psl, pw, ptbk, smm.n))

            @block.vector
            def _(v, tt=tt):
                if pend_evac[0] is not None:
                    pend_evac2.insert(0, pend_evac[0])
                    pend_evac[0] = None
                while pend_evac2:
                    psl, pw, pbk, pfill = pend_evac2.pop(0)
                    wait(v, smm, pfill)
                    if down_fill_prev[0]:
                        wait(v, smm, down_fill_prev[0])
                    nt = pw // 128
                    ins = v.tensor_copy(
                        out=bf(G_HQ, IF)[:, psl * 512:psl * 512 + nt * 128],
                        in_=psb[pbk][:, 0:nt * 128])
                    sv.tick(ins)
                    bank_free[pbk] = (sv, sv.n)
            hq_fm_ready = sv.n

            @block.tensor
            def _(t, tt=tt, hq_fm_ready=hq_fm_ready):
                wait(t, sv, hq_fm_ready)
                for n in range(2):
                    bk = 5 + n
                    pe_wait_free(t, bk)
                    for ic in range(NIC):
                        ins = t.matmul(
                            ps[bk][:], bf(G_HQ, IF)[:, ic * 128:(ic + 1) * 128],
                            ARb[:, wdn_col(ic) + n * 512:wdn_col(ic) + n * 512 + 512],
                            start=(ic == 0), stop=(ic == NIC - 1))
                    smm.tick(ins)
                    bank_fill[bk] = (smm, smm.n)
                down_fill_prev[0] = smm.n

            @block.vector
            def _(v, tt=tt, x2_ld=x2_ld):
                wait(v, *x2_ld)
                for n in range(2):
                    bk = 5 + n
                    use_wait_fill(v, bk)
                    ins = v.tensor_tensor(out=f32(G_X2 + n * 512, 512),
                                          in0=ps[bk][:],
                                          in1=f32(G_X2 + n * 512, 512), op=ALU.add)
                    sv.tick(ins)
                    bank_free[bk] = (sv, sv.n)
            g_out_done.append(sv.n)

            @block.sync
            def _(y, tt=tt):
                wait(y, sv, g_out_done[-1])
                ins = y.dma_start(out=yh[tt * 128:(tt + 1) * 128, :],
                                  in_=f32(G_X2, D))
                qyo.tick(ins, 16)
            y_stores.append(qyo.n)
            if tt == 0:
                dump_arena("G0")

        nc.leave_named_scope(*_scope_st.pop(), False)

        @block.sync
        def _(y):
            wait(y, qyo, qyo.n)

    return nc


_NC_CACHE = {}


def kernel(L0_attn_x, L0_attn_Wk, L0_attn_Wv, L0_attn_Wr, L0_attn_Wo,
           L0_attn_time_decay, L0_attn_time_first, L0_attn_mixer_scale,
           L0_attn_Wfc, L0_attn_Wmlp, L0_attn_mlp_scale,
           L0_ffn_w_gate, L0_ffn_w_up, L0_ffn_w_down):
    x = np.ascontiguousarray(np.asarray(L0_attn_x, dtype=np.float32))
    B = x.shape[0]
    assert x.shape == (8, T, D)

    def t_(a):
        return np.ascontiguousarray(np.asarray(a, dtype=np.float32).T)

    shared = {
        "wkT": t_(L0_attn_Wk),
        "wvT": t_(L0_attn_Wv),
        "wrT": t_(L0_attn_Wr),
        "woT": t_(L0_attn_Wo),
        "wfcT": t_(L0_attn_Wfc),
        "wmlpT": t_(L0_attn_Wmlp),
        "wg": np.ascontiguousarray(np.asarray(L0_ffn_w_gate, dtype=np.float32)),
        "wu": np.ascontiguousarray(np.asarray(L0_ffn_w_up, dtype=np.float32)),
        "wd": np.ascontiguousarray(np.asarray(L0_ffn_w_down, dtype=np.float32)),
        "tdp": np.ascontiguousarray(
            np.asarray(L0_attn_time_decay, dtype=np.float32).reshape(8, 128).T),
        "tfp": np.ascontiguousarray(
            np.asarray(L0_attn_time_first, dtype=np.float32).reshape(8, 128).T),
        "msp": np.ascontiguousarray(np.broadcast_to(
            np.asarray(L0_attn_mixer_scale, dtype=np.float32)[None, :], (128, D))),
        "mlsp": np.ascontiguousarray(np.broadcast_to(
            np.asarray(L0_attn_mlp_scale, dtype=np.float32)[None, :], (128, D))),
        "identp": np.eye(128, dtype=ml_dtypes.bfloat16),
    }

    key = ("nc", _DBG_PHASE)
    if key not in _NC_CACHE:
        _NC_CACHE[key] = build_nc(_DBG_PHASE)
    nc = _NC_CACHE[key]

    in_maps = [dict(shared, x=np.ascontiguousarray(x[c])) for c in range(B)]
    res = run_bass_kernel_spmd(nc, in_maps, core_ids=list(range(B)))
    if _DBG_PHASE is not None:
        _NC_CACHE["dbg"] = [r.get("dbgarena") for r in res.results]
    out = np.stack([r["y"] for r in res.results], axis=0)
    return out.astype(np.float32)


# version 10
# speedup vs baseline: 1.4911x; 1.0313x over previous
import numpy as np
import ml_dtypes
from contextlib import ExitStack

import concourse.bass as bass
import concourse.mybir as mybir
from concourse.bass_utils import run_bass_kernel_spmd

F32 = mybir.dt.float32
BF16 = mybir.dt.bfloat16
AF = mybir.ActivationFunctionType
ALU = mybir.AluOpType
AXX = mybir.AxisListType.X

T = 2048
D = 1024
H = 4096
IF = 2816
NTT = 16
NDC = 8
NHC = 32
NIC = 22
IF_SLICES = [(0, 512), (512, 512), (1024, 512), (1536, 512), (2048, 512), (2560, 256)]

O_IDENT = 0
O_EW = 64
O_EU = 72
O_ST = 80
O_MS = 272
O_MLS = 1296
O_XA = 2320
O_NRM = 18704
O_RW = 26896
O_SCR = 35088
O_WT = 43360
O_WO = 44384
O_XT = 48480
ACOLS = 50688

(ST_SUM, ST_SSQ, ST_M, ST_MSQ, ST_VAR, ST_CN, ST_INV, ST_BN, ST_BI, ST_C2) = range(10)
EPS_LN = 1e-5
EPS_RMS = 1e-6


def _stc(slot):
    return O_ST + 16 * slot


class Sem:
    def __init__(self, h, name):
        self.h = h
        self.name = name
        self.n = 0

    def tick(self, ins, k=1):
        ins.then_inc(self.h, k)
        self.n += k
        return self.n


_DBG_PHASE = None


def build_nc(dbg_phase=None):
    nc = bass.Bass("TRN2", target_bir_lowering=False)
    dbgh = None
    if dbg_phase is not None:
        dbgh = nc.dram_tensor("dbgarena", [128, ACOLS], F32,
                              kind="ExternalOutput").ap()

    xh = nc.dram_tensor("x", [T, D], F32, kind="ExternalInput").ap()
    yh = nc.dram_tensor("y", [T, D], F32, kind="ExternalOutput").ap()
    x2h = nc.dram_tensor("x2s", [T, D], F32, kind="Internal").ap()
    wqgh = nc.dram_tensor("wqg_s", [IF, D], BF16, kind="Internal").ap()
    wquh = nc.dram_tensor("wqu_s", [IF, D], BF16, kind="Internal").ap()
    wqdh = nc.dram_tensor("wqd_s", [D, IF], BF16, kind="Internal").ap()
    wkh = nc.dram_tensor("wkT", [D, D], F32, kind="ExternalInput").ap()
    wvh = nc.dram_tensor("wvT", [D, D], F32, kind="ExternalInput").ap()
    wrh = nc.dram_tensor("wrT", [D, D], F32, kind="ExternalInput").ap()
    woh = nc.dram_tensor("woT", [D, D], F32, kind="ExternalInput").ap()
    wfch = nc.dram_tensor("wfcT", [D, H], F32, kind="ExternalInput").ap()
    wmlph = nc.dram_tensor("wmlpT", [H, D], F32, kind="ExternalInput").ap()
    wgh = nc.dram_tensor("wg", [IF, D], F32, kind="ExternalInput").ap()
    wuh = nc.dram_tensor("wu", [IF, D], F32, kind="ExternalInput").ap()
    wdh = nc.dram_tensor("wd", [D, IF], F32, kind="ExternalInput").ap()
    tdh = nc.dram_tensor("tdp", [128, 8], F32, kind="ExternalInput").ap()
    tfh = nc.dram_tensor("tfp", [128, 8], F32, kind="ExternalInput").ap()
    msh = nc.dram_tensor("msp", [128, D], F32, kind="ExternalInput").ap()
    mlsh = nc.dram_tensor("mlsp", [128, D], F32, kind="ExternalInput").ap()
    idh = nc.dram_tensor("identp", [128, 128], BF16, kind="ExternalInput").ap()

    with ExitStack() as es:
        AR = es.enter_context(nc.sbuf_tensor("arena", [128, ACOLS], F32))
        ps = [es.enter_context(nc.psum_tensor(f"psum{i}", [128, 512], F32))
              for i in range(8)]
        qy = Sem(es.enter_context(nc.semaphore("qy")), "qy")
        qs = Sem(es.enter_context(nc.semaphore("qs")), "qs")
        qg = Sem(es.enter_context(nc.semaphore("qg")), "qg")
        qga = Sem(es.enter_context(nc.semaphore("qga")), "qga")
        qgb = Sem(es.enter_context(nc.semaphore("qgb")), "qgb")
        qxa = Sem(es.enter_context(nc.semaphore("qxa")), "qxa")
        qxb = Sem(es.enter_context(nc.semaphore("qxb")), "qxb")
        qyo = Sem(es.enter_context(nc.semaphore("qyo")), "qyo")
        qga2 = Sem(es.enter_context(nc.semaphore("qga2")), "qga2")
        qgb2 = Sem(es.enter_context(nc.semaphore("qgb2")), "qgb2")
        qgc2 = Sem(es.enter_context(nc.semaphore("qgc2")), "qgc2")
        qgd2 = Sem(es.enter_context(nc.semaphore("qgd2")), "qgd2")
        qwa = Sem(es.enter_context(nc.semaphore("qwa")), "qwa")
        qwb = Sem(es.enter_context(nc.semaphore("qwb")), "qwb")
        qfa = Sem(es.enter_context(nc.semaphore("qfa")), "qfa")
        qfb = Sem(es.enter_context(nc.semaphore("qfb")), "qfb")
        smm = Sem(es.enter_context(nc.semaphore("smm")), "smm")
        sv = Sem(es.enter_context(nc.semaphore("sv")), "sv")
        ss = Sem(es.enter_context(nc.semaphore("ss")), "ss")
        sg = Sem(es.enter_context(nc.semaphore("sg")), "sg")
        block = es.enter_context(nc.Block())

        _scope_st = []
        ARb = AR[:].bitcast(BF16)

        def f32(c0, w):
            return AR[:, c0:c0 + w]

        def bf(c0f32, wbf):
            return ARb[:, 2 * c0f32:2 * c0f32 + wbf]

        ident = ARb[:, 0:128]
        psb = [p[:].bitcast(BF16) for p in ps]

        bank_free = [(None, 0)] * 8
        bank_fill = [(None, 0)] * 8

        def wait(e, sem, val):
            assert val <= sem.n, f"wait {sem.name}={val} > emitted {sem.n}"
            if val > 0:
                e.wait_ge(sem.h, val)

        def pe_wait_free(e, bk):
            s, v = bank_free[bk]
            if s is not None:
                wait(e, s, v)

        def use_wait_fill(e, bk):
            s, v = bank_fill[bk]
            assert s is not None, f"bank {bk} never filled"
            wait(e, s, v)

        def dump_arena(tag):
            if dbg_phase != tag:
                return
            @block.sync
            def _(y):
                wait(y, sv, sv.n)
                wait(y, ss, ss.n)
                wait(y, sg, sg.n)
                wait(y, smm, smm.n)
                wait(y, qg, qg.n)
                wait(y, qs, qs.n)
                wait(y, qga, qga.n)
                wait(y, qgb, qgb.n)
                wait(y, qxa, qxa.n)
                wait(y, qxb, qxb.n)
                wait(y, qyo, qyo.n)
                ins = y.dma_start(out=dbgh[:], in_=AR[:])
                qy.tick(ins, 16)
            barrier = qy.n
            @block.vector
            def _(v):
                wait(v, qy, barrier)
            @block.scalar
            def _(s):
                wait(s, qy, barrier)
            @block.gpsimd
            def _(g):
                wait(g, qy, barrier)
            @block.tensor
            def _(t):
                wait(t, qy, barrier)

        def nrmv(dc, t0, w):
            return ARb[:, 2 * O_NRM + dc * T + t0: 2 * O_NRM + dc * T + t0 + w]

        def rwv(dc, t0, w):
            return ARb[:, 2 * O_RW + dc * T + t0: 2 * O_RW + dc * T + t0 + w]

        def emit_quant(v, cz, cqa, cqu, cqn, cqh, cbm, cst, w):
            nb = w // 16
            zg = f32(cz, w).rearrange("p (g k) -> p g k", k=16)
            v.reduce_max(out=f32(cbm, nb), in_=zg, axis=AXX, apply_absolute_value=True)
            v.tensor_scalar(out=f32(cqn, w), in0=f32(cz, w), scalar1=0.0,
                            scalar2=None, op0=ALU.is_ge, op1=ALU.bypass)
            v.tensor_scalar(out=f32(cbm + nb, nb), in0=f32(cbm, nb), scalar1=1.0 / 6.0,
                            scalar2=1e-12, op0=ALU.mult, op1=ALU.max)
            v.tensor_scalar(out=f32(cqn, w), in0=f32(cqn, w), scalar1=2.0,
                            scalar2=-1.0, op0=ALU.mult, op1=ALU.add)
            v.reciprocal(out=f32(cbm + 2 * nb, nb), in_=f32(cbm + nb, nb))
            v.engine_nop()
            rcp_b = f32(cbm + 2 * nb, nb).unsqueeze(2).broadcast_to([128, nb, 16])
            scl_b = f32(cbm + nb, nb).unsqueeze(2).broadcast_to([128, nb, 16])
            qag = f32(cqa, w).rearrange("p (g k) -> p g k", k=16)
            qug = f32(cqu, w).rearrange("p (g k) -> p g k", k=16)
            v.tensor_tensor(out=qug, in0=qag, in1=rcp_b, op=ALU.mult)
            v.tensor_tensor(out=qag, in0=f32(cqn, w).rearrange("p (g k) -> p g k", k=16),
                            in1=scl_b, op=ALU.mult)
            u = f32(cqu, w)
            v.tensor_scalar(out=f32(cqn, w), in0=u, scalar1=0.25, scalar2=None,
                            op0=ALU.is_gt)
            v.tensor_scalar(out=f32(cqh, w), in0=u, scalar1=2.5, scalar2=None,
                            op0=ALU.is_gt)
            for tn, th in ((0.75, 3.5), (1.25, 5.0), (1.75, 5.0)):
                v.scalar_tensor_tensor(out=f32(cqn, w), in0=u, scalar=tn,
                                       in1=f32(cqn, w), op0=ALU.is_gt, op1=ALU.add)
                v.scalar_tensor_tensor(out=f32(cqh, w), in0=u, scalar=th,
                                       in1=f32(cqh, w), op0=ALU.is_gt, op1=ALU.add)
            v.engine_nop()
            v.scalar_tensor_tensor(out=f32(cqn, w), in0=f32(cqn, w), scalar=0.5,
                                   in1=f32(cqh, w), op0=ALU.mult, op1=ALU.add)
            v.engine_nop(); v.engine_nop()
            return v.tensor_tensor(
                out=bf(cst, w).rearrange("p (g k) -> p g k", k=16),
                in0=f32(cqn, w).rearrange("p (g k) -> p g k", k=16),
                in1=qag, op=ALU.mult)

        _scope_st.append(("phP0", nc.enter_named_scope("phP0", False)[0]))
        @block.sync
        def _(y):
            qy.tick(y.dma_start(out=ARb[:, 0:128], in_=idh[:]), 16)
            qy.tick(y.dma_start(out=f32(O_EW, 8), in_=tdh[:]), 16)
            qy.tick(y.dma_start(out=f32(O_EU, 8), in_=tfh[:]), 16)
            qy.tick(y.dma_start(out=f32(O_MS, D), in_=msh[:]), 16)
            qy.tick(y.dma_start(out=f32(O_MLS, D), in_=mlsh[:]), 16)
            for tt in range(NTT):
                ins = y.dma_start(out=f32(O_XA + tt * D, D),
                                  in_=xh[tt * 128:(tt + 1) * 128, :])
                qy.tick(ins, 16)
        n_pre = 16 * 5
        x_all = qy.n

        @block.scalar
        def _(s):
            wait(s, qy, x_all)
            s.activation(out=f32(O_EW, 8), in_=f32(O_EW, 8), func=AF.Exp)
            s.activation(out=f32(O_EW, 8), in_=f32(O_EW, 8), func=AF.Exp, scale=-1.0)
            ins = s.activation(out=f32(O_EU, 8), in_=f32(O_EU, 8), func=AF.Exp)
            ss.tick(ins)
        ew_ready = ss.n

        nc.leave_named_scope(*_scope_st.pop(), False)
        _scope_st.append(("phA", nc.enter_named_scope("phA", False)[0]))
        @block.vector
        def _(v):
            wait(v, qy, x_all)
            for tt in range(NTT):
                ins = v.tensor_scalar(
                    out=f32(35600, D), in0=f32(O_XA + tt * D, D), scalar1=1.0,
                    scalar2=None, op0=ALU.mult, op1=ALU.add,
                    accum_out=AR[:, _stc(ST_SUM) + tt:_stc(ST_SUM) + tt + 1])
                sv.tick(ins)

        @block.scalar
        def _(s):
            wait(s, qy, x_all)
            for tt in range(NTT):
                ins = s.activation(
                    out=f32(36624, D), in_=f32(O_XA + tt * D, D), func=AF.Square,
                    accum_out=AR[:, _stc(ST_SSQ) + tt:_stc(ST_SSQ) + tt + 1])
                ss.tick(ins)
        a_ssq = ss.n

        @block.vector
        def _(v):
            wait(v, ss, a_ssq)
            v.tensor_scalar(out=f32(_stc(ST_M), 16), in0=f32(_stc(ST_SUM), 16),
                            scalar1=1.0 / D, scalar2=None, op0=ALU.mult,
                            op1=ALU.bypass)
            v.engine_nop(); v.engine_nop(); v.engine_nop()
            v.tensor_tensor(out=f32(_stc(ST_MSQ), 16), in0=f32(_stc(ST_M), 16),
                            in1=f32(_stc(ST_M), 16), op=ALU.mult)
            v.engine_nop(); v.engine_nop(); v.engine_nop()
            ins = v.scalar_tensor_tensor(
                out=f32(_stc(ST_VAR), 16), in0=f32(_stc(ST_SSQ), 16), scalar=1.0 / D,
                in1=f32(_stc(ST_MSQ), 16), op0=ALU.mult, op1=ALU.subtract)
            sv.tick(ins)
        a_var = sv.n

        @block.vector
        def _(v):
            v.engine_nop(); v.engine_nop(); v.engine_nop()
            v.tensor_scalar(out=f32(_stc(ST_CN), 16), in0=f32(_stc(ST_VAR), 16),
                            scalar1=1.0 + EPS_RMS, scalar2=EPS_LN * EPS_RMS,
                            op0=ALU.mult, op1=ALU.add)
            ins = v.tensor_scalar(out=f32(_stc(ST_INV), 16), in0=f32(_stc(ST_VAR), 16),
                                  scalar1=EPS_LN, scalar2=None, op0=ALU.add,
                                  op1=ALU.bypass)
            sv.tick(ins)
        a_args = sv.n

        @block.scalar
        def _(s):
            wait(s, sv, a_args)
            s.activation(out=f32(_stc(ST_CN), 16), in_=f32(_stc(ST_CN), 16),
                         func=AF.Ln)
            s.activation(out=f32(_stc(ST_CN), 16), in_=f32(_stc(ST_CN), 16),
                         func=AF.Exp, scale=-0.5)
            s.activation(out=f32(_stc(ST_INV), 16), in_=f32(_stc(ST_INV), 16),
                         func=AF.Ln)
            ins = s.activation(out=f32(_stc(ST_INV), 16), in_=f32(_stc(ST_INV), 16),
                               func=AF.Exp, scale=-0.5)
            ss.tick(ins)
        a_invcn = ss.n

        @block.vector
        def _(v):
            wait(v, ss, a_invcn)
            v.scalar_tensor_tensor(out=f32(_stc(ST_BN), 16), in0=f32(_stc(ST_M), 16),
                                   scalar=-1.0, in1=f32(_stc(ST_CN), 16),
                                   op0=ALU.mult, op1=ALU.mult)
            ins = v.scalar_tensor_tensor(out=f32(_stc(ST_BI), 16),
                                         in0=f32(_stc(ST_M), 16), scalar=-1.0,
                                         in1=f32(_stc(ST_INV), 16),
                                         op0=ALU.mult, op1=ALU.mult)
            sv.tick(ins)
        a_smalls = sv.n

        N1STG = 35088
        for tt in range(NTT):
            @block.scalar
            def _(s, tt=tt):
                wait(s, sv, a_smalls)
                if bank_fill[7][0] is not None:
                    wait(s, smm, bank_fill[7][1])
                ins = s.activation(
                    out=bf(N1STG, D), in_=f32(O_XA + tt * D, D), func=AF.Identity,
                    bias=AR[:, _stc(ST_BN) + tt:_stc(ST_BN) + tt + 1],
                    scale=AR[:, _stc(ST_CN) + tt:_stc(ST_CN) + tt + 1])
                ss.tick(ins)
            n1_made = ss.n

            @block.tensor
            def _(t, tt=tt, n1_made=n1_made):
                wait(t, ss, n1_made)
                for half, bk in ((0, 6), (1, 7)):
                    pe_wait_free(t, bk)
                    for j in range(4):
                        ins = t.transpose(
                            psb[bk][:, j * 128:(j + 1) * 128],
                            bf(N1STG, D)[:, (half * 4 + j) * 128:(half * 4 + j + 1) * 128],
                            ident)
                    smm.tick(ins)
                    bank_fill[bk] = (smm, smm.n)

            @block.vector
            def _(v, tt=tt):
                nview = ARb[:, 2 * O_NRM:2 * O_NRM + 8 * T].rearrange(
                    "p (c t) -> p c t", c=8)
                for half, bk in ((0, 6), (1, 7)):
                    use_wait_fill(v, bk)
                    ins = v.tensor_copy(
                        out=nview[:, half * 4:half * 4 + 4, tt * 128:(tt + 1) * 128],
                        in_=psb[bk][:, 0:512].rearrange("p (c t) -> p c t", c=4))
                    sv.tick(ins)
                    bank_free[bk] = (sv, sv.n)
                ins = v.tensor_scalar(
                    out=f32(O_XA + tt * D, D), in0=f32(O_XA + tt * D, D),
                    scalar1=AR[:, _stc(ST_INV) + tt:_stc(ST_INV) + tt + 1],
                    scalar2=AR[:, _stc(ST_BI) + tt:_stc(ST_BI) + tt + 1],
                    op0=ALU.mult, op1=ALU.add)
                sv.tick(ins)
        normed1_ready = sv.n
        dump_arena("A")

        nc.leave_named_scope(*_scope_st.pop(), False)
        _scope_st.append(("phB", nc.enter_named_scope("phB", False)[0]))
        C_KV, C_S, C_R, C_KSB = 35088, 37152, 39232, 41280
        wt_slot = [O_WT, O_WT + 512]
        wload_count = []
        wload_used = {}
        load_idx = [0]
        mlp_guard = [None]

        def wq_load(g, src_ap, nbf=1024):
            li = load_idx[0]
            load_idx[0] += 1
            if li >= 2:
                s_, v_ = wload_used[li - 2]
                wait(g, s_, v_)
            if mlp_guard[0] is not None:
                wait(g, smm, mlp_guard[0])
                mlp_guard[0] = None
            dst = bf(wt_slot[li % 2], nbf)
            if src_ap.ndim == 3:
                dst = dst.rearrange("p (c m) -> p c m", c=8)
            qsem = qga if li % 2 == 0 else qgb
            ins = g.dma_start(out=dst, in_=src_ap)
            qsem.tick(ins, 16)
            wload_count.append((qsem, qsem.n))
            return li

        def colslice8(wh, o):
            return wh[:, o * 128:(o + 1) * 128].rearrange("(c p) m -> p c m", p=128)

        @block.vector
        def _(v):
            ins = v.memset(f32(C_S, 1), 0.0)
            sv.tick(ins)
        szero = sv.n

        for o in range(NDC):
            lis = {}

            @block.gpsimd
            def _(g, o=o):
                lis["k"] = wq_load(g, colslice8(wkh, o))
                lis["v"] = wq_load(g, colslice8(wvh, o))

            @block.tensor
            def _(t, o=o):
                if o == 0:
                    wait(t, sv, normed1_ready)
                for nm, b0 in (("k", 0), ("v", 4)):
                    li = lis[nm]
                    wait(t, *wload_count[li])
                    for bk in range(b0, b0 + 4):
                        pe_wait_free(t, bk)
                    wslot = bf(wt_slot[li % 2], 1024)
                    for t4 in range(4):
                        for dc in range(NDC):
                            ins = t.matmul(ps[b0 + t4][:],
                                           wslot[:, dc * 128:(dc + 1) * 128],
                                           nrmv(dc, t4 * 512, 512),
                                           start=(dc == 0), stop=(dc == NDC - 1))
                    smm.tick(ins)
                    for bk in range(b0, b0 + 4):
                        bank_fill[bk] = (smm, smm.n)
                    wload_used[li] = (smm, smm.n)

            @block.gpsimd
            def _(g, o=o):
                lis["r"] = wq_load(g, colslice8(wrh, o))

            @block.scalar
            def _(s, o=o):
                use_wait_fill(s, 0)
                for t4 in range(4):
                    ins = s.copy(out=f32(C_KSB + t4 * 512, 512), in_=ps[t4][:])
                ss.tick(ins)
            k_cp = ss.n
            for bk in range(4):
                bank_free[bk] = (ss, k_cp)

            @block.vector
            def _(v, o=o, k_cp=k_cp):
                wait(v, ss, k_cp)
                use_wait_fill(v, 4)
                for t4 in range(4):
                    ins = v.tensor_tensor(out=f32(C_KV + t4 * 512, 512),
                                          in0=f32(C_KSB + t4 * 512, 512),
                                          in1=ps[4 + t4][:], op=ALU.mult)
                sv.tick(ins)
            kv_made = sv.n
            for bk in range(4, 8):
                bank_free[bk] = (sv, kv_made)

            @block.tensor
            def _(t, o=o):
                li = lis["r"]
                wait(t, *wload_count[li])
                for bk in range(4):
                    pe_wait_free(t, bk)
                wslot = bf(wt_slot[li % 2], 1024)
                for t4 in range(4):
                    for dc in range(NDC):
                        ins = t.matmul(ps[t4][:], wslot[:, dc * 128:(dc + 1) * 128],
                                       nrmv(dc, t4 * 512, 512),
                                       start=(dc == 0), stop=(dc == NDC - 1))
                smm.tick(ins)
                for bk in range(4):
                    bank_fill[bk] = (smm, smm.n)
                wload_used[li] = (smm, smm.n)

            @block.scalar
            def _(s, o=o):
                use_wait_fill(s, 0)
                for t4 in range(4):
                    ins = s.activation(out=f32(C_R + t4 * 512, 512), in_=ps[t4][:],
                                       func=AF.Sigmoid)
                ss.tick(ins)
            r_made = ss.n
            for bk in range(4):
                bank_free[bk] = (ss, r_made)

            @block.vector
            def _(v, o=o, r_made=r_made):
                if o == 0:
                    wait(v, sv, szero)
                    wait(v, ss, ew_ready)
                ewb = AR[:, O_EW + o:O_EW + o + 1].broadcast_to([128, T])
                v.engine_nop(); v.engine_nop(); v.engine_nop()
                v.tensor_tensor_scan(out=f32(C_S + 1, T), data0=ewb,
                                     data1=f32(C_KV, T), initial=0.0,
                                     op0=ALU.mult, op1=ALU.add)
                v.engine_nop(); v.engine_nop(); v.engine_nop()
                v.scalar_tensor_tensor(out=f32(C_KV, T), in0=f32(C_KV, T),
                                       scalar=AR[:, O_EU + o:O_EU + o + 1],
                                       in1=f32(C_S, T), op0=ALU.mult, op1=ALU.add)
                wait(v, ss, r_made)
                v.engine_nop(); v.engine_nop(); v.engine_nop()
                ins = v.tensor_tensor(out=rwv(o, 0, T), in0=f32(C_KV, T),
                                      in1=f32(C_R, T), op=ALU.mult)
                sv.tick(ins)
        rw_ready = sv.n
        dump_arena("B")

        nc.leave_named_scope(*_scope_st.pop(), False)
        _scope_st.append(("phC", nc.enter_named_scope("phC", False)[0]))
        FP_Z = (48480, 48992, 49504)
        FP_QA = (39184, 39696, 40208)
        FP_QU, FP_QN, FP_QH, FP_BM = 40720, 41232, 41744, 42256
        FP_ST = (42352, 42608)
        fpjobs = []
        for rb in range(NIC):
            fpjobs.append(("g", rb, 0, 512))
            fpjobs.append(("g", rb, 512, 512))
        for rb in range(NIC):
            fpjobs.append(("u", rb, 0, 512))
            fpjobs.append(("u", rb, 512, 512))
        for rb in range(8):
            for c0 in range(0, IF, 512):
                fpjobs.append(("d", rb, c0, min(512, IF - c0)))
        fp_q_done = []
        fp_state = [0]

        def fp_src(kind, rb, c0, w):
            wsrc = {"g": wgh, "u": wuh, "d": wdh}[kind]
            return wsrc[rb * 128:(rb + 1) * 128, c0:c0 + w]

        def fp_dst(kind, rb, c0, w):
            wdst = {"g": wqgh, "u": wquh, "d": wqdh}[kind]
            return wdst[rb * 128:(rb + 1) * 128, c0:c0 + w]

        def fp_emit_wb(y, j):
            wait(y, sv, fp_q_done[j])
            ins = y.dma_start(out=fp_dst(*fpjobs[j]),
                              in_=bf(FP_ST[j % 2], fpjobs[j][3]))
            (qwa if j % 2 == 0 else qwb).tick(ins, 16)

        def emit_fp_job():
            j = fp_state[0]
            if j >= len(fpjobs):
                return
            fp_state[0] += 1
            kind, rb, c0, w = fpjobs[j]
            qf = qfa if j % 2 == 0 else qfb

            @block.sync
            def _(y, j=j, kind=kind, rb=rb, c0=c0, w=w, qf=qf):
                if j == 0:
                    wait(y, sv, rw_ready)
                if j >= 3:
                    wait(y, sv, fp_q_done[j - 3])
                ins = y.dma_start(out=f32(FP_Z[j % 3], w),
                                  in_=fp_src(kind, rb, c0, w))
                qf.tick(ins, 16)
                if j >= 1:
                    fp_emit_wb(y, j - 1)

            @block.scalar
            def _(s, j=j, w=w, qf=qf):
                if j == 0:
                    wait(s, sv, rw_ready)
                if j >= 3:
                    wait(s, sv, fp_q_done[j - 3])
                wait(s, qf, qf.n)
                ins = s.activation(out=f32(FP_QA[j % 3], w),
                                   in_=f32(FP_Z[j % 3], w), func=AF.Abs)
                ss.tick(ins)
            fp_abs = ss.n

            @block.vector
            def _(v, j=j, w=w, fp_abs=fp_abs):
                wait(v, ss, fp_abs)
                if j >= 2:
                    qwp = qwa if j % 2 == 0 else qwb
                    wait(v, qwp, qwp.n)
                ins = emit_quant(v, FP_Z[j % 3], FP_QA[j % 3], FP_QU, FP_QN,
                                 FP_QH, FP_BM, FP_ST[j % 2], w)
                sv.tick(ins)
            fp_q_done.append(sv.n)

        def emit_fp_flush():
            @block.sync
            def _(y):
                fp_emit_wb(y, len(fpjobs) - 1)

        @block.gpsimd
        def _(g):
            for dc in range(NDC):
                ins = g.dma_start(out=bf(O_WO + dc * 512, 1024),
                                  in_=woh[dc * 128:(dc + 1) * 128, :])
                qg.tick(ins, 16)
        wo_loaded = qg.n

        C_TMP = 35088
        g_adds = []
        for tt in range(NTT):
            for n in range(2):
                bk = (tt * 2 + n) % 4

                @block.tensor
                def _(t, tt=tt, n=n, bk=bk):
                    if tt == 0 and n == 0:
                        wait(t, qg, wo_loaded)
                        wait(t, sv, rw_ready)
                    pe_wait_free(t, bk)
                    for dc in range(NDC):
                        ins = t.matmul(
                            ps[bk][:], rwv(dc, tt * 128, 128),
                            bf(O_WO + dc * 512, 1024)[:, n * 512:(n + 1) * 512],
                            start=(dc == 0), stop=(dc == NDC - 1))
                    smm.tick(ins)
                    bank_fill[bk] = (smm, smm.n)

                @block.vector
                def _(v, tt=tt, n=n, bk=bk):
                    use_wait_fill(v, bk)
                    if len(g_adds) >= 2:
                        wait(v, sg, g_adds[-2])
                    ins = v.tensor_tensor(out=f32(C_TMP + (n % 2) * 512, 512),
                                          in0=ps[bk][:],
                                          in1=f32(O_MS + n * 512, 512), op=ALU.mult)
                    sv.tick(ins)
                    bank_free[bk] = (sv, sv.n)
                tmp_made = sv.n

                @block.gpsimd
                def _(g, tt=tt, n=n, tmp_made=tmp_made):
                    wait(g, sv, tmp_made)
                    ins = g.tensor_tensor(out=f32(O_XA + tt * D + n * 512, 512),
                                          in0=f32(O_XA + tt * D + n * 512, 512),
                                          in1=f32(C_TMP + (n % 2) * 512, 512),
                                          op=ALU.add)
                    sg.tick(ins)
                g_adds.append(sg.n)
                emit_fp_job()
        xa_after_c = sg.n
        dump_arena("C")

        nc.leave_named_scope(*_scope_st.pop(), False)
        _scope_st.append(("phD", nc.enter_named_scope("phD", False)[0]))
        C_HSB, C_ESB = 35088, 37136
        mlp_idx = [0]
        mlp_used = {}
        mlp_count = {}
        mlp_sems = [qga2, qgb2, qgc2, qgd2]

        def g_col(hc):
            if hc < 16:
                return 2 * O_RW + hc * 1024
            if hc < 24:
                return 2 * 22800 + (hc - 16) * 1024
            return 2 * O_WO + (hc - 24) * 1024

        hsb_used = [0, 0]
        for half in range(2):
            tt0 = half * 8

            @block.scalar
            def _(s, half=half, tt0=tt0):
                wait(s, sg, sg.n)
                for i in range(8):
                    ins = s.activation(
                        out=f32(36624, D), in_=f32(O_XA + (tt0 + i) * D, D),
                        func=AF.Square,
                        accum_out=AR[:, _stc(ST_SSQ) + i:_stc(ST_SSQ) + i + 1])
                ss.tick(ins)
            d_ssq = ss.n

            @block.vector
            def _(v, d_ssq=d_ssq):
                wait(v, ss, d_ssq)
                ins = v.tensor_scalar(out=f32(_stc(ST_C2), 8), in0=f32(_stc(ST_SSQ), 8),
                                      scalar1=1.0 / D, scalar2=EPS_RMS,
                                      op0=ALU.mult, op1=ALU.add)
                sv.tick(ins)
            d_arg = sv.n

            @block.scalar
            def _(s, d_arg=d_arg):
                wait(s, sv, d_arg)
                s.activation(out=f32(_stc(ST_C2), 8), in_=f32(_stc(ST_C2), 8),
                             func=AF.Ln)
                ins = s.activation(out=f32(_stc(ST_C2), 8), in_=f32(_stc(ST_C2), 8),
                                   func=AF.Exp, scale=-0.5)
                ss.tick(ins)
            c2_done = ss.n

            for i in range(8):
                @block.scalar
                def _(s, i=i, tt0=tt0, c2_done=c2_done):
                    wait(s, ss, c2_done)
                    if bank_fill[7][0] is not None:
                        wait(s, smm, bank_fill[7][1])
                    ins = s.activation(
                        out=bf(N1STG, D), in_=f32(O_XA + (tt0 + i) * D, D),
                        func=AF.Identity,
                        scale=AR[:, _stc(ST_C2) + i:_stc(ST_C2) + i + 1])
                    ss.tick(ins)
                n2_made = ss.n

                @block.tensor
                def _(t, i=i, n2_made=n2_made):
                    wait(t, ss, n2_made)
                    for hf, bk in ((0, 6), (1, 7)):
                        pe_wait_free(t, bk)
                        for j in range(4):
                            ins = t.transpose(
                                psb[bk][:, j * 128:(j + 1) * 128],
                                bf(N1STG, D)[:, (hf * 4 + j) * 128:(hf * 4 + j + 1) * 128],
                                ident)
                        smm.tick(ins)
                        bank_fill[bk] = (smm, smm.n)

                @block.vector
                def _(v, i=i):
                    n2view = ARb[:, 2 * O_NRM:2 * O_NRM + 8 * 1024].rearrange(
                        "p (c t) -> p c t", c=8)
                    for hf, bk in ((0, 6), (1, 7)):
                        use_wait_fill(v, bk)
                        ins = v.tensor_copy(
                            out=n2view[:, hf * 4:hf * 4 + 4, i * 128:(i + 1) * 128],
                            in_=psb[bk][:, 0:512].rearrange("p (c t) -> p c t", c=4))
                        sv.tick(ins)
                        bank_free[bk] = (sv, sv.n)
                emit_fp_job()
            n2_ready = sv.n

            def n2v(dc, t0, w):
                return ARb[:, 2 * O_NRM + dc * 1024 + t0: 2 * O_NRM + dc * 1024 + t0 + w]

            for hc in range(NHC):
                lis_fc_holder = [None]

                @block.gpsimd
                def _(g, hc=hc, lis_fc_holder=lis_fc_holder):
                    lis_fc_holder[0] = wq_load(g, wfch[:, hc * 128:(hc + 1) * 128]
                                               .rearrange("(c p) m -> p c m", p=128))

                for t4 in range(2):
                    bk = t4

                    @block.tensor
                    def _(t, hc=hc, t4=t4, bk=bk, n2_ready=n2_ready,
                          lis_fc_holder=lis_fc_holder):
                        li = lis_fc_holder[0]
                        if hc == 0 and t4 == 0:
                            wait(t, sv, n2_ready)
                        wait(t, *wload_count[li])
                        pe_wait_free(t, bk)
                        wslot = bf(wt_slot[li % 2], 1024)
                        for dc in range(NDC):
                            ins = t.matmul(ps[bk][:], wslot[:, dc * 128:(dc + 1) * 128],
                                           n2v(dc, t4 * 512, 512),
                                           start=(dc == 0), stop=(dc == NDC - 1))
                        smm.tick(ins)
                        bank_fill[bk] = (smm, smm.n)
                        if t4 == 1:
                            wload_used[li] = (smm, smm.n)

                    @block.scalar
                    def _(s, t4=t4, bk=bk):
                        use_wait_fill(s, bk)
                        wait(s, sv, hsb_used[t4])
                        ins = s.copy(out=f32(C_HSB + t4 * 512, 512), in_=ps[bk][:])
                        ss.tick(ins)
                    h_cp = ss.n
                    bank_free[bk] = (ss, h_cp)

                    @block.vector
                    def _(v, hc=hc, t4=t4, h_cp=h_cp):
                        wait(v, ss, h_cp)
                        v.scalar_tensor_tensor(out=f32(C_ESB + t4 * 512, 512),
                                               in0=f32(C_HSB + t4 * 512, 512),
                                               scalar=0.25,
                                               in1=f32(C_HSB + t4 * 512, 512),
                                               op0=ALU.mult, op1=ALU.max)
                        v.engine_nop(); v.engine_nop(); v.engine_nop()
                        ins = v.tensor_tensor(
                            out=ARb[:, g_col(hc) + t4 * 512:g_col(hc) + t4 * 512 + 512],
                            in0=f32(C_HSB + t4 * 512, 512),
                            in1=f32(C_ESB + t4 * 512, 512), op=ALU.mult)
                        sv.tick(ins)
                    hsb_used[t4] = sv.n
                emit_fp_job()
            g_ready = sv.n

            for n in range(2):
                for hc in range(NHC):
                    li_holder = [None]

                    @block.gpsimd
                    def _(g, hc=hc, n=n, li_holder=li_holder):
                        li = mlp_idx[0]
                        mlp_idx[0] += 1
                        if li >= 4:
                            s_, v_ = mlp_used[li - 4]
                            wait(g, s_, v_)
                        if li % 32 < 4 and load_idx[0] >= 1 and                                 (load_idx[0] - 1) in wload_used:
                            wait(g, *wload_used[load_idx[0] - 1])
                        qsem = mlp_sems[li % 4]
                        ins = g.dma_start(
                            out=bf(O_WT + (li % 4) * 256, 512),
                            in_=wmlph[hc * 128:(hc + 1) * 128, n * 512:(n + 1) * 512])
                        qsem.tick(ins, 16)
                        mlp_count[li] = (qsem, qsem.n)
                        li_holder[0] = li

                    @block.tensor
                    def _(t, hc=hc, n=n, li_holder=li_holder, g_ready=g_ready):
                        li = li_holder[0]
                        if hc == 0:
                            wait(t, sv, g_ready)
                            for bk in range(8):
                                pe_wait_free(t, bk)
                        wait(t, *mlp_count[li])
                        wslot = bf(O_WT + (li % 4) * 256, 512)
                        for i in range(8):
                            ins = t.matmul(
                                ps[i][:],
                                ARb[:, g_col(hc) + i * 128:g_col(hc) + i * 128 + 128],
                                wslot,
                                start=(hc == 0), stop=(hc == NHC - 1))
                        smm.tick(ins)
                        mlp_used[li] = (smm, smm.n)
                        if hc == NHC - 1:
                            for i in range(8):
                                bank_fill[i] = (smm, smm.n)
                            mlp_guard[0] = smm.n
                    if hc % 4 == 0:
                        emit_fp_job()

                for i in range(8):
                    @block.vector
                    def _(v, i=i, n=n):
                        use_wait_fill(v, i)
                        if len(g_adds) >= 2:
                            wait(v, sg, g_adds[-2])
                        ins = v.tensor_tensor(out=f32(C_TMP + (i % 2) * 512, 512),
                                              in0=ps[i][:],
                                              in1=f32(O_MLS + n * 512, 512),
                                              op=ALU.mult)
                        sv.tick(ins)
                        bank_free[i] = (sv, sv.n)
                    tmp_made = sv.n

                    @block.gpsimd
                    def _(g, i=i, n=n, tt0=tt0, tmp_made=tmp_made):
                        wait(g, sv, tmp_made)
                        ins = g.tensor_tensor(
                            out=f32(O_XA + (tt0 + i) * D + n * 512, 512),
                            in0=f32(O_XA + (tt0 + i) * D + n * 512, 512),
                            in1=f32(C_TMP + (i % 2) * 512, 512), op=ALU.add)
                        sg.tick(ins)
                    g_adds.append(sg.n)
        while fp_state[0] < len(fpjobs):
            emit_fp_job()
        emit_fp_flush()
        fp_last_q = fp_q_done[-1]
        xa_after_d = sg.n
        dump_arena("D")

        nc.leave_named_scope(*_scope_st.pop(), False)
        _scope_st.append(("phE", nc.enter_named_scope("phE", False)[0]))
        C_Z = (35088, 36112)
        C_QA = (37136, 38160)
        C_QU, C_QN, C_QH, C_BM, C_QST = 39184, 40208, 41232, 42256, 42448
        xt_slot = [O_XT, O_XT + 1024]
        x2_stored = [0] * NTT
        e_adds = []
        for tt in range(NTT):
            @block.sync
            def _(y, tt=tt):
                if tt == 0:
                    wait(y, sv, fp_last_q)
                if len(e_adds) >= 2:
                    wait(y, sv, e_adds[-2])
                qsem = qxa if tt % 2 == 0 else qxb
                ins = y.dma_start(out=f32(xt_slot[tt % 2], D),
                                  in_=xh[tt * 128:(tt + 1) * 128, :])
                qsem.tick(ins, 16)
            xt_ld = (qxa if tt % 2 == 0 else qxb, (qxa if tt % 2 == 0 else qxb).n)

            @block.vector
            def _(v, tt=tt, xt_ld=xt_ld):
                if tt == 0:
                    wait(v, sg, xa_after_d)
                wait(v, *xt_ld)
                v.tensor_tensor(out=f32(O_XA + tt * D, D), in0=f32(O_XA + tt * D, D),
                                in1=f32(xt_slot[tt % 2], D), op=ALU.add)
                v.engine_nop(); v.engine_nop(); v.engine_nop()
                ins = v.tensor_scalar(
                    out=f32(C_QA[0], D), in0=f32(O_XA + tt * D, D), scalar1=1.0,
                    scalar2=None, op0=ALU.mult, op1=ALU.add,
                    accum_out=AR[:, _stc(ST_SUM) + tt:_stc(ST_SUM) + tt + 1])
                sv.tick(ins)
            e_adds.append(sv.n)
            x2_done = sv.n

            @block.sync
            def _(y, tt=tt, x2_done=x2_done):
                wait(y, sv, x2_done)
                ins = y.dma_start(out=x2h[tt * 128:(tt + 1) * 128, :],
                                  in_=f32(O_XA + tt * D, D))
                qs.tick(ins, 16)
            x2_stored[tt] = qs.n

            @block.scalar
            def _(s, tt=tt, x2_done=x2_done):
                wait(s, sv, x2_done)
                ins = s.activation(
                    out=f32(36624, D), in_=f32(O_XA + tt * D, D), func=AF.Square,
                    accum_out=AR[:, _stc(ST_SSQ) + tt:_stc(ST_SSQ) + tt + 1])
                ss.tick(ins)
        e_ssq = ss.n

        @block.vector
        def _(v):
            wait(v, ss, e_ssq)
            v.tensor_scalar(out=f32(_stc(ST_M), 16), in0=f32(_stc(ST_SUM), 16),
                            scalar1=1.0 / D, scalar2=None, op0=ALU.mult,
                            op1=ALU.bypass)
            v.engine_nop(); v.engine_nop(); v.engine_nop()
            v.tensor_tensor(out=f32(_stc(ST_MSQ), 16), in0=f32(_stc(ST_M), 16),
                            in1=f32(_stc(ST_M), 16), op=ALU.mult)
            v.engine_nop(); v.engine_nop(); v.engine_nop()
            ins = v.scalar_tensor_tensor(
                out=f32(_stc(ST_VAR), 16), in0=f32(_stc(ST_SSQ), 16), scalar=1.0 / D,
                in1=f32(_stc(ST_MSQ), 16), op0=ALU.mult, op1=ALU.subtract)
            sv.tick(ins)
        e_var = sv.n

        @block.vector
        def _(v):
            v.engine_nop(); v.engine_nop(); v.engine_nop()
            ins = v.tensor_scalar(out=f32(_stc(ST_INV), 16), in0=f32(_stc(ST_VAR), 16),
                                  scalar1=EPS_LN, scalar2=None, op0=ALU.add,
                                  op1=ALU.bypass)
            sv.tick(ins)
        e_arg = sv.n

        @block.scalar
        def _(s):
            wait(s, sv, e_arg)
            s.activation(out=f32(_stc(ST_INV), 16), in_=f32(_stc(ST_INV), 16),
                         func=AF.Ln)
            ins = s.activation(out=f32(_stc(ST_INV), 16), in_=f32(_stc(ST_INV), 16),
                               func=AF.Exp, scale=-0.5)
            ss.tick(ins)
        e_inv = ss.n

        @block.vector
        def _(v):
            wait(v, ss, e_inv)
            ins = v.scalar_tensor_tensor(out=f32(_stc(ST_BI), 16),
                                         in0=f32(_stc(ST_M), 16), scalar=-1.0,
                                         in1=f32(_stc(ST_INV), 16),
                                         op0=ALU.mult, op1=ALU.mult)
            sv.tick(ins)
        e_smalls = sv.n

        e_q_done = []
        for tt in range(NTT):
            @block.scalar
            def _(s, tt=tt):
                if tt == 0:
                    wait(s, sv, e_smalls)
                if tt >= 2:
                    wait(s, sv, e_q_done[tt - 2])
                s.activation(out=f32(C_Z[tt % 2], D), in_=f32(O_XA + tt * D, D),
                             func=AF.Identity,
                             bias=AR[:, _stc(ST_BI) + tt:_stc(ST_BI) + tt + 1],
                             scale=AR[:, _stc(ST_INV) + tt:_stc(ST_INV) + tt + 1])
                ins = s.activation(out=f32(C_QA[tt % 2], D), in_=f32(C_Z[tt % 2], D),
                                   func=AF.Abs)
                ss.tick(ins)
            z_made = ss.n

            @block.vector
            def _(v, tt=tt, z_made=z_made):
                if tt == 0:
                    wait(v, qwa, qwa.n)
                    wait(v, qwb, qwb.n)
                wait(v, ss, z_made)
                if bank_fill[7][0] is not None and tt > 0:
                    wait(v, smm, bank_fill[7][1])
                ins = emit_quant(v, C_Z[tt % 2], C_QA[tt % 2], C_QU, C_QN, C_QH,
                                 C_BM, C_QST, D)
                sv.tick(ins)
            q_made = sv.n
            e_q_done.append(q_made)

            @block.tensor
            def _(t, tt=tt, q_made=q_made):
                wait(t, sv, q_made)
                for hf, bk in ((0, 6), (1, 7)):
                    pe_wait_free(t, bk)
                    for j in range(4):
                        ins = t.transpose(
                            psb[bk][:, j * 128:(j + 1) * 128],
                            bf(C_QST, D)[:, (hf * 4 + j) * 128:(hf * 4 + j + 1) * 128],
                            ident)
                    smm.tick(ins)
                    bank_fill[bk] = (smm, smm.n)

            @block.vector
            def _(v, tt=tt):
                xqview = ARb[:, 2 * O_NRM:2 * O_NRM + 8 * T].rearrange(
                    "p (c t) -> p c t", c=8)
                for hf, bk in ((0, 6), (1, 7)):
                    use_wait_fill(v, bk)
                    ins = v.tensor_copy(
                        out=xqview[:, hf * 4:hf * 4 + 4, tt * 128:(tt + 1) * 128],
                        in_=psb[bk][:, 0:512].rearrange("p (c t) -> p c t", c=4))
                    sv.tick(ins)
                    bank_free[bk] = (sv, sv.n)
        xq_ready = sv.n
        dump_arena("E")

        nc.leave_named_scope(*_scope_st.pop(), False)
        _scope_st.append(("phF", nc.enter_named_scope("phF", False)[0]))
        def wgate_col(dc):
            return 2 * O_XA + dc * IF

        def wup_col(dc):
            if dc < 3:
                return 2 * 13584 + dc * IF
            return 2 * O_SCR + (dc - 3) * IF

        def wdn_col(ic):
            if ic < 16:
                return 2 * O_RW + ic * 1024
            if ic < 18:
                return 2 * 42128 + (ic - 16) * 1024
            if ic < 20:
                return 2 * O_WT + (ic - 18) * 1024
            return 2 * O_XT + (ic - 20) * 1024

        def f2_evac(e, kind, rb, i0, cnt, bk):
            off = rb * 128
            src4 = psb[bk][:, 0:cnt * 128].rearrange("p (c t) -> p c t", c=cnt)
            if kind == "g":
                gv = ARb[:, 2 * O_XA:2 * O_XA + 8 * IF].rearrange(
                    "p (c t) -> p c t", t=IF)
                return e.tensor_copy(out=gv[:, i0:i0 + 4, off:off + 128], in_=src4)
            if kind == "u":
                if i0 == 0:
                    uv = ARb[:, 2 * 13584:2 * 13584 + 3 * IF].rearrange(
                        "p (c t) -> p c t", t=IF)
                    e.tensor_copy(out=uv[:, 0:3, off:off + 128],
                           in_=psb[bk][:, 0:384].rearrange("p (c t) -> p c t", c=3))
                    return e.tensor_copy(
                        out=ARb[:, wup_col(3) + off:wup_col(3) + off + 128],
                        in_=psb[bk][:, 384:512])
                uv = ARb[:, 2 * O_SCR:2 * O_SCR + 5 * IF].rearrange(
                    "p (c t) -> p c t", t=IF)
                return e.tensor_copy(out=uv[:, 1:5, off:off + 128], in_=src4)
            if i0 < 16:
                dv = ARb[:, 2 * O_RW:2 * O_RW + 16 * 1024].rearrange(
                    "p (c t) -> p c t", t=1024)
                return e.tensor_copy(out=dv[:, i0:i0 + 4, off:off + 128], in_=src4)
            if i0 == 16:
                dv = ARb[:, 2 * 42128:2 * 42128 + 2048].rearrange(
                    "p (c t) -> p c t", t=1024)
                e.tensor_copy(out=dv[:, 0:2, off:off + 128],
                       in_=psb[bk][:, 0:256].rearrange("p (c t) -> p c t", c=2))
                dv2 = ARb[:, 2 * O_WT:2 * O_WT + 2048].rearrange(
                    "p (c t) -> p c t", t=1024)
                return e.tensor_copy(out=dv2[:, 0:2, off:off + 128],
                              in_=psb[bk][:, 256:512].rearrange(
                                  "p (c t) -> p c t", c=2))
            dv = ARb[:, 2 * O_XT:2 * O_XT + 2048].rearrange(
                "p (c t) -> p c t", t=1024)
            return e.tensor_copy(out=dv[:, 0:2, off:off + 128],
                          in_=psb[bk][:, 0:256].rearrange("p (c t) -> p c t", c=2))

        FS = (44384, 45792)
        f2chunks = []
        for rb in range(NIC):
            f2chunks.append(("g", rb))
            f2chunks.append(("u", rb))
        for rb in range(8):
            f2chunks.append(("d", rb))
        f2_src = {"g": wqgh, "u": wquh, "d": wqdh}
        f2_done = []
        mlp_guard_final = mlp_guard[0]
        dgroups = [(0, 4, 6), (4, 4, 7), (8, 4, 6), (12, 4, 7), (16, 4, 6),
                   (20, 2, 7)]
        for k, (kind, rb) in enumerate(f2chunks):
            nbf = IF if kind == "d" else 1024
            qf = qfa if k % 2 == 0 else qfb

            @block.sync
            def _(y, k=k, kind=kind, rb=rb, nbf=nbf, qf=qf):
                if k == 0:
                    if mlp_guard_final is not None:
                        wait(y, smm, mlp_guard_final)
                    wait(y, qwa, qwa.n)
                    wait(y, qwb, qwb.n)
                if k >= 2:
                    wait(y, sv, f2_done[k - 2])
                ins = y.dma_start(out=bf(FS[k % 2], nbf),
                                  in_=f2_src[kind][rb * 128:(rb + 1) * 128, :])
                qf.tick(ins, 16)
            ldw = (qf, qf.n)

            groups = dgroups if kind == "d" else [(0, 4, 6), (4, 4, 7)]
            for gi, (i0, cnt, bk) in enumerate(groups):
                @block.tensor
                def _(t, k=k, gi=gi, i0=i0, cnt=cnt, bk=bk, nbf=nbf, ldw=ldw):
                    if k == 0 and gi == 0:
                        wait(t, sv, xq_ready)
                    if gi == 0:
                        wait(t, *ldw)
                    pe_wait_free(t, bk)
                    for j in range(cnt):
                        ins = t.transpose(
                            psb[bk][:, j * 128:(j + 1) * 128],
                            bf(FS[k % 2], nbf)[:, (i0 + j) * 128:(i0 + j + 1) * 128],
                            ident)
                    smm.tick(ins)
                    bank_fill[bk] = (smm, smm.n)

                @block.vector
                def _(v, k=k, gi=gi, kind=kind, rb=rb, i0=i0, cnt=cnt, bk=bk):
                    if k == 0 and gi == 0:
                        wait(v, qs, x2_stored[NTT - 1])
                    use_wait_fill(v, bk)
                    ins = f2_evac(v, kind, rb, i0, cnt, bk)
                    sv.tick(ins)
                    bank_free[bk] = (sv, sv.n)
            f2_done.append(sv.n)
        wq_ss_ready = sv.n
        dump_arena("F")

        nc.leave_named_scope(*_scope_st.pop(), False)
        _scope_st.append(("phG", nc.enter_named_scope("phG", False)[0]))
        G_SIL, G_QU = 272, 1808
        H2S = (784, 49504)
        QAS = (1296, 50016)
        G_QN, G_QH, G_BM = 44384, 44896, 45408
        STS = (45536, 48224)
        G_HQ, G_X2 = 45792, 47200

        def xqv(dc, t0, w):
            return ARb[:, 2 * O_NRM + dc * T + t0: 2 * O_NRM + dc * T + t0 + w]

        y_stores = []
        g_out_done = []
        down_fill_prev = [0]
        sil_cons = [0]
        pend_evac = [None]
        pend_evac2 = []
        pend_tp = []
        st_guard = [0, 0]
        for tt in range(NTT):
            @block.sync
            def _(y, tt=tt):
                if tt == 0:
                    wait(y, qs, x2_stored[NTT - 1])
                if y_stores:
                    wait(y, qyo, y_stores[-1])
                ins = y.dma_start(out=f32(G_X2, D),
                                  in_=x2h[tt * 128:(tt + 1) * 128, :])
                qxa.tick(ins, 16)
            x2_ld = (qxa, qxa.n)

            for sl, (o0, w) in enumerate(IF_SLICES):
                gbk, ubk = (0, 2)[sl % 2], (1, 3)[sl % 2]
                tbk = (4, 7)[sl % 2]
                h2c = H2S[sl % 2]
                qac = QAS[sl % 2]
                stc = STS[sl % 2]

                for wnm, bk in (("gate", gbk), ("up", ubk)):
                    @block.tensor
                    def _(t, tt=tt, sl=sl, o0=o0, w=w, wnm=wnm, bk=bk):
                        if tt == 0 and sl == 0 and wnm == "gate":
                            wait(t, sv, wq_ss_ready)
                        pe_wait_free(t, bk)
                        for dc in range(NDC):
                            col = (wgate_col(dc) if wnm == "gate" else wup_col(dc)) + o0
                            ins = t.matmul(ps[bk][:, 0:w],
                                           xqv(dc, tt * 128, 128),
                                           ARb[:, col:col + w],
                                           start=(dc == 0), stop=(dc == NDC - 1))
                        smm.tick(ins)
                        bank_fill[bk] = (smm, smm.n)

                @block.tensor
                def _(t, sl=sl):
                    if pend_tp:
                        psl, pw, ptbk, pstc, phq = pend_tp.pop(0)
                        wait(t, sv, phq)
                        pe_wait_free(t, ptbk)
                        for j in range(pw // 128):
                            ins = t.transpose(psb[ptbk][:, j * 128:(j + 1) * 128],
                                              bf(pstc, pw)[:, j * 128:(j + 1) * 128],
                                              ident)
                        smm.tick(ins)
                        bank_fill[ptbk] = (smm, smm.n)
                        st_guard[psl % 2] = smm.n
                        assert pend_evac[0] is None
                        pend_evac[0] = (psl, pw, ptbk, smm.n)

                @block.scalar
                def _(s, w=w, gbk=gbk):
                    use_wait_fill(s, gbk)
                    if sil_cons[0]:
                        wait(s, sv, sil_cons[0])
                    ins = s.activation(out=f32(G_SIL, w), in_=ps[gbk][:, 0:w],
                                       func=AF.Silu)
                    ss.tick(ins)
                sil_made = ss.n
                bank_free[gbk] = (ss, sil_made)

                @block.vector
                def _(v, w=w, sil_made=sil_made, ubk=ubk, h2c=h2c):
                    wait(v, ss, sil_made)
                    use_wait_fill(v, ubk)
                    ins = v.tensor_tensor(out=f32(h2c, w), in0=ps[ubk][:, 0:w],
                                          in1=f32(G_SIL, w), op=ALU.mult)
                    sv.tick(ins)
                h2_made = sv.n
                sil_cons[0] = h2_made
                bank_free[ubk] = (sv, h2_made)

                @block.scalar
                def _(s, w=w, h2_made=h2_made, h2c=h2c, qac=qac):
                    wait(s, sv, h2_made)
                    ins = s.activation(out=f32(qac, w), in_=f32(h2c, w), func=AF.Abs)
                    ss.tick(ins)
                habs = ss.n

                @block.vector
                def _(v, w=w, habs=habs, h2c=h2c, qac=qac, stc=stc, tt=tt, sl=sl):
                    if pend_evac[0] is not None:
                        psl, pw, pbk, pfill = pend_evac[0]
                        wait(v, smm, pfill)
                        if down_fill_prev[0]:
                            wait(v, smm, down_fill_prev[0])
                        nt = pw // 128
                        ins = v.tensor_copy(
                            out=bf(G_HQ, IF)[:, psl * 512:psl * 512 + nt * 128],
                            in_=psb[pbk][:, 0:nt * 128])
                        sv.tick(ins)
                        bank_free[pbk] = (sv, sv.n)
                        pend_evac[0] = None
                    wait(v, ss, habs)
                    if st_guard[sl % 2]:
                        wait(v, smm, st_guard[sl % 2])
                    v.engine_nop(); v.engine_nop()
                    ins = emit_quant(v, h2c, qac, G_QU, G_QN, G_QH, G_BM, stc, w)
                    sv.tick(ins)
                hq_made = sv.n
                pend_tp.append((sl, w, tbk, stc, hq_made))

            @block.tensor
            def _(t, tt=tt):
                while pend_tp:
                    psl, pw, ptbk, pstc, phq = pend_tp.pop(0)
                    wait(t, sv, phq)
                    pe_wait_free(t, ptbk)
                    for j in range(pw // 128):
                        ins = t.transpose(psb[ptbk][:, j * 128:(j + 1) * 128],
                                          bf(pstc, pw)[:, j * 128:(j + 1) * 128],
                                          ident)
                    smm.tick(ins)
                    bank_fill[ptbk] = (smm, smm.n)
                    st_guard[psl % 2] = smm.n
                    pend_evac2.append((psl, pw, ptbk, smm.n))

            @block.vector
            def _(v, tt=tt):
                if pend_evac[0] is not None:
                    pend_evac2.insert(0, pend_evac[0])
                    pend_evac[0] = None
                while pend_evac2:
                    psl, pw, pbk, pfill = pend_evac2.pop(0)
                    wait(v, smm, pfill)
                    if down_fill_prev[0]:
                        wait(v, smm, down_fill_prev[0])
                    nt = pw // 128
                    ins = v.tensor_copy(
                        out=bf(G_HQ, IF)[:, psl * 512:psl * 512 + nt * 128],
                        in_=psb[pbk][:, 0:nt * 128])
                    sv.tick(ins)
                    bank_free[pbk] = (sv, sv.n)
            hq_fm_ready = sv.n

            @block.tensor
            def _(t, tt=tt, hq_fm_ready=hq_fm_ready):
                wait(t, sv, hq_fm_ready)
                for n in range(2):
                    bk = 5 + n
                    pe_wait_free(t, bk)
                    for ic in range(NIC):
                        ins = t.matmul(
                            ps[bk][:], bf(G_HQ, IF)[:, ic * 128:(ic + 1) * 128],
                            ARb[:, wdn_col(ic) + n * 512:wdn_col(ic) + n * 512 + 512],
                            start=(ic == 0), stop=(ic == NIC - 1))
                    smm.tick(ins)
                    bank_fill[bk] = (smm, smm.n)
                down_fill_prev[0] = smm.n

            @block.vector
            def _(v, tt=tt, x2_ld=x2_ld):
                wait(v, *x2_ld)
                for n in range(2):
                    bk = 5 + n
                    use_wait_fill(v, bk)
                    ins = v.tensor_tensor(out=f32(G_X2 + n * 512, 512),
                                          in0=ps[bk][:],
                                          in1=f32(G_X2 + n * 512, 512), op=ALU.add)
                    sv.tick(ins)
                    bank_free[bk] = (sv, sv.n)
            g_out_done.append(sv.n)

            @block.sync
            def _(y, tt=tt):
                wait(y, sv, g_out_done[-1])
                ins = y.dma_start(out=yh[tt * 128:(tt + 1) * 128, :],
                                  in_=f32(G_X2, D))
                qyo.tick(ins, 16)
            y_stores.append(qyo.n)
            if tt == 0:
                dump_arena("G0")

        nc.leave_named_scope(*_scope_st.pop(), False)

        @block.sync
        def _(y):
            wait(y, qyo, qyo.n)

    return nc


_NC_CACHE = {}


def kernel(L0_attn_x, L0_attn_Wk, L0_attn_Wv, L0_attn_Wr, L0_attn_Wo,
           L0_attn_time_decay, L0_attn_time_first, L0_attn_mixer_scale,
           L0_attn_Wfc, L0_attn_Wmlp, L0_attn_mlp_scale,
           L0_ffn_w_gate, L0_ffn_w_up, L0_ffn_w_down):
    x = np.ascontiguousarray(np.asarray(L0_attn_x, dtype=np.float32))
    B = x.shape[0]
    assert x.shape == (8, T, D)

    def t_(a):
        return np.ascontiguousarray(np.asarray(a, dtype=np.float32).T)

    shared = {
        "wkT": t_(L0_attn_Wk),
        "wvT": t_(L0_attn_Wv),
        "wrT": t_(L0_attn_Wr),
        "woT": t_(L0_attn_Wo),
        "wfcT": t_(L0_attn_Wfc),
        "wmlpT": t_(L0_attn_Wmlp),
        "wg": np.ascontiguousarray(np.asarray(L0_ffn_w_gate, dtype=np.float32)),
        "wu": np.ascontiguousarray(np.asarray(L0_ffn_w_up, dtype=np.float32)),
        "wd": np.ascontiguousarray(np.asarray(L0_ffn_w_down, dtype=np.float32)),
        "tdp": np.ascontiguousarray(
            np.asarray(L0_attn_time_decay, dtype=np.float32).reshape(8, 128).T),
        "tfp": np.ascontiguousarray(
            np.asarray(L0_attn_time_first, dtype=np.float32).reshape(8, 128).T),
        "msp": np.ascontiguousarray(np.broadcast_to(
            np.asarray(L0_attn_mixer_scale, dtype=np.float32)[None, :], (128, D))),
        "mlsp": np.ascontiguousarray(np.broadcast_to(
            np.asarray(L0_attn_mlp_scale, dtype=np.float32)[None, :], (128, D))),
        "identp": np.eye(128, dtype=ml_dtypes.bfloat16),
    }

    key = ("nc", _DBG_PHASE)
    if key not in _NC_CACHE:
        _NC_CACHE[key] = build_nc(_DBG_PHASE)
    nc = _NC_CACHE[key]

    in_maps = [dict(shared, x=np.ascontiguousarray(x[c])) for c in range(B)]
    res = run_bass_kernel_spmd(nc, in_maps, core_ids=list(range(B)))
    if _DBG_PHASE is not None:
        _NC_CACHE["dbg"] = [r.get("dbgarena") for r in res.results]
    out = np.stack([r["y"] for r in res.results], axis=0)
    return out.astype(np.float32)


# version 11 (final)
# speedup vs baseline: 1.5474x; 1.0378x over previous
import numpy as np
import ml_dtypes
from contextlib import ExitStack

import concourse.bass as bass
import concourse.mybir as mybir
from concourse.bass_utils import run_bass_kernel_spmd

F32 = mybir.dt.float32
BF16 = mybir.dt.bfloat16
AF = mybir.ActivationFunctionType
ALU = mybir.AluOpType
AXX = mybir.AxisListType.X

T = 2048
D = 1024
H = 4096
IF = 2816
NTT = 16
NDC = 8
NHC = 32
NIC = 22
IF_SLICES = [(0, 512), (512, 512), (1024, 512), (1536, 512), (2048, 512), (2560, 256)]

O_IDENT = 0
O_EW = 64
O_EU = 72
O_ST = 80
O_MS = 272
O_MLS = 1296
O_XA = 2320
O_NRM = 18704
O_RW = 26896
O_SCR = 35088
O_WT = 43360
O_WO = 44384
O_XT = 48480
ACOLS = 50688

(ST_SUM, ST_SSQ, ST_M, ST_MSQ, ST_VAR, ST_CN, ST_INV, ST_BN, ST_BI, ST_C2) = range(10)
EPS_LN = 1e-5
EPS_RMS = 1e-6


def _stc(slot):
    return O_ST + 16 * slot


class Sem:
    def __init__(self, h, name):
        self.h = h
        self.name = name
        self.n = 0

    def tick(self, ins, k=1):
        ins.then_inc(self.h, k)
        self.n += k
        return self.n


_DBG_PHASE = None


def build_nc(dbg_phase=None):
    nc = bass.Bass("TRN2", target_bir_lowering=False)
    dbgh = None
    if dbg_phase is not None:
        dbgh = nc.dram_tensor("dbgarena", [128, ACOLS], F32,
                              kind="ExternalOutput").ap()

    xh = nc.dram_tensor("x", [T, D], F32, kind="ExternalInput").ap()
    yh = nc.dram_tensor("y", [T, D], F32, kind="ExternalOutput").ap()
    x2h = nc.dram_tensor("x2s", [T, D], F32, kind="Internal").ap()
    wqgh = nc.dram_tensor("wqg_s", [IF, D], BF16, kind="Internal").ap()
    wquh = nc.dram_tensor("wqu_s", [IF, D], BF16, kind="Internal").ap()
    wqdh = nc.dram_tensor("wqd_s", [D, IF], BF16, kind="Internal").ap()
    wkh = nc.dram_tensor("wkT", [D, D], F32, kind="ExternalInput").ap()
    wvh = nc.dram_tensor("wvT", [D, D], F32, kind="ExternalInput").ap()
    wrh = nc.dram_tensor("wrT", [D, D], F32, kind="ExternalInput").ap()
    woh = nc.dram_tensor("woT", [D, D], F32, kind="ExternalInput").ap()
    wfch = nc.dram_tensor("wfcT", [D, H], F32, kind="ExternalInput").ap()
    wmlph = nc.dram_tensor("wmlpT", [H, D], F32, kind="ExternalInput").ap()
    wgh = nc.dram_tensor("wg", [IF, D], F32, kind="ExternalInput").ap()
    wuh = nc.dram_tensor("wu", [IF, D], F32, kind="ExternalInput").ap()
    wdh = nc.dram_tensor("wd", [D, IF], F32, kind="ExternalInput").ap()
    tdh = nc.dram_tensor("tdp", [128, 8], F32, kind="ExternalInput").ap()
    tfh = nc.dram_tensor("tfp", [128, 8], F32, kind="ExternalInput").ap()
    msh = nc.dram_tensor("msp", [128, D], F32, kind="ExternalInput").ap()
    mlsh = nc.dram_tensor("mlsp", [128, D], F32, kind="ExternalInput").ap()
    idh = nc.dram_tensor("identp", [128, 128], BF16, kind="ExternalInput").ap()

    with ExitStack() as es:
        AR = es.enter_context(nc.sbuf_tensor("arena", [128, ACOLS], F32))
        ps = [es.enter_context(nc.psum_tensor(f"psum{i}", [128, 512], F32))
              for i in range(8)]
        qy = Sem(es.enter_context(nc.semaphore("qy")), "qy")
        qs = Sem(es.enter_context(nc.semaphore("qs")), "qs")
        qg = Sem(es.enter_context(nc.semaphore("qg")), "qg")
        qga = Sem(es.enter_context(nc.semaphore("qga")), "qga")
        qgb = Sem(es.enter_context(nc.semaphore("qgb")), "qgb")
        qxa = Sem(es.enter_context(nc.semaphore("qxa")), "qxa")
        qxb = Sem(es.enter_context(nc.semaphore("qxb")), "qxb")
        qyo = Sem(es.enter_context(nc.semaphore("qyo")), "qyo")
        qga2 = Sem(es.enter_context(nc.semaphore("qga2")), "qga2")
        qgb2 = Sem(es.enter_context(nc.semaphore("qgb2")), "qgb2")
        qgc2 = Sem(es.enter_context(nc.semaphore("qgc2")), "qgc2")
        qgd2 = Sem(es.enter_context(nc.semaphore("qgd2")), "qgd2")
        qwa = Sem(es.enter_context(nc.semaphore("qwa")), "qwa")
        qwb = Sem(es.enter_context(nc.semaphore("qwb")), "qwb")
        qfa = Sem(es.enter_context(nc.semaphore("qfa")), "qfa")
        qfb = Sem(es.enter_context(nc.semaphore("qfb")), "qfb")
        smm = Sem(es.enter_context(nc.semaphore("smm")), "smm")
        sv = Sem(es.enter_context(nc.semaphore("sv")), "sv")
        ss = Sem(es.enter_context(nc.semaphore("ss")), "ss")
        sg = Sem(es.enter_context(nc.semaphore("sg")), "sg")
        block = es.enter_context(nc.Block())

        _scope_st = []
        ARb = AR[:].bitcast(BF16)

        def f32(c0, w):
            return AR[:, c0:c0 + w]

        def bf(c0f32, wbf):
            return ARb[:, 2 * c0f32:2 * c0f32 + wbf]

        ident = ARb[:, 0:128]
        psb = [p[:].bitcast(BF16) for p in ps]

        bank_free = [(None, 0)] * 8
        bank_fill = [(None, 0)] * 8

        def wait(e, sem, val):
            assert val <= sem.n, f"wait {sem.name}={val} > emitted {sem.n}"
            if val > 0:
                e.wait_ge(sem.h, val)

        def pe_wait_free(e, bk):
            s, v = bank_free[bk]
            if s is not None:
                wait(e, s, v)

        def use_wait_fill(e, bk):
            s, v = bank_fill[bk]
            assert s is not None, f"bank {bk} never filled"
            wait(e, s, v)

        def dump_arena(tag):
            if dbg_phase != tag:
                return
            @block.sync
            def _(y):
                wait(y, sv, sv.n)
                wait(y, ss, ss.n)
                wait(y, sg, sg.n)
                wait(y, smm, smm.n)
                wait(y, qg, qg.n)
                wait(y, qs, qs.n)
                wait(y, qga, qga.n)
                wait(y, qgb, qgb.n)
                wait(y, qxa, qxa.n)
                wait(y, qxb, qxb.n)
                wait(y, qyo, qyo.n)
                ins = y.dma_start(out=dbgh[:], in_=AR[:])
                qy.tick(ins, 16)
            barrier = qy.n
            @block.vector
            def _(v):
                wait(v, qy, barrier)
            @block.scalar
            def _(s):
                wait(s, qy, barrier)
            @block.gpsimd
            def _(g):
                wait(g, qy, barrier)
            @block.tensor
            def _(t):
                wait(t, qy, barrier)

        def nrmv(dc, t0, w):
            return ARb[:, 2 * O_NRM + dc * T + t0: 2 * O_NRM + dc * T + t0 + w]

        def rwv(dc, t0, w):
            return ARb[:, 2 * O_RW + dc * T + t0: 2 * O_RW + dc * T + t0 + w]

        def emit_quant(v, cz, cqa, cqu, cqn, cqh, cbm, cst, w):
            nb = w // 16
            zg = f32(cz, w).rearrange("p (g k) -> p g k", k=16)
            v.reduce_max(out=f32(cbm, nb), in_=zg, axis=AXX, apply_absolute_value=True)
            v.tensor_scalar(out=f32(cqn, w), in0=f32(cz, w), scalar1=0.0,
                            scalar2=None, op0=ALU.is_ge, op1=ALU.bypass)
            v.tensor_scalar(out=f32(cbm + nb, nb), in0=f32(cbm, nb), scalar1=1.0 / 6.0,
                            scalar2=1e-12, op0=ALU.mult, op1=ALU.max)
            v.tensor_scalar(out=f32(cqn, w), in0=f32(cqn, w), scalar1=2.0,
                            scalar2=-1.0, op0=ALU.mult, op1=ALU.add)
            v.reciprocal(out=f32(cbm + 2 * nb, nb), in_=f32(cbm + nb, nb))
            v.engine_nop()
            rcp_b = f32(cbm + 2 * nb, nb).unsqueeze(2).broadcast_to([128, nb, 16])
            scl_b = f32(cbm + nb, nb).unsqueeze(2).broadcast_to([128, nb, 16])
            qag = f32(cqa, w).rearrange("p (g k) -> p g k", k=16)
            qug = f32(cqu, w).rearrange("p (g k) -> p g k", k=16)
            v.tensor_tensor(out=qug, in0=qag, in1=rcp_b, op=ALU.mult)
            v.tensor_tensor(out=qag, in0=f32(cqn, w).rearrange("p (g k) -> p g k", k=16),
                            in1=scl_b, op=ALU.mult)
            u = f32(cqu, w)
            v.tensor_scalar(out=f32(cqn, w), in0=u, scalar1=0.25, scalar2=None,
                            op0=ALU.is_gt)
            v.tensor_scalar(out=f32(cqh, w), in0=u, scalar1=2.5, scalar2=None,
                            op0=ALU.is_gt)
            for tn, th in ((0.75, 3.5), (1.25, 5.0), (1.75, 5.0)):
                v.scalar_tensor_tensor(out=f32(cqn, w), in0=u, scalar=tn,
                                       in1=f32(cqn, w), op0=ALU.is_gt, op1=ALU.add)
                v.scalar_tensor_tensor(out=f32(cqh, w), in0=u, scalar=th,
                                       in1=f32(cqh, w), op0=ALU.is_gt, op1=ALU.add)
            v.engine_nop()
            v.scalar_tensor_tensor(out=f32(cqn, w), in0=f32(cqn, w), scalar=0.5,
                                   in1=f32(cqh, w), op0=ALU.mult, op1=ALU.add)
            v.engine_nop(); v.engine_nop()
            return v.tensor_tensor(
                out=bf(cst, w).rearrange("p (g k) -> p g k", k=16),
                in0=f32(cqn, w).rearrange("p (g k) -> p g k", k=16),
                in1=qag, op=ALU.mult)

        _scope_st.append(("phP0", nc.enter_named_scope("phP0", False)[0]))
        @block.sync
        def _(y):
            qy.tick(y.dma_start(out=ARb[:, 0:128], in_=idh[:]), 16)
            qy.tick(y.dma_start(out=f32(O_EW, 8), in_=tdh[:]), 16)
            qy.tick(y.dma_start(out=f32(O_EU, 8), in_=tfh[:]), 16)
            qy.tick(y.dma_start(out=f32(O_MS, D), in_=msh[:]), 16)
            qy.tick(y.dma_start(out=f32(O_MLS, D), in_=mlsh[:]), 16)
            for tt in range(NTT):
                ins = y.dma_start(out=f32(O_XA + tt * D, D),
                                  in_=xh[tt * 128:(tt + 1) * 128, :])
                qy.tick(ins, 16)
        n_pre = 16 * 5
        x_all = qy.n

        @block.scalar
        def _(s):
            wait(s, qy, x_all)
            s.activation(out=f32(O_EW, 8), in_=f32(O_EW, 8), func=AF.Exp)
            s.activation(out=f32(O_EW, 8), in_=f32(O_EW, 8), func=AF.Exp, scale=-1.0)
            ins = s.activation(out=f32(O_EU, 8), in_=f32(O_EU, 8), func=AF.Exp)
            ss.tick(ins)
        ew_ready = ss.n

        nc.leave_named_scope(*_scope_st.pop(), False)
        _scope_st.append(("phA", nc.enter_named_scope("phA", False)[0]))
        @block.vector
        def _(v):
            wait(v, qy, x_all)
            for tt in range(NTT):
                ins = v.tensor_scalar(
                    out=f32(35600, D), in0=f32(O_XA + tt * D, D), scalar1=1.0,
                    scalar2=None, op0=ALU.mult, op1=ALU.add,
                    accum_out=AR[:, _stc(ST_SUM) + tt:_stc(ST_SUM) + tt + 1])
                sv.tick(ins)

        @block.scalar
        def _(s):
            wait(s, qy, x_all)
            for tt in range(NTT):
                ins = s.activation(
                    out=f32(36624, D), in_=f32(O_XA + tt * D, D), func=AF.Square,
                    accum_out=AR[:, _stc(ST_SSQ) + tt:_stc(ST_SSQ) + tt + 1])
                ss.tick(ins)
        a_ssq = ss.n

        @block.vector
        def _(v):
            wait(v, ss, a_ssq)
            v.tensor_scalar(out=f32(_stc(ST_M), 16), in0=f32(_stc(ST_SUM), 16),
                            scalar1=1.0 / D, scalar2=None, op0=ALU.mult,
                            op1=ALU.bypass)
            v.engine_nop(); v.engine_nop(); v.engine_nop()
            v.tensor_tensor(out=f32(_stc(ST_MSQ), 16), in0=f32(_stc(ST_M), 16),
                            in1=f32(_stc(ST_M), 16), op=ALU.mult)
            v.engine_nop(); v.engine_nop(); v.engine_nop()
            ins = v.scalar_tensor_tensor(
                out=f32(_stc(ST_VAR), 16), in0=f32(_stc(ST_SSQ), 16), scalar=1.0 / D,
                in1=f32(_stc(ST_MSQ), 16), op0=ALU.mult, op1=ALU.subtract)
            sv.tick(ins)
        a_var = sv.n

        @block.vector
        def _(v):
            v.engine_nop(); v.engine_nop(); v.engine_nop()
            v.tensor_scalar(out=f32(_stc(ST_CN), 16), in0=f32(_stc(ST_VAR), 16),
                            scalar1=1.0 + EPS_RMS, scalar2=EPS_LN * EPS_RMS,
                            op0=ALU.mult, op1=ALU.add)
            ins = v.tensor_scalar(out=f32(_stc(ST_INV), 16), in0=f32(_stc(ST_VAR), 16),
                                  scalar1=EPS_LN, scalar2=None, op0=ALU.add,
                                  op1=ALU.bypass)
            sv.tick(ins)
        a_args = sv.n

        @block.scalar
        def _(s):
            wait(s, sv, a_args)
            s.activation(out=f32(_stc(ST_CN), 16), in_=f32(_stc(ST_CN), 16),
                         func=AF.Ln)
            s.activation(out=f32(_stc(ST_CN), 16), in_=f32(_stc(ST_CN), 16),
                         func=AF.Exp, scale=-0.5)
            s.activation(out=f32(_stc(ST_INV), 16), in_=f32(_stc(ST_INV), 16),
                         func=AF.Ln)
            ins = s.activation(out=f32(_stc(ST_INV), 16), in_=f32(_stc(ST_INV), 16),
                               func=AF.Exp, scale=-0.5)
            ss.tick(ins)
        a_invcn = ss.n

        @block.vector
        def _(v):
            wait(v, ss, a_invcn)
            v.scalar_tensor_tensor(out=f32(_stc(ST_BN), 16), in0=f32(_stc(ST_M), 16),
                                   scalar=-1.0, in1=f32(_stc(ST_CN), 16),
                                   op0=ALU.mult, op1=ALU.mult)
            ins = v.scalar_tensor_tensor(out=f32(_stc(ST_BI), 16),
                                         in0=f32(_stc(ST_M), 16), scalar=-1.0,
                                         in1=f32(_stc(ST_INV), 16),
                                         op0=ALU.mult, op1=ALU.mult)
            sv.tick(ins)
        a_smalls = sv.n

        N1STG = 35088
        for tt in range(NTT):
            @block.scalar
            def _(s, tt=tt):
                wait(s, sv, a_smalls)
                if bank_fill[7][0] is not None:
                    wait(s, smm, bank_fill[7][1])
                ins = s.activation(
                    out=bf(N1STG, D), in_=f32(O_XA + tt * D, D), func=AF.Identity,
                    bias=AR[:, _stc(ST_BN) + tt:_stc(ST_BN) + tt + 1],
                    scale=AR[:, _stc(ST_CN) + tt:_stc(ST_CN) + tt + 1])
                ss.tick(ins)
            n1_made = ss.n

            @block.tensor
            def _(t, tt=tt, n1_made=n1_made):
                wait(t, ss, n1_made)
                for half, bk in ((0, 6), (1, 7)):
                    pe_wait_free(t, bk)
                    for j in range(4):
                        ins = t.transpose(
                            psb[bk][:, j * 128:(j + 1) * 128],
                            bf(N1STG, D)[:, (half * 4 + j) * 128:(half * 4 + j + 1) * 128],
                            ident)
                    smm.tick(ins)
                    bank_fill[bk] = (smm, smm.n)

            @block.vector
            def _(v, tt=tt):
                nview = ARb[:, 2 * O_NRM:2 * O_NRM + 8 * T].rearrange(
                    "p (c t) -> p c t", c=8)
                for half, bk in ((0, 6), (1, 7)):
                    use_wait_fill(v, bk)
                    ins = v.tensor_copy(
                        out=nview[:, half * 4:half * 4 + 4, tt * 128:(tt + 1) * 128],
                        in_=psb[bk][:, 0:512].rearrange("p (c t) -> p c t", c=4))
                    sv.tick(ins)
                    bank_free[bk] = (sv, sv.n)
                ins = v.tensor_scalar(
                    out=f32(O_XA + tt * D, D), in0=f32(O_XA + tt * D, D),
                    scalar1=AR[:, _stc(ST_INV) + tt:_stc(ST_INV) + tt + 1],
                    scalar2=AR[:, _stc(ST_BI) + tt:_stc(ST_BI) + tt + 1],
                    op0=ALU.mult, op1=ALU.add)
                sv.tick(ins)
        normed1_ready = sv.n
        dump_arena("A")

        nc.leave_named_scope(*_scope_st.pop(), False)
        _scope_st.append(("phB", nc.enter_named_scope("phB", False)[0]))
        C_KV, C_S, C_R, C_KSB = 35088, 37152, 39232, 41280
        wt_slot = [O_WT, O_WT + 512]
        wload_count = []
        wload_used = {}
        load_idx = [0]
        mlp_guard = [None]

        def wq_load(g, src_ap, nbf=1024):
            li = load_idx[0]
            load_idx[0] += 1
            if li >= 2:
                s_, v_ = wload_used[li - 2]
                wait(g, s_, v_)
            if mlp_guard[0] is not None:
                wait(g, smm, mlp_guard[0])
                mlp_guard[0] = None
            dst = bf(wt_slot[li % 2], nbf)
            if src_ap.ndim == 3:
                dst = dst.rearrange("p (c m) -> p c m", c=8)
            qsem = qga if li % 2 == 0 else qgb
            ins = g.dma_start(out=dst, in_=src_ap)
            qsem.tick(ins, 16)
            wload_count.append((qsem, qsem.n))
            return li

        def colslice8(wh, o):
            return wh[:, o * 128:(o + 1) * 128].rearrange("(c p) m -> p c m", p=128)

        @block.vector
        def _(v):
            ins = v.memset(f32(C_S, 1), 0.0)
            sv.tick(ins)
        szero = sv.n

        for o in range(NDC):
            lis = {}

            @block.gpsimd
            def _(g, o=o):
                lis["k"] = wq_load(g, colslice8(wkh, o))
                lis["v"] = wq_load(g, colslice8(wvh, o))

            @block.tensor
            def _(t, o=o):
                if o == 0:
                    wait(t, sv, normed1_ready)
                for nm, b0 in (("k", 0), ("v", 4)):
                    li = lis[nm]
                    wait(t, *wload_count[li])
                    for bk in range(b0, b0 + 4):
                        pe_wait_free(t, bk)
                    wslot = bf(wt_slot[li % 2], 1024)
                    for t4 in range(4):
                        for dc in range(NDC):
                            ins = t.matmul(ps[b0 + t4][:],
                                           wslot[:, dc * 128:(dc + 1) * 128],
                                           nrmv(dc, t4 * 512, 512),
                                           start=(dc == 0), stop=(dc == NDC - 1))
                    smm.tick(ins)
                    for bk in range(b0, b0 + 4):
                        bank_fill[bk] = (smm, smm.n)
                    wload_used[li] = (smm, smm.n)

            @block.gpsimd
            def _(g, o=o):
                lis["r"] = wq_load(g, colslice8(wrh, o))

            @block.scalar
            def _(s, o=o):
                use_wait_fill(s, 0)
                for t4 in range(4):
                    ins = s.copy(out=f32(C_KSB + t4 * 512, 512), in_=ps[t4][:])
                ss.tick(ins)
            k_cp = ss.n
            for bk in range(4):
                bank_free[bk] = (ss, k_cp)

            @block.vector
            def _(v, o=o, k_cp=k_cp):
                wait(v, ss, k_cp)
                use_wait_fill(v, 4)
                for t4 in range(4):
                    ins = v.tensor_tensor(out=f32(C_KV + t4 * 512, 512),
                                          in0=f32(C_KSB + t4 * 512, 512),
                                          in1=ps[4 + t4][:], op=ALU.mult)
                sv.tick(ins)
            kv_made = sv.n
            for bk in range(4, 8):
                bank_free[bk] = (sv, kv_made)

            @block.tensor
            def _(t, o=o):
                li = lis["r"]
                wait(t, *wload_count[li])
                for bk in range(4):
                    pe_wait_free(t, bk)
                wslot = bf(wt_slot[li % 2], 1024)
                for t4 in range(4):
                    for dc in range(NDC):
                        ins = t.matmul(ps[t4][:], wslot[:, dc * 128:(dc + 1) * 128],
                                       nrmv(dc, t4 * 512, 512),
                                       start=(dc == 0), stop=(dc == NDC - 1))
                smm.tick(ins)
                for bk in range(4):
                    bank_fill[bk] = (smm, smm.n)
                wload_used[li] = (smm, smm.n)

            @block.scalar
            def _(s, o=o):
                use_wait_fill(s, 0)
                for t4 in range(4):
                    ins = s.activation(out=f32(C_R + t4 * 512, 512), in_=ps[t4][:],
                                       func=AF.Sigmoid)
                ss.tick(ins)
            r_made = ss.n
            for bk in range(4):
                bank_free[bk] = (ss, r_made)

            @block.vector
            def _(v, o=o, r_made=r_made):
                if o == 0:
                    wait(v, sv, szero)
                    wait(v, ss, ew_ready)
                ewb = AR[:, O_EW + o:O_EW + o + 1].broadcast_to([128, T])
                v.engine_nop(); v.engine_nop(); v.engine_nop()
                v.tensor_tensor_scan(out=f32(C_S + 1, T), data0=ewb,
                                     data1=f32(C_KV, T), initial=0.0,
                                     op0=ALU.mult, op1=ALU.add)
                v.engine_nop(); v.engine_nop(); v.engine_nop()
                v.scalar_tensor_tensor(out=f32(C_KV, T), in0=f32(C_KV, T),
                                       scalar=AR[:, O_EU + o:O_EU + o + 1],
                                       in1=f32(C_S, T), op0=ALU.mult, op1=ALU.add)
                wait(v, ss, r_made)
                v.engine_nop(); v.engine_nop(); v.engine_nop()
                ins = v.tensor_tensor(out=rwv(o, 0, T), in0=f32(C_KV, T),
                                      in1=f32(C_R, T), op=ALU.mult)
                sv.tick(ins)
        rw_ready = sv.n
        dump_arena("B")

        nc.leave_named_scope(*_scope_st.pop(), False)
        _scope_st.append(("phC", nc.enter_named_scope("phC", False)[0]))
        FP_Z = (48480, 48992, 49504)
        FP_QA = (39184, 39696, 40208)
        FP_QU, FP_QN, FP_QH, FP_BM = 40720, 41232, 41744, 42256
        FP_ST = (42352, 42608)
        fpjobs = []
        for rb in range(NIC):
            fpjobs.append(("g", rb, 0, 512))
            fpjobs.append(("g", rb, 512, 512))
        for rb in range(NIC):
            fpjobs.append(("u", rb, 0, 512))
            fpjobs.append(("u", rb, 512, 512))
        for rb in range(8):
            for c0 in range(0, IF, 512):
                fpjobs.append(("d", rb, c0, min(512, IF - c0)))
        fp_q_done = []
        fp_state = [0]

        def fp_src(kind, rb, c0, w):
            wsrc = {"g": wgh, "u": wuh, "d": wdh}[kind]
            return wsrc[rb * 128:(rb + 1) * 128, c0:c0 + w]

        def fp_dst(kind, rb, c0, w):
            wdst = {"g": wqgh, "u": wquh, "d": wqdh}[kind]
            return wdst[rb * 128:(rb + 1) * 128, c0:c0 + w]

        def fp_emit_wb(y, j):
            wait(y, sv, fp_q_done[j])
            ins = y.dma_start(out=fp_dst(*fpjobs[j]),
                              in_=bf(FP_ST[j % 2], fpjobs[j][3]))
            (qwa if j % 2 == 0 else qwb).tick(ins, 16)

        def emit_fp_job():
            j = fp_state[0]
            if j >= len(fpjobs):
                return
            fp_state[0] += 1
            kind, rb, c0, w = fpjobs[j]
            qf = qfa if j % 2 == 0 else qfb

            @block.sync
            def _(y, j=j, kind=kind, rb=rb, c0=c0, w=w, qf=qf):
                if j == 0:
                    wait(y, sv, rw_ready)
                if j >= 3:
                    wait(y, sv, fp_q_done[j - 3])
                ins = y.dma_start(out=f32(FP_Z[j % 3], w),
                                  in_=fp_src(kind, rb, c0, w))
                qf.tick(ins, 16)
                if j >= 1:
                    fp_emit_wb(y, j - 1)

            @block.scalar
            def _(s, j=j, w=w, qf=qf):
                if j == 0:
                    wait(s, sv, rw_ready)
                if j >= 3:
                    wait(s, sv, fp_q_done[j - 3])
                wait(s, qf, qf.n)
                ins = s.activation(out=f32(FP_QA[j % 3], w),
                                   in_=f32(FP_Z[j % 3], w), func=AF.Abs)
                ss.tick(ins)
            fp_abs = ss.n

            @block.vector
            def _(v, j=j, w=w, fp_abs=fp_abs):
                wait(v, ss, fp_abs)
                if j >= 2:
                    qwp = qwa if j % 2 == 0 else qwb
                    wait(v, qwp, qwp.n)
                ins = emit_quant(v, FP_Z[j % 3], FP_QA[j % 3], FP_QU, FP_QN,
                                 FP_QH, FP_BM, FP_ST[j % 2], w)
                sv.tick(ins)
            fp_q_done.append(sv.n)

        def emit_fp_flush():
            @block.sync
            def _(y):
                fp_emit_wb(y, len(fpjobs) - 1)

        @block.gpsimd
        def _(g):
            for dc in range(NDC):
                ins = g.dma_start(out=bf(O_WO + dc * 512, 1024),
                                  in_=woh[dc * 128:(dc + 1) * 128, :])
                qg.tick(ins, 16)
        wo_loaded = qg.n

        C_TMP = 35088
        g_adds = []
        for tt in range(NTT):
            for n in range(2):
                bk = (tt * 2 + n) % 4

                @block.tensor
                def _(t, tt=tt, n=n, bk=bk):
                    if tt == 0 and n == 0:
                        wait(t, qg, wo_loaded)
                        wait(t, sv, rw_ready)
                    pe_wait_free(t, bk)
                    for dc in range(NDC):
                        ins = t.matmul(
                            ps[bk][:], rwv(dc, tt * 128, 128),
                            bf(O_WO + dc * 512, 1024)[:, n * 512:(n + 1) * 512],
                            start=(dc == 0), stop=(dc == NDC - 1))
                    smm.tick(ins)
                    bank_fill[bk] = (smm, smm.n)

                @block.vector
                def _(v, tt=tt, n=n, bk=bk):
                    use_wait_fill(v, bk)
                    if len(g_adds) >= 2:
                        wait(v, sg, g_adds[-2])
                    ins = v.tensor_tensor(out=f32(C_TMP + (n % 2) * 512, 512),
                                          in0=ps[bk][:],
                                          in1=f32(O_MS + n * 512, 512), op=ALU.mult)
                    sv.tick(ins)
                    bank_free[bk] = (sv, sv.n)
                tmp_made = sv.n

                @block.gpsimd
                def _(g, tt=tt, n=n, tmp_made=tmp_made):
                    wait(g, sv, tmp_made)
                    ins = g.tensor_tensor(out=f32(O_XA + tt * D + n * 512, 512),
                                          in0=f32(O_XA + tt * D + n * 512, 512),
                                          in1=f32(C_TMP + (n % 2) * 512, 512),
                                          op=ALU.add)
                    sg.tick(ins)
                g_adds.append(sg.n)
                emit_fp_job()
        xa_after_c = sg.n
        dump_arena("C")

        nc.leave_named_scope(*_scope_st.pop(), False)
        _scope_st.append(("phD", nc.enter_named_scope("phD", False)[0]))
        C_HSB, C_ESB = 35088, 37136
        mlp_idx = [0]
        mlp_used = {}
        mlp_count = {}
        mlp_sems = [qga2, qgb2, qgc2, qgd2]

        def g_col(hc):
            if hc < 16:
                return 2 * O_RW + hc * 1024
            if hc < 24:
                return 2 * 22800 + (hc - 16) * 1024
            return 2 * O_WO + (hc - 24) * 1024

        hsb_used = [0, 0]
        for half in range(2):
            tt0 = half * 8

            @block.scalar
            def _(s, half=half, tt0=tt0):
                wait(s, sg, sg.n)
                for i in range(8):
                    ins = s.activation(
                        out=f32(36624, D), in_=f32(O_XA + (tt0 + i) * D, D),
                        func=AF.Square,
                        accum_out=AR[:, _stc(ST_SSQ) + i:_stc(ST_SSQ) + i + 1])
                ss.tick(ins)
            d_ssq = ss.n

            @block.vector
            def _(v, d_ssq=d_ssq):
                wait(v, ss, d_ssq)
                ins = v.tensor_scalar(out=f32(_stc(ST_C2), 8), in0=f32(_stc(ST_SSQ), 8),
                                      scalar1=1.0 / D, scalar2=EPS_RMS,
                                      op0=ALU.mult, op1=ALU.add)
                sv.tick(ins)
            d_arg = sv.n

            @block.scalar
            def _(s, d_arg=d_arg):
                wait(s, sv, d_arg)
                s.activation(out=f32(_stc(ST_C2), 8), in_=f32(_stc(ST_C2), 8),
                             func=AF.Ln)
                ins = s.activation(out=f32(_stc(ST_C2), 8), in_=f32(_stc(ST_C2), 8),
                                   func=AF.Exp, scale=-0.5)
                ss.tick(ins)
            c2_done = ss.n

            for i in range(8):
                @block.scalar
                def _(s, i=i, tt0=tt0, c2_done=c2_done):
                    wait(s, ss, c2_done)
                    if bank_fill[7][0] is not None:
                        wait(s, smm, bank_fill[7][1])
                    ins = s.activation(
                        out=bf(N1STG, D), in_=f32(O_XA + (tt0 + i) * D, D),
                        func=AF.Identity,
                        scale=AR[:, _stc(ST_C2) + i:_stc(ST_C2) + i + 1])
                    ss.tick(ins)
                n2_made = ss.n

                @block.tensor
                def _(t, i=i, n2_made=n2_made):
                    wait(t, ss, n2_made)
                    for hf, bk in ((0, 6), (1, 7)):
                        pe_wait_free(t, bk)
                        for j in range(4):
                            ins = t.transpose(
                                psb[bk][:, j * 128:(j + 1) * 128],
                                bf(N1STG, D)[:, (hf * 4 + j) * 128:(hf * 4 + j + 1) * 128],
                                ident)
                        smm.tick(ins)
                        bank_fill[bk] = (smm, smm.n)

                @block.vector
                def _(v, i=i):
                    n2view = ARb[:, 2 * O_NRM:2 * O_NRM + 8 * 1024].rearrange(
                        "p (c t) -> p c t", c=8)
                    for hf, bk in ((0, 6), (1, 7)):
                        use_wait_fill(v, bk)
                        ins = v.tensor_copy(
                            out=n2view[:, hf * 4:hf * 4 + 4, i * 128:(i + 1) * 128],
                            in_=psb[bk][:, 0:512].rearrange("p (c t) -> p c t", c=4))
                        sv.tick(ins)
                        bank_free[bk] = (sv, sv.n)
                emit_fp_job()
            n2_ready = sv.n

            def n2v(dc, t0, w):
                return ARb[:, 2 * O_NRM + dc * 1024 + t0: 2 * O_NRM + dc * 1024 + t0 + w]

            for hc in range(NHC):
                lis_fc_holder = [None]

                @block.gpsimd
                def _(g, hc=hc, lis_fc_holder=lis_fc_holder):
                    lis_fc_holder[0] = wq_load(g, wfch[:, hc * 128:(hc + 1) * 128]
                                               .rearrange("(c p) m -> p c m", p=128))

                for t4 in range(2):
                    bk = t4

                    @block.tensor
                    def _(t, hc=hc, t4=t4, bk=bk, n2_ready=n2_ready,
                          lis_fc_holder=lis_fc_holder):
                        li = lis_fc_holder[0]
                        if hc == 0 and t4 == 0:
                            wait(t, sv, n2_ready)
                        wait(t, *wload_count[li])
                        pe_wait_free(t, bk)
                        wslot = bf(wt_slot[li % 2], 1024)
                        for dc in range(NDC):
                            ins = t.matmul(ps[bk][:], wslot[:, dc * 128:(dc + 1) * 128],
                                           n2v(dc, t4 * 512, 512),
                                           start=(dc == 0), stop=(dc == NDC - 1))
                        smm.tick(ins)
                        bank_fill[bk] = (smm, smm.n)
                        if t4 == 1:
                            wload_used[li] = (smm, smm.n)

                    @block.scalar
                    def _(s, t4=t4, bk=bk):
                        use_wait_fill(s, bk)
                        wait(s, sv, hsb_used[t4])
                        ins = s.copy(out=f32(C_HSB + t4 * 512, 512), in_=ps[bk][:])
                        ss.tick(ins)
                    h_cp = ss.n
                    bank_free[bk] = (ss, h_cp)

                    @block.vector
                    def _(v, hc=hc, t4=t4, h_cp=h_cp):
                        wait(v, ss, h_cp)
                        v.scalar_tensor_tensor(out=f32(C_ESB + t4 * 512, 512),
                                               in0=f32(C_HSB + t4 * 512, 512),
                                               scalar=0.25,
                                               in1=f32(C_HSB + t4 * 512, 512),
                                               op0=ALU.mult, op1=ALU.max)
                        v.engine_nop(); v.engine_nop(); v.engine_nop()
                        ins = v.tensor_tensor(
                            out=ARb[:, g_col(hc) + t4 * 512:g_col(hc) + t4 * 512 + 512],
                            in0=f32(C_HSB + t4 * 512, 512),
                            in1=f32(C_ESB + t4 * 512, 512), op=ALU.mult)
                        sv.tick(ins)
                    hsb_used[t4] = sv.n
                emit_fp_job()
            g_ready = sv.n

            for n in range(2):
                for hc in range(NHC):
                    li_holder = [None]

                    @block.gpsimd
                    def _(g, hc=hc, n=n, li_holder=li_holder):
                        li = mlp_idx[0]
                        mlp_idx[0] += 1
                        if li >= 4:
                            s_, v_ = mlp_used[li - 4]
                            wait(g, s_, v_)
                        if li % 32 < 4 and load_idx[0] >= 1 and                                 (load_idx[0] - 1) in wload_used:
                            wait(g, *wload_used[load_idx[0] - 1])
                        qsem = mlp_sems[li % 4]
                        ins = g.dma_start(
                            out=bf(O_WT + (li % 4) * 256, 512),
                            in_=wmlph[hc * 128:(hc + 1) * 128, n * 512:(n + 1) * 512])
                        qsem.tick(ins, 16)
                        mlp_count[li] = (qsem, qsem.n)
                        li_holder[0] = li

                    @block.tensor
                    def _(t, hc=hc, n=n, li_holder=li_holder, g_ready=g_ready):
                        li = li_holder[0]
                        if hc == 0:
                            wait(t, sv, g_ready)
                            for bk in range(8):
                                pe_wait_free(t, bk)
                        wait(t, *mlp_count[li])
                        wslot = bf(O_WT + (li % 4) * 256, 512)
                        for i in range(8):
                            ins = t.matmul(
                                ps[i][:],
                                ARb[:, g_col(hc) + i * 128:g_col(hc) + i * 128 + 128],
                                wslot,
                                start=(hc == 0), stop=(hc == NHC - 1))
                        smm.tick(ins)
                        mlp_used[li] = (smm, smm.n)
                        if hc == NHC - 1:
                            for i in range(8):
                                bank_fill[i] = (smm, smm.n)
                            mlp_guard[0] = smm.n
                    if hc % 4 == 0:
                        emit_fp_job()

                for i in range(8):
                    @block.vector
                    def _(v, i=i, n=n):
                        use_wait_fill(v, i)
                        if len(g_adds) >= 2:
                            wait(v, sg, g_adds[-2])
                        ins = v.tensor_tensor(out=f32(C_TMP + (i % 2) * 512, 512),
                                              in0=ps[i][:],
                                              in1=f32(O_MLS + n * 512, 512),
                                              op=ALU.mult)
                        sv.tick(ins)
                        bank_free[i] = (sv, sv.n)
                    tmp_made = sv.n

                    @block.gpsimd
                    def _(g, i=i, n=n, tt0=tt0, tmp_made=tmp_made):
                        wait(g, sv, tmp_made)
                        ins = g.tensor_tensor(
                            out=f32(O_XA + (tt0 + i) * D + n * 512, 512),
                            in0=f32(O_XA + (tt0 + i) * D + n * 512, 512),
                            in1=f32(C_TMP + (i % 2) * 512, 512), op=ALU.add)
                        sg.tick(ins)
                    g_adds.append(sg.n)
        while fp_state[0] < len(fpjobs):
            emit_fp_job()
        emit_fp_flush()
        fp_last_q = fp_q_done[-1]
        xa_after_d = sg.n
        dump_arena("D")

        nc.leave_named_scope(*_scope_st.pop(), False)
        _scope_st.append(("phE", nc.enter_named_scope("phE", False)[0]))
        C_Z = (35088, 36112)
        C_QA = (37136, 38160)
        C_QU, C_QN, C_QH, C_BM, C_QST = 39184, 40208, 41232, 42256, 42448
        xt_slot = [O_XT, O_XT + 1024]
        x2_stored = [0] * NTT
        e_adds = []
        for tt in range(NTT):
            @block.sync
            def _(y, tt=tt):
                if tt == 0:
                    wait(y, sv, fp_last_q)
                if len(e_adds) >= 2:
                    wait(y, sv, e_adds[-2])
                qsem = qxa if tt % 2 == 0 else qxb
                ins = y.dma_start(out=f32(xt_slot[tt % 2], D),
                                  in_=xh[tt * 128:(tt + 1) * 128, :])
                qsem.tick(ins, 16)
            xt_ld = (qxa if tt % 2 == 0 else qxb, (qxa if tt % 2 == 0 else qxb).n)

            @block.vector
            def _(v, tt=tt, xt_ld=xt_ld):
                if tt == 0:
                    wait(v, sg, xa_after_d)
                wait(v, *xt_ld)
                v.tensor_tensor(out=f32(O_XA + tt * D, D), in0=f32(O_XA + tt * D, D),
                                in1=f32(xt_slot[tt % 2], D), op=ALU.add)
                v.engine_nop(); v.engine_nop(); v.engine_nop()
                ins = v.tensor_scalar(
                    out=f32(C_QA[0], D), in0=f32(O_XA + tt * D, D), scalar1=1.0,
                    scalar2=None, op0=ALU.mult, op1=ALU.add,
                    accum_out=AR[:, _stc(ST_SUM) + tt:_stc(ST_SUM) + tt + 1])
                sv.tick(ins)
            e_adds.append(sv.n)
            x2_done = sv.n

            @block.sync
            def _(y, tt=tt, x2_done=x2_done):
                wait(y, sv, x2_done)
                ins = y.dma_start(out=x2h[tt * 128:(tt + 1) * 128, :],
                                  in_=f32(O_XA + tt * D, D))
                qs.tick(ins, 16)
            x2_stored[tt] = qs.n

            @block.scalar
            def _(s, tt=tt, x2_done=x2_done):
                wait(s, sv, x2_done)
                ins = s.activation(
                    out=f32(36624, D), in_=f32(O_XA + tt * D, D), func=AF.Square,
                    accum_out=AR[:, _stc(ST_SSQ) + tt:_stc(ST_SSQ) + tt + 1])
                ss.tick(ins)
        e_ssq = ss.n

        @block.vector
        def _(v):
            wait(v, ss, e_ssq)
            v.tensor_scalar(out=f32(_stc(ST_M), 16), in0=f32(_stc(ST_SUM), 16),
                            scalar1=1.0 / D, scalar2=None, op0=ALU.mult,
                            op1=ALU.bypass)
            v.engine_nop(); v.engine_nop(); v.engine_nop()
            v.tensor_tensor(out=f32(_stc(ST_MSQ), 16), in0=f32(_stc(ST_M), 16),
                            in1=f32(_stc(ST_M), 16), op=ALU.mult)
            v.engine_nop(); v.engine_nop(); v.engine_nop()
            ins = v.scalar_tensor_tensor(
                out=f32(_stc(ST_VAR), 16), in0=f32(_stc(ST_SSQ), 16), scalar=1.0 / D,
                in1=f32(_stc(ST_MSQ), 16), op0=ALU.mult, op1=ALU.subtract)
            sv.tick(ins)
        e_var = sv.n

        @block.vector
        def _(v):
            v.engine_nop(); v.engine_nop(); v.engine_nop()
            ins = v.tensor_scalar(out=f32(_stc(ST_INV), 16), in0=f32(_stc(ST_VAR), 16),
                                  scalar1=EPS_LN, scalar2=None, op0=ALU.add,
                                  op1=ALU.bypass)
            sv.tick(ins)
        e_arg = sv.n

        @block.scalar
        def _(s):
            wait(s, sv, e_arg)
            s.activation(out=f32(_stc(ST_INV), 16), in_=f32(_stc(ST_INV), 16),
                         func=AF.Ln)
            ins = s.activation(out=f32(_stc(ST_INV), 16), in_=f32(_stc(ST_INV), 16),
                               func=AF.Exp, scale=-0.5)
            ss.tick(ins)
        e_inv = ss.n

        @block.vector
        def _(v):
            wait(v, ss, e_inv)
            ins = v.scalar_tensor_tensor(out=f32(_stc(ST_BI), 16),
                                         in0=f32(_stc(ST_M), 16), scalar=-1.0,
                                         in1=f32(_stc(ST_INV), 16),
                                         op0=ALU.mult, op1=ALU.mult)
            sv.tick(ins)
        e_smalls = sv.n

        e_q_done = []
        for tt in range(NTT):
            @block.scalar
            def _(s, tt=tt):
                if tt == 0:
                    wait(s, sv, e_smalls)
                if tt >= 2:
                    wait(s, sv, e_q_done[tt - 2])
                s.activation(out=f32(C_Z[tt % 2], D), in_=f32(O_XA + tt * D, D),
                             func=AF.Identity,
                             bias=AR[:, _stc(ST_BI) + tt:_stc(ST_BI) + tt + 1],
                             scale=AR[:, _stc(ST_INV) + tt:_stc(ST_INV) + tt + 1])
                ins = s.activation(out=f32(C_QA[tt % 2], D), in_=f32(C_Z[tt % 2], D),
                                   func=AF.Abs)
                ss.tick(ins)
            z_made = ss.n

            @block.vector
            def _(v, tt=tt, z_made=z_made):
                if tt == 0:
                    wait(v, qwa, qwa.n)
                    wait(v, qwb, qwb.n)
                wait(v, ss, z_made)
                if bank_fill[7][0] is not None and tt > 0:
                    wait(v, smm, bank_fill[7][1])
                ins = emit_quant(v, C_Z[tt % 2], C_QA[tt % 2], C_QU, C_QN, C_QH,
                                 C_BM, C_QST, D)
                sv.tick(ins)
            q_made = sv.n
            e_q_done.append(q_made)

            @block.tensor
            def _(t, tt=tt, q_made=q_made):
                wait(t, sv, q_made)
                for hf, bk in ((0, 6), (1, 7)):
                    pe_wait_free(t, bk)
                    for j in range(4):
                        ins = t.transpose(
                            psb[bk][:, j * 128:(j + 1) * 128],
                            bf(C_QST, D)[:, (hf * 4 + j) * 128:(hf * 4 + j + 1) * 128],
                            ident)
                    smm.tick(ins)
                    bank_fill[bk] = (smm, smm.n)

            @block.vector
            def _(v, tt=tt):
                xqview = ARb[:, 2 * O_NRM:2 * O_NRM + 8 * T].rearrange(
                    "p (c t) -> p c t", c=8)
                for hf, bk in ((0, 6), (1, 7)):
                    use_wait_fill(v, bk)
                    ins = v.tensor_copy(
                        out=xqview[:, hf * 4:hf * 4 + 4, tt * 128:(tt + 1) * 128],
                        in_=psb[bk][:, 0:512].rearrange("p (c t) -> p c t", c=4))
                    sv.tick(ins)
                    bank_free[bk] = (sv, sv.n)
        xq_ready = sv.n
        dump_arena("E")

        nc.leave_named_scope(*_scope_st.pop(), False)
        _scope_st.append(("phF", nc.enter_named_scope("phF", False)[0]))
        def wgate_col(dc):
            return 2 * O_XA + dc * IF

        def wup_col(dc):
            if dc < 3:
                return 2 * 13584 + dc * IF
            return 2 * O_SCR + (dc - 3) * IF

        def wdn_col(ic):
            if ic < 16:
                return 2 * O_RW + ic * 1024
            if ic < 18:
                return 2 * 42128 + (ic - 16) * 1024
            if ic < 20:
                return 2 * O_WT + (ic - 18) * 1024
            return 2 * O_XT + (ic - 20) * 1024

        def f2_evac(e, kind, rb, i0, cnt, bk):
            off = rb * 128
            src4 = psb[bk][:, 0:cnt * 128].rearrange("p (c t) -> p c t", c=cnt)
            if kind == "g":
                gv = ARb[:, 2 * O_XA:2 * O_XA + 8 * IF].rearrange(
                    "p (c t) -> p c t", t=IF)
                return e.tensor_copy(out=gv[:, i0:i0 + 4, off:off + 128], in_=src4)
            if kind == "u":
                if i0 == 0:
                    uv = ARb[:, 2 * 13584:2 * 13584 + 3 * IF].rearrange(
                        "p (c t) -> p c t", t=IF)
                    e.tensor_copy(out=uv[:, 0:3, off:off + 128],
                           in_=psb[bk][:, 0:384].rearrange("p (c t) -> p c t", c=3))
                    return e.tensor_copy(
                        out=ARb[:, wup_col(3) + off:wup_col(3) + off + 128],
                        in_=psb[bk][:, 384:512])
                uv = ARb[:, 2 * O_SCR:2 * O_SCR + 5 * IF].rearrange(
                    "p (c t) -> p c t", t=IF)
                return e.tensor_copy(out=uv[:, 1:5, off:off + 128], in_=src4)
            if i0 < 16:
                dv = ARb[:, 2 * O_RW:2 * O_RW + 16 * 1024].rearrange(
                    "p (c t) -> p c t", t=1024)
                return e.tensor_copy(out=dv[:, i0:i0 + 4, off:off + 128], in_=src4)
            if i0 == 16:
                dv = ARb[:, 2 * 42128:2 * 42128 + 2048].rearrange(
                    "p (c t) -> p c t", t=1024)
                e.tensor_copy(out=dv[:, 0:2, off:off + 128],
                       in_=psb[bk][:, 0:256].rearrange("p (c t) -> p c t", c=2))
                dv2 = ARb[:, 2 * O_WT:2 * O_WT + 2048].rearrange(
                    "p (c t) -> p c t", t=1024)
                return e.tensor_copy(out=dv2[:, 0:2, off:off + 128],
                              in_=psb[bk][:, 256:512].rearrange(
                                  "p (c t) -> p c t", c=2))
            dv = ARb[:, 2 * O_XT:2 * O_XT + 2048].rearrange(
                "p (c t) -> p c t", t=1024)
            return e.tensor_copy(out=dv[:, 0:2, off:off + 128],
                          in_=psb[bk][:, 0:256].rearrange("p (c t) -> p c t", c=2))

        FS = (44384, 45792)
        f2chunks = []
        for rb in range(NIC):
            f2chunks.append(("g", rb))
            f2chunks.append(("u", rb))
        for rb in range(8):
            f2chunks.append(("d", rb))
        f2_src = {"g": wqgh, "u": wquh, "d": wqdh}
        f2_done = []
        mlp_guard_final = mlp_guard[0]
        dgroups = [(0, 4, 6), (4, 4, 7), (8, 4, 6), (12, 4, 7), (16, 4, 6),
                   (20, 2, 7)]
        for k, (kind, rb) in enumerate(f2chunks):
            nbf = IF if kind == "d" else 1024
            qf = qfa if k % 2 == 0 else qfb

            @block.sync
            def _(y, k=k, kind=kind, rb=rb, nbf=nbf, qf=qf):
                if k == 0:
                    if mlp_guard_final is not None:
                        wait(y, smm, mlp_guard_final)
                    wait(y, qwa, qwa.n)
                    wait(y, qwb, qwb.n)
                if k >= 2:
                    wait(y, sv, f2_done[k - 2])
                ins = y.dma_start(out=bf(FS[k % 2], nbf),
                                  in_=f2_src[kind][rb * 128:(rb + 1) * 128, :])
                qf.tick(ins, 16)
            ldw = (qf, qf.n)

            groups = dgroups if kind == "d" else [(0, 4, 6), (4, 4, 7)]
            for gi, (i0, cnt, bk) in enumerate(groups):
                @block.tensor
                def _(t, k=k, gi=gi, i0=i0, cnt=cnt, bk=bk, nbf=nbf, ldw=ldw):
                    if k == 0 and gi == 0:
                        wait(t, sv, xq_ready)
                    if gi == 0:
                        wait(t, *ldw)
                    pe_wait_free(t, bk)
                    for j in range(cnt):
                        ins = t.transpose(
                            psb[bk][:, j * 128:(j + 1) * 128],
                            bf(FS[k % 2], nbf)[:, (i0 + j) * 128:(i0 + j + 1) * 128],
                            ident)
                    smm.tick(ins)
                    bank_fill[bk] = (smm, smm.n)

                @block.vector
                def _(v, k=k, gi=gi, kind=kind, rb=rb, i0=i0, cnt=cnt, bk=bk):
                    if k == 0 and gi == 0:
                        wait(v, qs, x2_stored[NTT - 1])
                    use_wait_fill(v, bk)
                    ins = f2_evac(v, kind, rb, i0, cnt, bk)
                    sv.tick(ins)
                    bank_free[bk] = (sv, sv.n)
            f2_done.append(sv.n)
        wq_ss_ready = sv.n
        dump_arena("F")

        nc.leave_named_scope(*_scope_st.pop(), False)
        _scope_st.append(("phG", nc.enter_named_scope("phG", False)[0]))
        G_SILS, G_QU = (272, 17808), 1808
        H2S = (784, 49504)
        QAS = (1296, 50016)
        G_QN, G_QH, G_BM = 44384, 44896, 45408
        STS = (45536, 48224)
        G_HQ, G_X2 = 45792, 47200

        def xqv(dc, t0, w):
            return ARb[:, 2 * O_NRM + dc * T + t0: 2 * O_NRM + dc * T + t0 + w]

        y_stores = []
        g_out_done = []
        down_fill_prev = [0]
        sil_cons = [0, 0]
        qa_guard = [0, 0]
        pend_q = [None]
        pend_evac = [None]
        pend_evac2 = []
        pend_tp = []
        st_guard = [0, 0]
        for tt in range(NTT):
            @block.sync
            def _(y, tt=tt):
                if tt == 0:
                    wait(y, qs, x2_stored[NTT - 1])
                if y_stores:
                    wait(y, qyo, y_stores[-1])
                ins = y.dma_start(out=f32(G_X2, D),
                                  in_=x2h[tt * 128:(tt + 1) * 128, :])
                qxa.tick(ins, 16)
            x2_ld = (qxa, qxa.n)

            for sl, (o0, w) in enumerate(IF_SLICES):
                gbk, ubk = (0, 2)[sl % 2], (1, 3)[sl % 2]
                tbk = (4, 7)[sl % 2]
                h2c = H2S[sl % 2]
                qac = QAS[sl % 2]
                stc = STS[sl % 2]

                for wnm, bk in (("gate", gbk), ("up", ubk)):
                    @block.tensor
                    def _(t, tt=tt, sl=sl, o0=o0, w=w, wnm=wnm, bk=bk):
                        if tt == 0 and sl == 0 and wnm == "gate":
                            wait(t, sv, wq_ss_ready)
                        pe_wait_free(t, bk)
                        for dc in range(NDC):
                            col = (wgate_col(dc) if wnm == "gate" else wup_col(dc)) + o0
                            ins = t.matmul(ps[bk][:, 0:w],
                                           xqv(dc, tt * 128, 128),
                                           ARb[:, col:col + w],
                                           start=(dc == 0), stop=(dc == NDC - 1))
                        smm.tick(ins)
                        bank_fill[bk] = (smm, smm.n)

                @block.tensor
                def _(t, sl=sl):
                    if pend_tp:
                        psl, pw, ptbk, pstc, phq = pend_tp.pop(0)
                        wait(t, sv, phq)
                        pe_wait_free(t, ptbk)
                        for j in range(pw // 128):
                            ins = t.transpose(psb[ptbk][:, j * 128:(j + 1) * 128],
                                              bf(pstc, pw)[:, j * 128:(j + 1) * 128],
                                              ident)
                        smm.tick(ins)
                        bank_fill[ptbk] = (smm, smm.n)
                        st_guard[psl % 2] = smm.n
                        assert pend_evac[0] is None
                        pend_evac[0] = (psl, pw, ptbk, smm.n)

                @block.scalar
                def _(s, w=w, gbk=gbk, sl=sl):
                    use_wait_fill(s, gbk)
                    if sil_cons[sl % 2]:
                        wait(s, sv, sil_cons[sl % 2])
                    ins = s.activation(out=f32(G_SILS[sl % 2], w),
                                       in_=ps[gbk][:, 0:w], func=AF.Silu)
                    ss.tick(ins)
                sil_made = ss.n
                bank_free[gbk] = (ss, sil_made)

                @block.vector
                def _(v, w=w, sil_made=sil_made, ubk=ubk, h2c=h2c, sl=sl):
                    wait(v, ss, sil_made)
                    use_wait_fill(v, ubk)
                    ins = v.tensor_tensor(out=f32(h2c, w), in0=ps[ubk][:, 0:w],
                                          in1=f32(G_SILS[sl % 2], w), op=ALU.mult)
                    sv.tick(ins)
                h2_made = sv.n
                sil_cons[sl % 2] = h2_made
                bank_free[ubk] = (sv, h2_made)

                @block.scalar
                def _(s, w=w, h2_made=h2_made, h2c=h2c, qac=qac, sl=sl):
                    wait(s, sv, h2_made)
                    if qa_guard[sl % 2]:
                        wait(s, sv, qa_guard[sl % 2])
                    ins = s.activation(out=f32(qac, w), in_=f32(h2c, w), func=AF.Abs)
                    ss.tick(ins)
                habs = ss.n

                @block.vector
                def _(v, w=w, habs=habs, h2c=h2c, qac=qac, stc=stc, tt=tt, sl=sl,
                      tbk=tbk):
                    if pend_evac[0] is not None:
                        psl, pw, pbk, pfill = pend_evac[0]
                        wait(v, smm, pfill)
                        if down_fill_prev[0]:
                            wait(v, smm, down_fill_prev[0])
                        nt = pw // 128
                        ins = v.tensor_copy(
                            out=bf(G_HQ, IF)[:, psl * 512:psl * 512 + nt * 128],
                            in_=psb[pbk][:, 0:nt * 128])
                        sv.tick(ins)
                        bank_free[pbk] = (sv, sv.n)
                        pend_evac[0] = None
                    if pend_q[0] is not None:
                        psl, pw, ph2c, pqac, pstc, ptbk, phabs = pend_q[0]
                        wait(v, ss, phabs)
                        if st_guard[psl % 2]:
                            wait(v, smm, st_guard[psl % 2])
                        v.engine_nop(); v.engine_nop()
                        ins = emit_quant(v, ph2c, pqac, G_QU, G_QN, G_QH, G_BM,
                                         pstc, pw)
                        sv.tick(ins)
                        qa_guard[psl % 2] = sv.n
                        pend_tp.append((psl, pw, ptbk, pstc, sv.n))
                        pend_q[0] = None
                    pend_q[0] = (sl, w, h2c, qac, stc, tbk, habs)

            @block.vector
            def _(v, tt=tt):
                if pend_q[0] is not None:
                    psl, pw, ph2c, pqac, pstc, ptbk, phabs = pend_q[0]
                    wait(v, ss, phabs)
                    if st_guard[psl % 2]:
                        wait(v, smm, st_guard[psl % 2])
                    v.engine_nop(); v.engine_nop()
                    ins = emit_quant(v, ph2c, pqac, G_QU, G_QN, G_QH, G_BM,
                                     pstc, pw)
                    sv.tick(ins)
                    qa_guard[psl % 2] = sv.n
                    pend_tp.append((psl, pw, ptbk, pstc, sv.n))
                    pend_q[0] = None

            @block.tensor
            def _(t, tt=tt):
                while pend_tp:
                    psl, pw, ptbk, pstc, phq = pend_tp.pop(0)
                    wait(t, sv, phq)
                    pe_wait_free(t, ptbk)
                    for j in range(pw // 128):
                        ins = t.transpose(psb[ptbk][:, j * 128:(j + 1) * 128],
                                          bf(pstc, pw)[:, j * 128:(j + 1) * 128],
                                          ident)
                    smm.tick(ins)
                    bank_fill[ptbk] = (smm, smm.n)
                    st_guard[psl % 2] = smm.n
                    pend_evac2.append((psl, pw, ptbk, smm.n))

            @block.vector
            def _(v, tt=tt):
                if pend_evac[0] is not None:
                    pend_evac2.insert(0, pend_evac[0])
                    pend_evac[0] = None
                while pend_evac2:
                    psl, pw, pbk, pfill = pend_evac2.pop(0)
                    wait(v, smm, pfill)
                    if down_fill_prev[0]:
                        wait(v, smm, down_fill_prev[0])
                    nt = pw // 128
                    ins = v.tensor_copy(
                        out=bf(G_HQ, IF)[:, psl * 512:psl * 512 + nt * 128],
                        in_=psb[pbk][:, 0:nt * 128])
                    sv.tick(ins)
                    bank_free[pbk] = (sv, sv.n)
            hq_fm_ready = sv.n

            @block.tensor
            def _(t, tt=tt, hq_fm_ready=hq_fm_ready):
                wait(t, sv, hq_fm_ready)
                for n in range(2):
                    bk = 5 + n
                    pe_wait_free(t, bk)
                    for ic in range(NIC):
                        ins = t.matmul(
                            ps[bk][:], bf(G_HQ, IF)[:, ic * 128:(ic + 1) * 128],
                            ARb[:, wdn_col(ic) + n * 512:wdn_col(ic) + n * 512 + 512],
                            start=(ic == 0), stop=(ic == NIC - 1))
                    smm.tick(ins)
                    bank_fill[bk] = (smm, smm.n)
                down_fill_prev[0] = smm.n

            @block.vector
            def _(v, tt=tt, x2_ld=x2_ld):
                wait(v, *x2_ld)
                for n in range(2):
                    bk = 5 + n
                    use_wait_fill(v, bk)
                    ins = v.tensor_tensor(out=f32(G_X2 + n * 512, 512),
                                          in0=ps[bk][:],
                                          in1=f32(G_X2 + n * 512, 512), op=ALU.add)
                    sv.tick(ins)
                    bank_free[bk] = (sv, sv.n)
            g_out_done.append(sv.n)

            @block.sync
            def _(y, tt=tt):
                wait(y, sv, g_out_done[-1])
                ins = y.dma_start(out=yh[tt * 128:(tt + 1) * 128, :],
                                  in_=f32(G_X2, D))
                qyo.tick(ins, 16)
            y_stores.append(qyo.n)
            if tt == 0:
                dump_arena("G0")

        nc.leave_named_scope(*_scope_st.pop(), False)

        @block.sync
        def _(y):
            wait(y, qyo, qyo.n)

    return nc


_NC_CACHE = {}


def kernel(L0_attn_x, L0_attn_Wk, L0_attn_Wv, L0_attn_Wr, L0_attn_Wo,
           L0_attn_time_decay, L0_attn_time_first, L0_attn_mixer_scale,
           L0_attn_Wfc, L0_attn_Wmlp, L0_attn_mlp_scale,
           L0_ffn_w_gate, L0_ffn_w_up, L0_ffn_w_down):
    x = np.ascontiguousarray(np.asarray(L0_attn_x, dtype=np.float32))
    B = x.shape[0]
    assert x.shape == (8, T, D)

    def t_(a):
        return np.ascontiguousarray(np.asarray(a, dtype=np.float32).T)

    shared = {
        "wkT": t_(L0_attn_Wk),
        "wvT": t_(L0_attn_Wv),
        "wrT": t_(L0_attn_Wr),
        "woT": t_(L0_attn_Wo),
        "wfcT": t_(L0_attn_Wfc),
        "wmlpT": t_(L0_attn_Wmlp),
        "wg": np.ascontiguousarray(np.asarray(L0_ffn_w_gate, dtype=np.float32)),
        "wu": np.ascontiguousarray(np.asarray(L0_ffn_w_up, dtype=np.float32)),
        "wd": np.ascontiguousarray(np.asarray(L0_ffn_w_down, dtype=np.float32)),
        "tdp": np.ascontiguousarray(
            np.asarray(L0_attn_time_decay, dtype=np.float32).reshape(8, 128).T),
        "tfp": np.ascontiguousarray(
            np.asarray(L0_attn_time_first, dtype=np.float32).reshape(8, 128).T),
        "msp": np.ascontiguousarray(np.broadcast_to(
            np.asarray(L0_attn_mixer_scale, dtype=np.float32)[None, :], (128, D))),
        "mlsp": np.ascontiguousarray(np.broadcast_to(
            np.asarray(L0_attn_mlp_scale, dtype=np.float32)[None, :], (128, D))),
        "identp": np.eye(128, dtype=ml_dtypes.bfloat16),
    }

    key = ("nc", _DBG_PHASE)
    if key not in _NC_CACHE:
        _NC_CACHE[key] = build_nc(_DBG_PHASE)
    nc = _NC_CACHE[key]

    in_maps = [dict(shared, x=np.ascontiguousarray(x[c])) for c in range(B)]
    res = run_bass_kernel_spmd(nc, in_maps, core_ids=list(range(B)))
    if _DBG_PHASE is not None:
        _NC_CACHE["dbg"] = [r.get("dbgarena") for r in res.results]
    out = np.stack([r["y"] for r in res.results], axis=0)
    return out.astype(np.float32)
